# version 02
# baseline (speedup 1.0000x reference)
import numpy as np
import ml_dtypes
from contextlib import ExitStack
import concourse.bass as bass
import concourse.mybir as mybir
from concourse.bass_utils import run_bass_kernel_spmd
from concourse.bass import OrderedEngineSet

F32 = mybir.dt.float32
BF16 = mybir.dt.bfloat16
FP16 = mybir.dt.float16
AF = mybir.ActivationFunctionType
ALU = mybir.AluOpType
AX = mybir.AxisListType
NPBF = ml_dtypes.bfloat16

B, T, D = 4, 2048, 2048
NH, HD = 16, 128
HPC = 8
FH = 8192
C = 128
NCK = T // C
TOK = 1024
RS2 = 1.0 / np.sqrt(2048.0)

def hadamard(n):
    h = np.array([[1.0]])
    while h.shape[0] < n:
        h = np.block([[h, h], [h, -h]])
    return h

def chunk_tile(a):
    Dr, Nc = a.shape
    nck = Dr // 128
    return np.ascontiguousarray(
        a.reshape(nck, 128, Nc).transpose(1, 0, 2).reshape(128, nck * Nc))

def head_tiles(aT, nheads, hd=128):
    Dr = aT.shape[0]
    nck = Dr // 128
    out = np.empty((nheads, 128, nck * hd), dtype=aT.dtype)
    for h in range(nheads):
        out[h] = chunk_tile(np.ascontiguousarray(aT[:, h * hd:(h + 1) * hd]))
    return out

def host_consts():
    c = {}
    inv = 1.0 / 10000.0 ** (np.arange(0, HD, 2, dtype=np.float64) / HD)
    t = np.arange(T, dtype=np.float64)
    fr = t[:, None] * inv[None, :]
    c['rope_cos'] = np.concatenate([np.cos(fr)] * 2, 1).T.astype(NPBF)
    sn = np.concatenate([np.sin(fr)] * 2, 1).T
    sn[0:64] = -sn[0:64]
    c['rope_sin'] = sn.astype(NPBF)
    kk, qq = np.meshgrid(np.arange(128), np.arange(128), indexing='ij')
    c['tri_keep'] = (kk <= qq).astype(NPBF)
    c['ident_bf'] = np.eye(128, dtype=np.float32).astype(NPBF)
    c['ident_f32'] = np.eye(128, dtype=np.float32)
    c['h128'] = hadamard(128).astype(NPBF)
    h64 = hadamard(64)
    z64 = np.zeros((64, 64))
    c['h64blk'] = np.block([[h64, z64], [z64, h64]]).astype(NPBF)
    tau = np.arange(C)[:, None]
    tt = np.arange(C)[None, :]
    U = (tau <= tt).astype(np.float64)
    packs = [np.zeros((C, C))]
    for lv in [1, 2, 4, 8, 16, 32, 64]:
        ablk = ((tt // lv) % 2) == 1
        CUM = (tau < lv * (tt // lv)).astype(np.float64)
        CUMB = (tau < lv * (tt // lv + 1)).astype(np.float64)
        packs.append(np.where(ablk, CUM - U, U - CUMB))
    packs.append(-U)
    packs.append(U - 1.0)
    c['cumpack'] = np.concatenate(packs, axis=1).astype(np.float16)
    masks = [np.concatenate([np.zeros((C, C)), np.eye(C)], 1).astype(NPBF)]
    tv, sv = np.meshgrid(np.arange(C), np.arange(C), indexing='ij')
    for lv in [1, 2, 4, 8, 16, 32, 64]:
        m = ((tv // lv == sv // lv + 1) & ((tv // lv) % 2 == 1)).astype(np.float64)
        masks.append(np.concatenate([m.T, m.T], 1).astype(NPBF))
    c['wm_masks'] = np.ascontiguousarray(np.stack(masks))
    return c

def make_pi():
    j = np.arange(FH)
    return (j % 64) * 128 + 2 * (j // 128) + (j // 64) % 2

CONSTS = host_consts()
PI = make_pi()

class Ch:
    def __init__(self, nc, sem):
        self.nc, self.sem = nc, sem
        self.val = 0
        self.last_eng = None
        self.last_name = ''
        self.pending = None
    def _flush(self):
        if self.pending is not None:
            inst, inc = self.pending
            inst.then_inc(self.sem, inc)
            self.val += inc
            self.pending = None
    PE_OPS = ('matmul', 'transpose', 'ldweights')
    def op(self, e, name, *args, **kw):
        is_dma = name.startswith('dma_start')
        pe_chain = (name in self.PE_OPS and self.last_eng is e
                    and self.last_name in self.PE_OPS)
        if not pe_chain:
            self._flush()
            e.wait_ge(self.sem, self.val)
            self.last_eng = e
        inst = getattr(e, name)(*args, **kw)
        self.last_name = name
        self.pending = (inst, 16 if is_dma else 1)
        return inst
    def finish(self, e):
        self._flush()
        e.wait_ge(self.sem, self.val)

class E:
    def __init__(self, ch, eng):
        self.ch, self.eng = ch, eng
    def __getattr__(self, name):
        def f(*a, **kw):
            return self.ch.op(self.eng, name, *a, **kw)
        return f

def build_A():
    nc = bass.Bass("TRN2", target_bir_lowering=False)
    xin = nc.dram_tensor("x_fm", [D, T], F32, kind="ExternalInput").ap()
    wq = nc.dram_tensor("wq_t", [HPC, 128, 16 * HD], F32, kind="ExternalInput").ap()
    wk = nc.dram_tensor("wk_t", [HPC, 128, 16 * HD], F32, kind="ExternalInput").ap()
    wv = nc.dram_tensor("wv_t", [HPC, 128, 16 * HD], F32, kind="ExternalInput").ap()
    bqkv = nc.dram_tensor("bqkv", [HD, 3 * HPC], F32, kind="ExternalInput").ap()
    cosd = nc.dram_tensor("rope_cos", [HD, T], BF16, kind="ExternalInput").ap()
    sind = nc.dram_tensor("rope_sin", [HD, T], BF16, kind="ExternalInput").ap()
    trid = nc.dram_tensor("tri_keep", [128, 128], BF16, kind="ExternalInput").ap()
    idnd = nc.dram_tensor("ident_bf", [128, 128], BF16, kind="ExternalInput").ap()
    ctxo = nc.dram_tensor("ctx", [HPC * HD, T], BF16, kind="ExternalOutput").ap()
    xno = nc.dram_tensor("xn_out", [D, T], BF16, kind="ExternalOutput").ap()

    st = ExitStack()
    sb = lambda n, s, d: st.enter_context(nc.sbuf_tensor(n, s, d))
    pt = lambda n, s, d: st.enter_context(nc.psum_tensor(n, s, d))
    with st:
        xn = sb("xn", [128, 16 * T], BF16, )
        wqt = sb("wqt", [128, 16 * HD], BF16, )
        wkt = sb("wkt", [128, 16 * HD], BF16, )
        wvt = sb("wvt", [128, 16 * HD], BF16, )
        biast = sb("biast", [128, 3 * HPC], F32, )
        cosb = sb("cosb", [128, T], BF16, )
        sinb = sb("sinb", [128, T], BF16, )
        trib = sb("trib", [128, 128], BF16, )
        idnb = sb("idnb", [128, 128], BF16, )
        ones_f = sb("ones_f", [128, 128], F32, )
        ones_b = sb("ones_b", [128, 128], BF16, )
        qf = sb("qf", [128, T], F32, )
        kf = sb("kf", [128, T], F32, )
        qb = sb("qb", [128, T], BF16, )
        kb = sb("kb", [128, T], BF16, )
        vb = sb("vb", [128, T], BF16, )
        vt = sb("vt", [128, T], BF16, )
        rotb = sb("rotb", [128, T], BF16, )
        pT = sb("pT", [128, 512], BF16, )
        sqb = sb("sqb", [128, 512], BF16, )
        colsq = sb("colsq", [1, T], F32, )
        bndneg = sb("bndneg", [1, T], BF16, )
        recipr = sb("recipr", [128, 512], F32, )
        tmp_f = sb("tmp_f", [128, 512], F32, )
        tmpr = sb("tmpr", [1, 512], F32, )
        mean_sb = sb("mean_sb", [1, T], F32, )
        rstd_sb = sb("rstd_sb", [1, T], F32, )
        ctxb = sb("ctxb", [128, T], BF16, )
        kmax = sb("kmax", [1, 8], F32, )
        ccell = sb("ccell", [128, 8], F32, )
        ps_q = pt("ps_q", [128, 512], F32, )
        ps_k = pt("ps_k", [128, 512], F32, )
        ps_v = pt("ps_v", [128, 512], F32, )
        ps_s = pt("ps_s", [128, 512], F32, )
        ps_c = pt("ps_c", [128, 512], F32, )
        ps_r = pt("ps_r", [128, 512], F32, )

        ps_tr = pt("ps_tr", [128, 128], BF16, )
        sem = st.enter_context(nc.semaphore("chain"))
        blk = st.enter_context(nc.Block())
        engs = {'y': nc.sync, 'g': nc.gpsimd, 't': nc.tensor,
                'v': nc.vector, 's': nc.scalar}

        @blk.gpsimd
        def _body(_g):
          ch = Ch(nc, sem)
          sy, g, t, v, s = (E(ch, engs[k]) for k in 'ygtvs')
          sy.dma_start(out=cosb[:], in_=cosd[:, :])
          sy.dma_start(out=sinb[:], in_=sind[:, :])
          sy.dma_start(out=trib[:], in_=trid[:, :])
          sy.dma_start(out=idnb[:], in_=idnd[:, :])
          sy.dma_start(out=biast[:, 0:3 * HPC], in_=bqkv[:, :])
          g.memset(ones_f[:], 1.0)
          g.memset(ones_b[:], 1.0)
          for i, cv in enumerate([1e-5]):
              g.memset(ccell[:, i:i + 1], cv)
              nc.const_aps.register(F32, cv, ccell[:, i:i + 1])
          for ci in range(16):
              g.dma_start(out=xn[:, ci * T:(ci + 1) * T],
                          in_=xin[ci * 128:(ci + 1) * 128, :])
          for q in range(4):
              qs = slice(q * 512, (q + 1) * 512)
              for ci in range(16):
                  t.matmul(ps_q[0:1, 0:512], ones_b[:, 0:1],
                           xn[:, ci * T + qs.start:ci * T + qs.stop],
                           start=(ci == 0), stop=(ci == 15))
              v.tensor_scalar(out=mean_sb[0:1, qs], in0=ps_q[0:1, 0:512],
                              scalar1=1.0 / 2048.0, scalar2=None, op0=ALU.mult)
              for ci in range(16):
                  s.activation(out=sqb[:, 0:512],
                               in_=xn[:, ci * T + qs.start:ci * T + qs.stop],
                               func=AF.Square)
                  t.matmul(ps_k[0:1, 0:512], ones_b[:, 0:1], sqb[:, 0:512],
                           start=(ci == 0), stop=(ci == 15))
              v.tensor_scalar(out=rstd_sb[0:1, qs], in0=ps_k[0:1, 0:512],
                              scalar1=1.0 / 2048.0, scalar2=None, op0=ALU.mult)
              s.activation(out=tmpr[0:1, 0:512], in_=mean_sb[0:1, qs], func=AF.Square)
              v.tensor_tensor(out=rstd_sb[0:1, qs], in0=rstd_sb[0:1, qs],
                              in1=tmpr[0:1, 0:512], op=ALU.subtract)
              s.activation(out=tmpr[0:1, 0:512], in_=rstd_sb[0:1, qs], func=AF.Ln,
                           bias=1e-5)
              s.activation(out=rstd_sb[0:1, qs], in_=tmpr[0:1, 0:512], func=AF.Exp,
                           scale=-0.5)
          for ci in range(16):
              for q in range(4):
                  qs = slice(q * 512, (q + 1) * 512)
                  o = ci * T
                  t.matmul(ps_r[:, 0:512], ones_f[0:1, :], mean_sb[0:1, qs],
                           start=True, stop=True)
                  v.tensor_tensor(out=tmp_f[:, 0:512],
                                  in0=xn[:, o + qs.start:o + qs.stop],
                                  in1=ps_r[:, 0:512], op=ALU.subtract)
                  t.matmul(ps_r[:, 0:512], ones_f[0:1, :], rstd_sb[0:1, qs],
                           start=True, stop=True)
                  v.tensor_tensor(out=xn[:, o + qs.start:o + qs.stop],
                                  in0=tmp_f[:, 0:512], in1=ps_r[:, 0:512],
                                  op=ALU.mult)
          for ci in range(16):
              g.dma_start(out=xno[ci * 128:(ci + 1) * 128, :],
                          in_=xn[:, ci * T:(ci + 1) * T])
          for h in range(HPC):
              g.dma_start(out=wqt[:], in_=wq[h, :, :])
              g.dma_start(out=wkt[:], in_=wk[h, :, :])
              g.dma_start(out=wvt[:], in_=wv[h, :, :])
              for q in range(4):
                  qs = slice(q * 512, (q + 1) * 512)
                  for ci in range(16):
                      xnc = xn[:, ci * T + qs.start:ci * T + qs.stop]
                      wsl = slice(ci * HD, (ci + 1) * HD)
                      t.matmul(ps_q[:, 0:512], wqt[:, wsl], xnc,
                               start=(ci == 0), stop=(ci == 15))
                      t.matmul(ps_k[:, 0:512], wkt[:, wsl], xnc,
                               start=(ci == 0), stop=(ci == 15))
                      t.matmul(ps_v[:, 0:512], wvt[:, wsl], xnc,
                               start=(ci == 0), stop=(ci == 15))
                  s.add(qf[:, qs], ps_q[:, 0:512], biast[:, h:h + 1])
                  s.add(kf[:, qs], ps_k[:, 0:512], biast[:, HPC + h:HPC + h + 1])
                  s.add(tmp_f[:, 0:512], ps_v[:, 0:512],
                        biast[:, 2 * HPC + h:2 * HPC + h + 1])
                  v.tensor_copy(vb[:, qs], tmp_f[:, 0:512])
              for (src, dst) in ((qf, qb), (kf, kb)):
                  s.copy(rotb[0:64, :], src[64:128, :])
                  s.copy(rotb[64:128, :], src[0:64, :])
                  v.tensor_tensor(out=rotb[:, :], in0=rotb[:, :], in1=sinb[:, :],
                                  op=ALU.mult)
                  for q in range(4):
                      qs = slice(q * 512, (q + 1) * 512)
                      v.tensor_tensor(out=tmp_f[:, 0:512], in0=src[:, qs],
                                      in1=cosb[:, qs], op=ALU.mult)
                      v.tensor_tensor(out=dst[:, qs], in0=tmp_f[:, 0:512],
                                      in1=rotb[:, qs], op=ALU.add)
              for kt in range(16):
                  ks = slice(kt * 128, (kt + 1) * 128)
                  t.transpose(ps_tr[:], vb[:, ks], idnb[:])
                  v.tensor_copy(vt[:, ks], ps_tr[:])
              for q in range(4):
                  qs = slice(q * 512, (q + 1) * 512)
                  s.activation(out=sqb[:, 0:512], in_=kb[:, qs], func=AF.Square)
                  t.matmul(ps_q[0:1, 0:512], ones_b[:, 0:1], sqb[:, 0:512],
                           start=True, stop=True)
                  v.tensor_copy(colsq[0:1, qs], ps_q[0:1, 0:512])
              v.reduce_max(kmax[0:1, 0:1], colsq[0:1, :], axis=AX.X)
              for q in range(4):
                  qs = slice(q * 512, (q + 1) * 512)
                  s.activation(out=sqb[:, 0:512], in_=qb[:, qs], func=AF.Square)
                  t.matmul(ps_q[0:1, 0:512], ones_b[:, 0:1], sqb[:, 0:512],
                           start=True, stop=True)
                  v.tensor_scalar(out=tmpr[0:1, 0:512], in0=ps_q[0:1, 0:512],
                                  scalar1=kmax[0:1, 0:1], scalar2=None, op0=ALU.mult)
                  s.activation(out=tmpr[0:1, 0:512], in_=tmpr[0:1, 0:512],
                               func=AF.Sqrt)
                  v.tensor_scalar(out=bndneg[0:1, qs], in0=tmpr[0:1, 0:512],
                                  scalar1=-1.0, scalar2=None, op0=ALU.mult)
              for q in range(4):
                  qs = slice(q * 512, (q + 1) * 512)
                  nkt = 4 * (q + 1)
                  for kt in range(nkt):
                      ks = slice(kt * 128, (kt + 1) * 128)
                      t.matmul(ps_s[:, 0:512], kb[:, ks], qb[:, qs],
                               start=True, stop=False)
                      t.matmul(ps_s[:, 0:512], ones_b[0:1, :], bndneg[0:1, qs],
                               start=False, stop=True)
                      s.activation(out=pT[:, 0:512], in_=ps_s[:, 0:512], func=AF.Exp,
                                   scale=1.0 / np.sqrt(128.0))
                      if kt * 128 >= qs.start:
                          sub = kt * 128 - qs.start
                          if sub > 0:
                              v.memset(pT[:, 0:sub], 0.0)
                          v.tensor_tensor(out=pT[:, sub:sub + 128],
                                          in0=pT[:, sub:sub + 128], in1=trib[:],
                                          op=ALU.mult)
                      t.matmul(ps_q[0:1, 0:512], ones_b[:, 0:1], pT[:, 0:512],
                               start=(kt == 0), stop=(kt == nkt - 1))
                      t.matmul(ps_c[:, 0:512], vt[:, ks], pT[:, 0:512],
                               start=(kt == 0), stop=(kt == nkt - 1))
                  v.reciprocal(tmpr[0:1, 0:512], ps_q[0:1, 0:512])
                  t.matmul(ps_r[:, 0:512], ones_f[0:1, :], tmpr[0:1, 0:512],
                           start=True, stop=True)
                  v.tensor_copy(recipr[:, 0:512], ps_r[:, 0:512])
                  v.tensor_tensor(out=ctxb[:, qs], in0=ps_c[:, 0:512],
                                  in1=recipr[:, 0:512], op=ALU.mult)
              g.dma_start(out=ctxo[h * 128:(h + 1) * 128, :], in_=ctxb[:, 0:T])
          ch.finish(engs['g'])
    return nc


def prep_A(inp):
    x = np.asarray(inp['L0_attn_hidden_states'])
    wqT = head_tiles(np.ascontiguousarray(np.asarray(inp['L0_attn_Wq']).T), NH)
    wkT = head_tiles(np.ascontiguousarray(np.asarray(inp['L0_attn_Wk']).T), NH)
    wvT = head_tiles(np.ascontiguousarray(np.asarray(inp['L0_attn_Wv']).T), NH)
    bq = np.asarray(inp['L0_attn_bq']).reshape(NH, HD)
    bk = np.asarray(inp['L0_attn_bk']).reshape(NH, HD)
    bv = np.asarray(inp['L0_attn_bv']).reshape(NH, HD)
    maps = []
    for core in range(8):
        b, hh = core // 2, core % 2
        hs = slice(hh * HPC, (hh + 1) * HPC)
        bqkv = np.concatenate([bq[hs].T, bk[hs].T, bv[hs].T], axis=1)
        maps.append({
            'x_fm': np.ascontiguousarray(x[b].T),
            'wq_t': np.ascontiguousarray(wqT[hs]),
            'wk_t': np.ascontiguousarray(wkT[hs]),
            'wv_t': np.ascontiguousarray(wvT[hs]),
            'bqkv': np.ascontiguousarray(bqkv),
            'rope_cos': CONSTS['rope_cos'], 'rope_sin': CONSTS['rope_sin'],
            'tri_keep': CONSTS['tri_keep'], 'ident_bf': CONSTS['ident_bf'],
        })
    return maps

def build_B():
    nc = bass.Bass("TRN2", target_bir_lowering=False)
    xind = nc.dram_tensor("x_half", [D, TOK], F32, kind="ExternalInput").ap()
    xnd = nc.dram_tensor("xn_half", [D, TOK], BF16, kind="ExternalInput").ap()
    ctxd = nc.dram_tensor("ctx_full", [D, TOK], BF16, kind="ExternalInput").ap()
    wod = nc.dram_tensor("wo_t", [16, 128, D], F32, kind="ExternalInput").ap()
    bod = nc.dram_tensor("bo_s", [128, 16], F32, kind="ExternalInput").ap()
    w1d = nc.dram_tensor("w1_t", [64, 128, D], F32, kind="ExternalInput").ap()
    w3d = nc.dram_tensor("w3_t", [64, 128, D], F32, kind="ExternalInput").ap()
    w2d = nc.dram_tensor("w2_t", [FH, D], F32, kind="ExternalInput").ap()
    h128d = nc.dram_tensor("h128", [128, 128], BF16, kind="ExternalInput").ap()
    h64d = nc.dram_tensor("h64blk", [128, 128], BF16, kind="ExternalInput").ap()
    x1o = nc.dram_tensor("x1_half", [D, TOK], F32, kind="ExternalOutput").ap()
    w2s = nc.dram_tensor("w2s", [64, 128, D], BF16, kind="Internal").ap()

    st = ExitStack()
    sb = lambda n, s, d: st.enter_context(nc.sbuf_tensor(n, s, d))
    pt = lambda n, s, d: st.enter_context(nc.psum_tensor(n, s, d))
    with st:
        x1 = sb("x1", [128, 16 * TOK], F32, )
        xnt = sb("xnt", [128, 16 * TOK], BF16, )
        ctxt = sb("ctxt", [128, 16 * TOK], BF16, )
        hid = sb("hid", [128, 64 * 512], BF16, )
        wtile = sb("wtile", [128, D], BF16, )
        wtile2 = sb("wtile2", [128, D], BF16, )
        w2tl = sb("w2tl", [128, 1024], BF16, )
        h128b = sb("h128b", [128, 128], BF16, )
        h64b = sb("h64b", [128, 128], BF16, )
        bob = sb("bob", [128, 16], F32, )
        sA = sb("sA", [128, 512], BF16, )
        sB = sb("sB", [128, 512], BF16, )
        ps = [pt(f"psb{i}", [128, 512], F32) for i in range(8)]
        sem = st.enter_context(nc.semaphore("chain"))
        blk = st.enter_context(nc.Block())

        @blk.gpsimd
        def _body(_g):
            ch = Ch(nc, sem)
            sy, g, t, v, s = (E(ch, nc.sync), E(ch, nc.gpsimd), E(ch, nc.tensor),
                              E(ch, nc.vector), E(ch, nc.scalar))
            sy.dma_start(out=h128b[:], in_=h128d[:, :])
            sy.dma_start(out=h64b[:], in_=h64d[:, :])
            sy.dma_start(out=bob[:], in_=bod[:, :])
            for ci in range(16):
                sy.dma_start(out=x1[:, ci * TOK:(ci + 1) * TOK],
                             in_=xind[ci * 128:(ci + 1) * 128, :])
                g.dma_start(out=xnt[:, ci * TOK:(ci + 1) * TOK],
                            in_=xnd[ci * 128:(ci + 1) * 128, :])
                g.dma_start(out=ctxt[:, ci * TOK:(ci + 1) * TOK],
                            in_=ctxd[ci * 128:(ci + 1) * 128, :])
            for n in range(64):
                g.dma_start(out=wtile[:], in_=w2d[n * 128:(n + 1) * 128, :])
                for q in range(4):
                    t.matmul(ps[0][:, 0:512], h128b[:],
                             wtile[:, q * 512:(q + 1) * 512], start=True, stop=True)
                    v.tensor_copy(wtile2[:, q * 512:(q + 1) * 512], ps[0][:, 0:512])
                g.dma_start(out=w2s[:, n:n + 65:64, :], in_=wtile2[:, :])
            for dmb in range(16):
                g.dma_start(out=wtile[:], in_=wod[dmb, :, :])
                for q in range(2):
                    qs = slice(q * 512, (q + 1) * 512)
                    for ci in range(16):
                        t.matmul(ps[0][:, 0:512], wtile[:, ci * 128:(ci + 1) * 128],
                                 ctxt[:, ci * TOK + qs.start:ci * TOK + qs.stop],
                                 start=(ci == 0), stop=(ci == 15))
                    v.scalar_tensor_tensor(
                        out=x1[:, dmb * TOK + qs.start:dmb * TOK + qs.stop],
                        in0=ps[0][:, 0:512], scalar=bob[:, dmb:dmb + 1],
                        in1=x1[:, dmb * TOK + qs.start:dmb * TOK + qs.stop],
                        op0=ALU.add, op1=ALU.add)
            for th in range(2):
                ts_ = slice(th * 512, (th + 1) * 512)
                for hb in range(64):
                    g.dma_start(out=wtile[:], in_=w1d[hb, :, :])
                    g.dma_start(out=wtile2[:], in_=w3d[hb, :, :])
                    for ci in range(16):
                        xnc = xnt[:, ci * TOK + ts_.start:ci * TOK + ts_.stop]
                        t.matmul(ps[0][:, 0:512], wtile[:, ci * 128:(ci + 1) * 128],
                                 xnc, start=(ci == 0), stop=(ci == 15))
                        t.matmul(ps[1][:, 0:512], wtile2[:, ci * 128:(ci + 1) * 128],
                                 xnc, start=(ci == 0), stop=(ci == 15))
                    s.activation(out=sA[:, 0:512], in_=ps[0][:, 0:512], func=AF.Silu)
                    v.tensor_tensor(out=sB[:, 0:512], in0=sA[:, 0:512],
                                    in1=ps[1][:, 0:512], op=ALU.mult)
                    t.matmul(ps[2][:, 0:512], h64b[:], sB[:, 0:512],
                             start=True, stop=True)
                    s.mul(hid[:, hb * 512:(hb + 1) * 512], ps[2][:, 0:512],
                          1.0 / np.sqrt(8192.0))
                for grp in range(2):
                    for hc in range(64):
                        g.dma_start(out=w2tl[:],
                                    in_=w2s[hc, :, grp * 1024:(grp + 1) * 1024])
                        for j in range(8):
                            t.matmul(ps[j][:, 0:512],
                                     w2tl[:, j * 128:(j + 1) * 128],
                                     hid[:, hc * 512:(hc + 1) * 512],
                                     start=(hc == 0), stop=(hc == 63))
                    for j in range(8):
                        dmb = grp * 8 + j
                        o = dmb * TOK + ts_.start
                        v.tensor_tensor(out=x1[:, o:o + 512], in0=ps[j][:, 0:512],
                                        in1=x1[:, o:o + 512], op=ALU.add)
            for ci in range(16):
                g.dma_start(out=x1o[ci * 128:(ci + 1) * 128, :],
                            in_=x1[:, ci * TOK:(ci + 1) * TOK])
            ch.finish(nc.gpsimd)
    return nc

def prep_B(inp, a_results):
    x = np.asarray(inp['L0_attn_hidden_states'])
    woT = head_tiles(np.ascontiguousarray(np.asarray(inp['L0_attn_Wo']).T), 16)
    bo = np.asarray(inp['L0_attn_bo']).reshape(16, 128).T
    W1piT = np.ascontiguousarray(np.asarray(inp['L0_ffn_W1'])[PI].T)
    W3piT = np.ascontiguousarray(np.asarray(inp['L0_ffn_W3'])[PI].T)
    w1t = head_tiles(W1piT, 64)
    w3t = head_tiles(W3piT, 64)
    w2T = np.ascontiguousarray(np.asarray(inp['L0_ffn_W2']).T)
    maps = []
    for core in range(8):
        b, half = core // 2, core % 2
        cs = slice(half * TOK, (half + 1) * TOK)
        ctx_full = np.concatenate([a_results[2 * b]['ctx'][:, cs],
                                   a_results[2 * b + 1]['ctx'][:, cs]], axis=0)
        maps.append({
            'x_half': np.ascontiguousarray(x[b].T[:, cs]),
            'xn_half': np.ascontiguousarray(a_results[core]['xn_out'][:, cs]),
            'ctx_full': np.ascontiguousarray(ctx_full),
            'wo_t': woT, 'bo_s': np.ascontiguousarray(bo),
            'w1_t': w1t, 'w3_t': w3t, 'w2_t': w2T,
            'h128': CONSTS['h128'], 'h64blk': CONSTS['h64blk'],
        })
    return maps

def build_C():
    nc = bass.Bass("TRN2", target_bir_lowering=False)
    xind = nc.dram_tensor("x1_full", [D, T], F32, kind="ExternalInput").ap()
    wqd = nc.dram_tensor("wq1_t", [HPC, 128, D], F32, kind="ExternalInput").ap()
    wkd = nc.dram_tensor("wk1_t", [HPC, 128, D], F32, kind="ExternalInput").ap()
    wvd = nc.dram_tensor("wv1_t", [HPC, 128, D], F32, kind="ExternalInput").ap()
    cqd = nc.dram_tensor("cq_s", [HPC, 128, 4], F32, kind="ExternalInput").ap()
    ckd = nc.dram_tensor("ck_s", [HPC, 128, 4], F32, kind="ExternalInput").ap()
    cvd = nc.dram_tensor("cv_s", [HPC, 128, 4], F32, kind="ExternalInput").ap()
    alogd = nc.dram_tensor("alog_s", [1, HPC], F32, kind="ExternalInput").ap()
    wfad = nc.dram_tensor("wfa_t", [128, D], F32, kind="ExternalInput").ap()
    wfbd = nc.dram_tensor("wfb_s", [128, HPC * 128], F32, kind="ExternalInput").ap()
    dtbd = nc.dram_tensor("dtb_s", [HPC, 128], F32, kind="ExternalInput").ap()
    wbd = nc.dram_tensor("wb_t", [128, 16 * HPC], F32, kind="ExternalInput").ap()
    wgad = nc.dram_tensor("wga_t", [128, D], F32, kind="ExternalInput").ap()
    wgbd = nc.dram_tensor("wgb_s", [128, HPC * 128], F32, kind="ExternalInput").ap()
    onwd = nc.dram_tensor("onw", [128, 1], F32, kind="ExternalInput").ap()
    cpkd = nc.dram_tensor("cumpack", [128, 1280], FP16, kind="ExternalInput").ap()
    mskd = nc.dram_tensor("wm_masks", [8, 128, 256], BF16, kind="ExternalInput").ap()
    idnd = nc.dram_tensor("ident_bf", [128, 128], BF16, kind="ExternalInput").ap()
    oo = nc.dram_tensor("o_out", [HPC * HD, T], BF16, kind="ExternalOutput").ap()

    st = ExitStack()
    sb = lambda n, s, d: st.enter_context(nc.sbuf_tensor(n, s, d))
    pt = lambda n, s, d: st.enter_context(nc.psum_tensor(n, s, d))
    with st:
        xn = sb("xn", [128, 16 * T], BF16, )
        xfa = sb("xfa", [128, T], BF16, )
        xga = sb("xga", [128, T], BF16, )
        g_tm = sb("g_tm", [128, 16 * 1024], FP16, )
        wt1 = sb("wt1", [128, D], BF16, )
        wfab = sb("wfab", [128, D], BF16, )
        wgab = sb("wgab", [128, D], BF16, )
        wfbb = sb("wfbb", [128, HPC * 128], BF16, )
        wgbb = sb("wgbb", [128, HPC * 128], BF16, )
        wbb = sb("wbb", [128, 16 * HPC], BF16, )
        cwb = sb("cwb", [128, 12], F32, )
        dtbrep = sb("dtbrep", [128, 128], F32, )
        arep = sb("arep", [128, HPC], F32, )
        onwb = sb("onwb", [128, 1], F32, )
        cpkb = sb("cpkb", [128, 1280], FP16, )
        mskb = sb("mskb", [128, 8 * 256], BF16, )
        idnb = sb("idnb", [128, 128], BF16, )
        ones_f = sb("ones_f", [128, 128], F32, )
        ones_b = sb("ones_b", [128, 128], BF16, )
        ccell = sb("ccell", [128, 8], F32, )
        qcb = sb("qcb", [128, T], BF16, )
        kcb = sb("kcb", [128, T], BF16, )
        vcb = sb("vcb", [128, T], BF16, )
        cvA = sb("cvA", [128, T], BF16, )
        cvB = sb("cvB", [128, T], BF16, )
        sgo = sb("sgo", [128, T], BF16, )
        ofm = sb("ofm", [128, T], BF16, )
        Ebuf = sb("Ebuf", [128, 1280], BF16, )
        KE = sb("KE", [128, 1280], BF16, )
        QE = sb("QE", [128, 1280], BF16, )
        WM = sb("WM", [128, 256], BF16, )
        WMt = sb("WMt", [128, 256], BF16, )
        Atb = sb("Atb", [128, 128], BF16, )
        Ab = sb("Ab", [128, 128], BF16, )
        Pbuf = sb("Pbuf", [128, 6 * 128], BF16, )
        Qbuf = sb("Qbuf", [128, 6 * 128], BF16, )
        zb = sb("zb", [128, 128], BF16, )
        zb2 = sb("zb2", [128, 128], BF16, )
        ub = sb("ub", [128, 128], BF16, )
        rb = sb("rb", [128, 128], BF16, )
        vtm = sb("vtm", [128, 128], BF16, )
        kmt = sb("kmt", [128, 128], BF16, )
        Sf = sb("Sf", [128, 128], F32, )
        Sb = sb("Sb", [128, 128], BF16, )
        bpad = sb("bpad", [128, 512], F32, )
        bT = sb("bT", [128, 16 * HPC], F32, )
        t0f = sb("t0f", [128, 128], F32, )
        t1f = sb("t1f", [128, 128], F32, )
        sqb = sb("sqb", [128, 512], BF16, )
        tmpr = sb("tmpr", [1, 512], F32, )
        tmpr2 = sb("tmpr2", [1, 512], F32, )
        mean_sb = sb("mean_sb", [1, T], F32, )
        rstd_sb = sb("rstd_sb", [1, T], F32, )
        tmp_f = sb("tmp_f", [128, 512], F32, )
        recipr = sb("recipr", [128, 512], F32, )
        ps1 = pt("ps1", [128, 512], F32, )
        ps2 = pt("ps2", [128, 512], F32, )
        psD = pt("psD", [128, 1536], F32, )
        psWM = pt("psWM", [128, 512], F32, )
        psSm = pt("psSm", [128, 384], F32, )
        ps_trb = pt("ps_trb", [128, 128], BF16, )
        sem = st.enter_context(nc.semaphore("chain"))
        blk = st.enter_context(nc.Block())

        @blk.gpsimd
        def _body(_g):
            ch = Ch(nc, sem)
            sy, g, t, v, s = (E(ch, nc.sync), E(ch, nc.gpsimd), E(ch, nc.tensor),
                              E(ch, nc.vector), E(ch, nc.scalar))
            LN2 = float(np.log(2048.0 ** -0.25))
            sy.dma_start(out=cpkb[:], in_=cpkd[:, :])
            for i in range(8):
                sy.dma_start(out=mskb[:, i * 256:(i + 1) * 256], in_=mskd[i, :, :])
            sy.dma_start(out=idnb[:], in_=idnd[:, :])
            sy.dma_start(out=onwb[:], in_=onwd[:, :])
            g.dma_start(out=wfab[:], in_=wfad[:, :])
            g.dma_start(out=wgab[:], in_=wgad[:, :])
            g.dma_start(out=wfbb[:], in_=wfbd[:, :])
            g.dma_start(out=wgbb[:], in_=wgbd[:, :])
            g.dma_start(out=wbb[:], in_=wbd[:, :])
            g.memset(ones_f[:], 1.0)
            g.memset(ones_b[:], 1.0)
            g.memset(bpad[:], 0.0)
            for i, cv in enumerate([1e-5, 1e-6, 1.0, float(np.log(128.0 ** -0.5))]):
                g.memset(ccell[:, i:i + 1], cv)
                nc.const_aps.register(F32, cv, ccell[:, i:i + 1])
            for ci in range(16):
                g.dma_start(out=xn[:, ci * T:(ci + 1) * T],
                            in_=xind[ci * 128:(ci + 1) * 128, :])
            for q in range(4):
                qs = slice(q * 512, (q + 1) * 512)
                for ci in range(16):
                    t.matmul(ps1[0:1, 0:512], ones_b[:, 0:1],
                             xn[:, ci * T + qs.start:ci * T + qs.stop],
                             start=(ci == 0), stop=(ci == 15))
                v.tensor_scalar(out=mean_sb[0:1, qs], in0=ps1[0:1, 0:512],
                                scalar1=1.0 / 2048.0, scalar2=None, op0=ALU.mult)
                for ci in range(16):
                    s.activation(out=sqb[:, 0:512],
                                 in_=xn[:, ci * T + qs.start:ci * T + qs.stop],
                                 func=AF.Square)
                    t.matmul(ps2[0:1, 0:512], ones_b[:, 0:1], sqb[:, 0:512],
                             start=(ci == 0), stop=(ci == 15))
                v.tensor_scalar(out=rstd_sb[0:1, qs], in0=ps2[0:1, 0:512],
                                scalar1=1.0 / 2048.0, scalar2=None, op0=ALU.mult)
                s.activation(out=tmpr[0:1, 0:512], in_=mean_sb[0:1, qs], func=AF.Square)
                v.tensor_tensor(out=rstd_sb[0:1, qs], in0=rstd_sb[0:1, qs],
                                in1=tmpr[0:1, 0:512], op=ALU.subtract)
                s.activation(out=tmpr[0:1, 0:512], in_=rstd_sb[0:1, qs], func=AF.Ln,
                             bias=1e-5)
                s.activation(out=rstd_sb[0:1, qs], in_=tmpr[0:1, 0:512], func=AF.Exp,
                             scale=-0.5)
            for ci in range(16):
                for q in range(4):
                    qs = slice(q * 512, (q + 1) * 512)
                    o = ci * T
                    t.matmul(ps1[:, 0:512], ones_f[0:1, :], mean_sb[0:1, qs],
                             start=True, stop=True)
                    v.tensor_tensor(out=tmp_f[:, 0:512],
                                    in0=xn[:, o + qs.start:o + qs.stop],
                                    in1=ps1[:, 0:512], op=ALU.subtract)
                    t.matmul(ps1[:, 0:512], ones_f[0:1, :], rstd_sb[0:1, qs],
                             start=True, stop=True)
                    v.tensor_tensor(out=xn[:, o + qs.start:o + qs.stop],
                                    in0=tmp_f[:, 0:512], in1=ps1[:, 0:512],
                                    op=ALU.mult)
            for (wsb, dst) in ((wfab, xfa), (wgab, xga)):
                for q in range(4):
                    qs = slice(q * 512, (q + 1) * 512)
                    for ci in range(16):
                        t.matmul(ps1[:, 0:512], wsb[:, ci * 128:(ci + 1) * 128],
                                 xn[:, ci * T + qs.start:ci * T + qs.stop],
                                 start=(ci == 0), stop=(ci == 15))
                    v.tensor_copy(dst[:, qs], ps1[:, 0:512])
            g.dma_start(out=arep[0:1, 0:HPC], in_=alogd[:, :])
            s.activation(out=arep[0:1, 0:HPC], in_=arep[0:1, 0:HPC], func=AF.Exp)
            t.matmul(ps1[:, 0:HPC], ones_f[0:1, :], arep[0:1, 0:HPC],
                     start=True, stop=True)
            v.tensor_copy(arep[:, 0:HPC], ps1[:, 0:HPC])
            for q in range(4):
                qs = slice(q * 512, (q + 1) * 512)
                for ci in range(16):
                    t.matmul(ps1[0:HPC, 0:512], wbb[:, ci * HPC:(ci + 1) * HPC],
                             xn[:, ci * T + qs.start:ci * T + qs.stop],
                             start=(ci == 0), stop=(ci == 15))
                s.activation(out=bpad[0:HPC, 0:512], in_=ps1[0:HPC, 0:512],
                             func=AF.Sigmoid)
                for tl in range(4):
                    tt = q * 4 + tl
                    v.tensor_copy(sqb[:, 0:128], bpad[:, tl * 128:(tl + 1) * 128])
                    t.transpose(ps_trb[:], sqb[:, 0:128], idnb[:])
                    v.tensor_copy(bT[:, tt * HPC:tt * HPC + HPC], ps_trb[:, 0:HPC])
            for h in range(HPC):
                g.dma_start(out=cwb[:, 0:4], in_=cqd[h, :, :])
                g.dma_start(out=cwb[:, 4:8], in_=ckd[h, :, :])
                g.dma_start(out=cwb[:, 8:12], in_=cvd[h, :, :])
                for (wdr, dst) in ((wqd, qcb), (wkd, kcb), (wvd, vcb)):
                    g.dma_start(out=wt1[:], in_=wdr[h, :, :])
                    wsb = wt1
                    for q in range(4):
                        qs = slice(q * 512, (q + 1) * 512)
                        for ci in range(16):
                            t.matmul(ps1[:, 0:512], wsb[:, ci * 128:(ci + 1) * 128],
                                     xn[:, ci * T + qs.start:ci * T + qs.stop],
                                     start=(ci == 0), stop=(ci == 15))
                        v.tensor_copy(dst[:, qs], ps1[:, 0:512])
                for ti, src in enumerate((qcb, kcb, vcb)):
                    cw = cwb[:, ti * 4:(ti + 1) * 4]
                    v.tensor_scalar(out=cvA[:, :], in0=src[:, :],
                                    scalar1=cw[:, 3:4], scalar2=None, op0=ALU.mult)
                    bufs = [cvA, cvB]
                    for si, j in enumerate((2, 1, 0)):
                        sh = 3 - j
                        a, bq_ = bufs[si % 2], bufs[(si + 1) % 2]
                        v.scalar_tensor_tensor(out=bq_[:, sh:T], in0=src[:, 0:T - sh],
                                               scalar=cw[:, j:j + 1], in1=a[:, sh:T],
                                               op0=ALU.mult, op1=ALU.add)
                        v.tensor_copy(bq_[:, 0:sh], a[:, 0:sh])
                    s.activation(out=src[:, :], in_=cvB[:, :], func=AF.Silu)
                for ti, src in enumerate((qcb, kcb)):
                    for q in range(4):
                        qs = slice(q * 512, (q + 1) * 512)
                        s.activation(out=sqb[:, 0:512], in_=src[:, qs], func=AF.Square)
                        t.matmul(ps1[0:1, 0:512], ones_b[:, 0:1], sqb[:, 0:512],
                                 start=True, stop=True)
                        s.activation(out=tmpr[0:1, 0:512], in_=ps1[0:1, 0:512],
                                     func=AF.Ln, bias=1e-6)
                        if ti == 0:
                            s.activation(out=tmpr[0:1, 0:512], in_=tmpr[0:1, 0:512],
                                         func=AF.Exp, scale=-0.5,
                                         bias=float(np.log(128.0 ** -0.5)))
                        else:
                            s.activation(out=tmpr[0:1, 0:512], in_=tmpr[0:1, 0:512],
                                         func=AF.Exp, scale=-0.5)
                        t.matmul(ps1[:, 0:512], ones_f[0:1, :], tmpr[0:1, 0:512],
                                 start=True, stop=True)
                        v.tensor_tensor(out=src[:, qs], in0=src[:, qs],
                                        in1=ps1[:, 0:512], op=ALU.mult)
                t.matmul(ps1[:, 0:128], ones_f[0:1, :], dtbd[h:h + 1, :],
                         start=True, stop=True) if False else None
                g.dma_start(out=dtbrep[0:1, 0:128], in_=dtbd[h:h + 1, :])
                t.matmul(ps1[:, 0:128], ones_f[0:1, :], dtbrep[0:1, 0:128],
                         start=True, stop=True)
                v.tensor_copy(dtbrep[:, 0:128], ps1[:, 0:128])
                for tt in range(16):
                    t.matmul(ps2[:, 0:128], xfa[:, tt * 128:(tt + 1) * 128],
                             wfbb[:, h * 128:(h + 1) * 128], start=True, stop=True)
                    v.tensor_tensor(out=t0f[:, :], in0=ps2[:, 0:128],
                                    in1=dtbrep[:, 0:128], op=ALU.add)
                    s.activation(out=t1f[:, :], in_=t0f[:, :], func=AF.Abs)
                    s.activation(out=t1f[:, :], in_=t1f[:, :], func=AF.Exp, scale=-1.0)
                    s.activation(out=t1f[:, :], in_=t1f[:, :], func=AF.Ln, bias=1.0)
                    v.tensor_scalar(out=t0f[:, :], in0=t0f[:, :], scalar1=0.0,
                                    scalar2=None, op0=ALU.max)
                    v.tensor_tensor(out=t0f[:, :], in0=t0f[:, :], in1=t1f[:, :],
                                    op=ALU.add)
                    v.tensor_scalar(out=g_tm[:, tt * 1024 + h * 128:tt * 1024 + (h + 1) * 128],
                                    in0=t0f[:, :], scalar1=arep[:, h:h + 1],
                                    scalar2=None, op0=ALU.mult)
                for q in range(4):
                    qs = slice(q * 512, (q + 1) * 512)
                    t.matmul(ps1[:, 0:512], wgbb[:, h * 128:(h + 1) * 128],
                             xga[:, qs], start=True, stop=True)
                    s.activation(out=sgo[:, qs], in_=ps1[:, 0:512], func=AF.Sigmoid)
                g.memset(Sf[:], 0.0)
                g.memset(Sb[:], 0.0)
                for ck in range(16):
                    cs = slice(ck * 128, (ck + 1) * 128)
                    gsl = g_tm[:, ck * 1024 + h * 128:ck * 1024 + (h + 1) * 128]
                    for mi in range(3):
                        fs = slice(mi * 512, min((mi + 1) * 512, 1280))
                        t.matmul(psD[:, fs], gsl, cpkb[:, fs],
                                 start=True, stop=True)
                    s.activation(out=Ebuf[:, :], in_=psD[:, 0:1280], func=AF.Exp)
                    for si in range(1, 10):
                        sl = slice(si * 128, (si + 1) * 128)
                        v.tensor_tensor(out=KE[:, sl], in0=Ebuf[:, sl],
                                        in1=kcb[:, cs], op=ALU.mult)
                    for si in list(range(1, 8)) + [8]:
                        sl = slice(si * 128, (si + 1) * 128)
                        v.tensor_tensor(out=QE[:, sl], in0=Ebuf[:, sl],
                                        in1=qcb[:, cs], op=ALU.mult)
                    for si in range(8):
                        bsl = kcb[:, cs] if si == 0 else KE[:, si * 128:(si + 1) * 128]
                        asl = kcb[:, cs] if si == 0 else KE[:, si * 128:(si + 1) * 128]
                        aq = qcb[:, cs] if si == 0 else QE[:, si * 128:(si + 1) * 128]
                        t.matmul(psWM[:, 0:128], bsl, asl, start=True, stop=True)
                        t.matmul(psWM[:, 128:256], bsl, aq, start=True, stop=True)
                        if si == 0:
                            v.tensor_tensor(out=WM[:, :], in0=psWM[:, 0:256],
                                            in1=mskb[:, 0:256], op=ALU.mult)
                        else:
                            v.tensor_tensor(out=WMt[:, :], in0=psWM[:, 0:256],
                                            in1=mskb[:, si * 256:(si + 1) * 256],
                                            op=ALU.mult)
                            v.tensor_tensor(out=WM[:, :], in0=WM[:, :],
                                            in1=WMt[:, :], op=ALU.add)
                    bcol = bT[:, ck * HPC + h:ck * HPC + h + 1]
                    v.tensor_scalar(out=Atb[:, :], in0=WM[:, 0:128],
                                    scalar1=bcol, scalar2=None, op0=ALU.mult)
                    t.transpose(ps_trb[:], Atb[:, :], idnb[:])
                    v.tensor_copy(Ab[:, :], ps_trb[:])
                    t.transpose(ps_trb[:], vcb[:, cs], idnb[:])
                    v.tensor_copy(vtm[:, :], ps_trb[:])
                    t.matmul(psSm[:, 0:128], KE[:, 8 * 128:9 * 128], Sb[:, :],
                             start=True, stop=True)
                    v.tensor_tensor(out=rb[:, :], in0=vtm[:, :], in1=psSm[:, 0:128],
                                    op=ALU.subtract)
                    prevP, prevQ = Ab[:, :], Atb[:, :]
                    for j in range(1, 7):
                        if j < 6:
                            t.matmul(psSm[:, 128:256], prevQ, prevP, start=True, stop=True)
                            v.tensor_copy(Pbuf[:, (j - 1) * 128:j * 128], psSm[:, 128:256])
                        t.matmul(psSm[:, 256:384], prevP, prevQ, start=True, stop=True)
                        v.tensor_copy(Qbuf[:, (j - 1) * 128:j * 128], psSm[:, 256:384])
                        prevP = Pbuf[:, (j - 1) * 128:j * 128]
                        prevQ = Qbuf[:, (j - 1) * 128:j * 128]
                    zcur, znxt = zb, zb2
                    v.tensor_copy(zcur[:, :], rb[:, :])
                    for j in range(6, 0, -1):
                        t.matmul(psSm[:, 128:256], Qbuf[:, (j - 1) * 128:j * 128],
                                 zcur[:, :], start=True, stop=True)
                        v.tensor_tensor(out=znxt[:, :], in0=zcur[:, :],
                                        in1=psSm[:, 128:256], op=ALU.add)
                        zcur, znxt = znxt, zcur
                    t.matmul(psSm[:, 128:256], Atb[:, :], zcur[:, :], start=True, stop=True)
                    v.tensor_tensor(out=znxt[:, :], in0=zcur[:, :], in1=psSm[:, 128:256],
                                    op=ALU.subtract)
                    v.tensor_scalar(out=ub[:, :], in0=znxt[:, :], scalar1=bcol,
                                    scalar2=None, op0=ALU.mult)
                    t.matmul(psSm[:, 256:384], ub[:, :], WM[:, 128:256],
                             start=True, stop=False)
                    t.matmul(psSm[:, 256:384], Sb[:, :], QE[:, 8 * 128:9 * 128],
                             start=False, stop=True)
                    v.tensor_copy(ofm[:, cs], psSm[:, 256:384])
                    t.transpose(ps_trb[:], KE[:, 9 * 128:10 * 128], idnb[:])
                    v.tensor_copy(kmt[:, :], ps_trb[:])
                    t.matmul(psSm[:, 0:128], kmt[:, :], ub[:, :], start=True, stop=True)
                    v.scalar_tensor_tensor(out=Sf[:, :], in0=Sf[:, :],
                                           scalar=Ebuf[:, 8 * 128 + 127:8 * 128 + 128],
                                           in1=psSm[:, 0:128], op0=ALU.mult, op1=ALU.add)
                    v.tensor_copy(Sb[:, :], Sf[:, :])
                for q in range(4):
                    qs = slice(q * 512, (q + 1) * 512)
                    s.activation(out=sqb[:, 0:512], in_=ofm[:, qs], func=AF.Square)
                    t.matmul(ps1[0:1, 0:512], ones_b[:, 0:1], sqb[:, 0:512],
                             start=True, stop=True)
                    v.tensor_scalar(out=tmpr[0:1, 0:512], in0=ps1[0:1, 0:512],
                                    scalar1=1.0 / 128.0, scalar2=None, op0=ALU.mult)
                    s.activation(out=tmpr[0:1, 0:512], in_=tmpr[0:1, 0:512],
                                 func=AF.Ln, bias=1e-6)
                    s.activation(out=tmpr[0:1, 0:512], in_=tmpr[0:1, 0:512],
                                 func=AF.Exp, scale=-0.5)
                    t.matmul(ps1[:, 0:512], ones_f[0:1, :], tmpr[0:1, 0:512],
                             start=True, stop=True)
                    v.tensor_copy(recipr[:, 0:512], ps1[:, 0:512])
                    v.scalar_tensor_tensor(out=ofm[:, qs], in0=ofm[:, qs],
                                           scalar=onwb[:, 0:1], in1=sgo[:, qs],
                                           op0=ALU.mult, op1=ALU.mult)
                    v.tensor_tensor(out=ofm[:, qs], in0=ofm[:, qs],
                                    in1=recipr[:, 0:512], op=ALU.mult)
                g.dma_start(out=oo[h * 128:(h + 1) * 128, :], in_=ofm[:, 0:T])
            ch.finish(nc.gpsimd)
    return nc

def prep_C(inp, x1_full_by_b):
    wq = head_tiles(np.ascontiguousarray(np.asarray(inp['L1_attn_Wq'])), NH)
    wk = head_tiles(np.ascontiguousarray(np.asarray(inp['L1_attn_Wk'])), NH)
    wv = head_tiles(np.ascontiguousarray(np.asarray(inp['L1_attn_Wv'])), NH)
    cq = np.asarray(inp['L1_attn_conv_q_w']).reshape(NH, 128, 4)
    ck = np.asarray(inp['L1_attn_conv_k_w']).reshape(NH, 128, 4)
    cv = np.asarray(inp['L1_attn_conv_v_w']).reshape(NH, 128, 4)
    alog = np.asarray(inp['L1_attn_A_log']).reshape(1, NH)
    wfa = chunk_tile(np.ascontiguousarray(np.asarray(inp['L1_attn_Wfa'])))
    wfb = np.asarray(inp['L1_attn_Wfb'])
    dtb = np.asarray(inp['L1_attn_dt_bias']).reshape(NH, 128)
    wb = chunk_tile(np.ascontiguousarray(np.asarray(inp['L1_attn_Wb'])))
    wga = chunk_tile(np.ascontiguousarray(np.asarray(inp['L1_attn_Wga'])))
    wgb = np.asarray(inp['L1_attn_Wgb'])
    onw = np.asarray(inp['L1_attn_o_norm_w']).reshape(128, 1)
    maps = []
    for core in range(8):
        b, hh = core // 2, core % 2
        hs = slice(hh * HPC, (hh + 1) * HPC)
        wbt = chunk_tile(np.ascontiguousarray(
            np.asarray(inp['L1_attn_Wb'])[:, hs]))
        maps.append({
            'x1_full': x1_full_by_b[b],
            'wq1_t': np.ascontiguousarray(wq[hs]),
            'wk1_t': np.ascontiguousarray(wk[hs]),
            'wv1_t': np.ascontiguousarray(wv[hs]),
            'cq_s': np.ascontiguousarray(cq[hs]),
            'ck_s': np.ascontiguousarray(ck[hs]),
            'cv_s': np.ascontiguousarray(cv[hs]),
            'alog_s': np.ascontiguousarray(alog[:, hs]),
            'wfa_t': wfa,
            'wfb_s': np.ascontiguousarray(wfb[:, hh * 1024:(hh + 1) * 1024]),
            'dtb_s': np.ascontiguousarray(dtb[hs]),
            'wb_t': wbt,
            'wga_t': wga,
            'wgb_s': np.ascontiguousarray(wgb[:, hh * 1024:(hh + 1) * 1024]),
            'onw': onw,
            'cumpack': CONSTS['cumpack'], 'wm_masks': CONSTS['wm_masks'],
            'ident_bf': CONSTS['ident_bf'],
        })
    return maps

def build_D():
    nc = bass.Bass("TRN2", target_bir_lowering=False)
    xind = nc.dram_tensor("x1_half", [D, TOK], F32, kind="ExternalInput").ap()
    od = nc.dram_tensor("o_full", [D, TOK], BF16, kind="ExternalInput").ap()
    wod = nc.dram_tensor("wo1_t", [16, 128, D], F32, kind="ExternalInput").ap()
    gted = nc.dram_tensor("gate_t", [128, 16 * 8], F32, kind="ExternalInput").ap()
    w1d = nc.dram_tensor("w1e_t", [8, 8, 128, D], F32, kind="ExternalInput").ap()
    w3d = nc.dram_tensor("w3e_t", [8, 8, 128, D], F32, kind="ExternalInput").ap()
    w2d = nc.dram_tensor("w2e_t", [8, 16, 128, 1024], F32, kind="ExternalInput").ap()
    ws1d = nc.dram_tensor("ws1_t", [8, 128, D], F32, kind="ExternalInput").ap()
    ws3d = nc.dram_tensor("ws3_t", [8, 128, D], F32, kind="ExternalInput").ap()
    ws2d = nc.dram_tensor("ws2_t", [16, 128, 1024], F32, kind="ExternalInput").ap()
    idnfd = nc.dram_tensor("ident_f32", [128, 128], F32, kind="ExternalInput").ap()
    yo = nc.dram_tensor("y_half", [D, TOK], F32, kind="ExternalOutput").ap()

    st = ExitStack()
    sb = lambda n, s, d: st.enter_context(nc.sbuf_tensor(n, s, d))
    pt = lambda n, s, d: st.enter_context(nc.psum_tensor(n, s, d))
    with st:
        x1 = sb("x1", [128, 16 * TOK], F32, )
        xnt = sb("xnt", [128, 16 * TOK], BF16, )
        big = sb("big", [128, 32768], BF16, )
        hsb = sb("hsb", [128, 8 * 512], BF16, )
        wt1 = sb("wt1", [128, D], BF16, )
        wt2 = sb("wt2", [128, D], BF16, )
        gtb = sb("gtb", [128, 16 * 8], F32, )
        idnf = sb("idnf", [128, 128], F32, )
        ones_f = sb("ones_f", [128, 128], F32, )
        ones_b = sb("ones_b", [128, 128], BF16, )
        ccell = sb("ccell", [128, 8], F32, )
        mean_sb = sb("mean_sb", [1, TOK], F32, )
        rstd_sb = sb("rstd_sb", [1, TOK], F32, )
        svec = sb("svec", [8, 1], F32, )
        lsb = sb("lsb", [8, 512], F32, )
        msS = sb("msS", [8, 512], F32, )
        spad = sb("spad", [128, 512], F32, )
        stm = sb("stm", [128, 8 * 8], F32, )
        m1r = sb("m1r", [128, 8], F32, )
        m1b = sb("m1b", [128, 8], F32, )
        cum = sb("cum", [128, 8], F32, )
        s2b = sb("s2b", [128, 8], F32, )
        msum = sb("msum", [128, 8], F32, )
        betE = sb("betE", [128, 8 * 8], F32, )
        Reb = sb("Reb", [128, 512], BF16, )
        asb = sb("asb", [128, 512], BF16, )
        ssb = sb("ssb", [128, 512], BF16, )
        bsb = sb("bsb", [128, 512], BF16, )
        sqb = sb("sqb", [128, 512], BF16, )
        tmpr = sb("tmpr", [1, 512], F32, )
        tmp_f = sb("tmp_f", [128, 512], F32, )
        ps1 = pt("ps1", [128, 512], F32, )
        ps2 = pt("ps2", [128, 512], F32, )
        psf = pt("psf", [128, 128], F32, )
        sem = st.enter_context(nc.semaphore("chain"))
        blk = st.enter_context(nc.Block())

        @blk.gpsimd
        def _body(_g):
            ch = Ch(nc, sem)
            sy, g, t, v, s = (E(ch, nc.sync), E(ch, nc.gpsimd), E(ch, nc.tensor),
                              E(ch, nc.vector), E(ch, nc.scalar))
            sy.dma_start(out=gtb[:], in_=gted[:, :])
            sy.dma_start(out=idnf[:], in_=idnfd[:, :])
            g.memset(ones_f[:], 1.0)
            g.memset(ones_b[:], 1.0)
            for i, cv in enumerate([1e-5]):
                g.memset(ccell[:, i:i + 1], cv)
                nc.const_aps.register(F32, cv, ccell[:, i:i + 1])
            for ci in range(16):
                sy.dma_start(out=x1[:, ci * TOK:(ci + 1) * TOK],
                             in_=xind[ci * 128:(ci + 1) * 128, :])
                g.dma_start(out=big[:, ci * TOK:(ci + 1) * TOK],
                            in_=od[ci * 128:(ci + 1) * 128, :])
            for q in range(2):
                qs = slice(q * 512, (q + 1) * 512)
                for ci in range(16):
                    t.matmul(ps1[0:1, 0:512], ones_f[:, 0:1],
                             x1[:, ci * TOK + qs.start:ci * TOK + qs.stop],
                             start=(ci == 0), stop=(ci == 15))
                v.tensor_scalar(out=mean_sb[0:1, qs], in0=ps1[0:1, 0:512],
                                scalar1=1.0 / 2048.0, scalar2=None, op0=ALU.mult)
                for ci in range(16):
                    s.activation(out=tmp_f[:, 0:512],
                                 in_=x1[:, ci * TOK + qs.start:ci * TOK + qs.stop],
                                 func=AF.Square)
                    t.matmul(ps2[0:1, 0:512], ones_f[:, 0:1], tmp_f[:, 0:512],
                             start=(ci == 0), stop=(ci == 15))
                v.tensor_scalar(out=rstd_sb[0:1, qs], in0=ps2[0:1, 0:512],
                                scalar1=1.0 / 2048.0, scalar2=None, op0=ALU.mult)
                s.activation(out=tmpr[0:1, 0:512], in_=mean_sb[0:1, qs], func=AF.Square)
                v.tensor_tensor(out=rstd_sb[0:1, qs], in0=rstd_sb[0:1, qs],
                                in1=tmpr[0:1, 0:512], op=ALU.subtract)
                s.activation(out=tmpr[0:1, 0:512], in_=rstd_sb[0:1, qs], func=AF.Ln,
                             bias=1e-5)
                s.activation(out=rstd_sb[0:1, qs], in_=tmpr[0:1, 0:512], func=AF.Exp,
                             scale=-0.5)
            for ci in range(16):
                for q in range(2):
                    qs = slice(q * 512, (q + 1) * 512)
                    o = ci * TOK
                    t.matmul(ps1[:, 0:512], ones_f[0:1, :], mean_sb[0:1, qs],
                             start=True, stop=True)
                    v.tensor_tensor(out=tmp_f[:, 0:512],
                                    in0=x1[:, o + qs.start:o + qs.stop],
                                    in1=ps1[:, 0:512], op=ALU.subtract)
                    t.matmul(ps1[:, 0:512], ones_f[0:1, :], rstd_sb[0:1, qs],
                             start=True, stop=True)
                    v.tensor_tensor(out=xnt[:, o + qs.start:o + qs.stop],
                                    in0=tmp_f[:, 0:512], in1=ps1[:, 0:512],
                                    op=ALU.mult)
            for ci in range(16):
                t.matmul(ps1[0:1, 0:8], ones_f[:, 0:1], gtb[:, ci * 8:(ci + 1) * 8],
                         start=(ci == 0), stop=(ci == 15))
            v.tensor_copy(svec[0:1, 0:1], ps1[0:1, 0:1]) if False else None
            v.tensor_copy(spad[0:1, 0:8], ps1[0:1, 0:8])
            t.transpose(psf[:], spad[:, 0:128], idnf[:])
            v.tensor_copy(svec[0:8, 0:1], psf[0:8, 0:1])
            for q in range(2):
                qs = slice(q * 512, (q + 1) * 512)
                for ci in range(16):
                    t.matmul(ps1[0:8, 0:512], gtb[:, ci * 8:(ci + 1) * 8],
                             x1[:, ci * TOK + qs.start:ci * TOK + qs.stop],
                             start=(ci == 0), stop=(ci == 15))
                v.tensor_copy(lsb[:, 0:512], ps1[0:8, 0:512])
                t.matmul(ps2[0:8, 0:512], ones_f[0:1, 0:8], mean_sb[0:1, qs],
                         start=True, stop=True)
                v.tensor_scalar(out=msS[:, 0:512], in0=ps2[0:8, 0:512],
                                scalar1=svec[0:8, 0:1], scalar2=None, op0=ALU.mult)
                v.tensor_tensor(out=lsb[:, 0:512], in0=lsb[:, 0:512],
                                in1=msS[:, 0:512], op=ALU.subtract)
                t.matmul(ps2[0:8, 0:512], ones_f[0:1, 0:8], rstd_sb[0:1, qs],
                         start=True, stop=True)
                v.tensor_tensor(out=lsb[:, 0:512], in0=lsb[:, 0:512],
                                in1=ps2[0:8, 0:512], op=ALU.mult)
                s.activation(out=spad[0:8, 0:512], in_=lsb[:, 0:512], func=AF.Sigmoid)
                for tl in range(4):
                    tt = q * 4 + tl
                    t.transpose(psf[:], spad[:, tl * 128:(tl + 1) * 128], idnf[:])
                    v.tensor_copy(stm[:, tt * 8:(tt + 1) * 8], psf[:, 0:8])
            for tt in range(8):
                ssl = stm[:, tt * 8:(tt + 1) * 8]
                v.reduce_max(m1r[:, 0:1], ssl, axis=AX.X)
                v.tensor_scalar(out=m1r[:, 0:8], in0=ssl, scalar1=m1r[:, 0:1],
                                scalar2=None, op0=ALU.is_ge)
                v.tensor_tensor_scan(out=cum[:, 0:8], data0=ones_f[:, 0:8],
                                     data1=m1r[:, 0:8], initial=0.0,
                                     op0=ALU.mult, op1=ALU.add)
                v.tensor_scalar(out=cum[:, 0:8], in0=cum[:, 0:8], scalar1=1.0,
                                scalar2=None, op0=ALU.is_equal)
                v.tensor_tensor(out=m1b[:, 0:8], in0=cum[:, 0:8], in1=m1r[:, 0:8],
                                op=ALU.mult)
                v.scalar_tensor_tensor(out=s2b[:, 0:8], in0=m1b[:, 0:8],
                                       scalar=-2.0, in1=ssl, op0=ALU.mult,
                                       op1=ALU.add)
                v.reduce_max(m1r[:, 0:1], s2b[:, 0:8], axis=AX.X)
                v.tensor_scalar(out=m1r[:, 0:8], in0=s2b[:, 0:8],
                                scalar1=m1r[:, 0:1], scalar2=None, op0=ALU.is_ge)
                v.tensor_tensor_scan(out=cum[:, 0:8], data0=ones_f[:, 0:8],
                                     data1=m1r[:, 0:8], initial=0.0,
                                     op0=ALU.mult, op1=ALU.add)
                v.tensor_scalar(out=cum[:, 0:8], in0=cum[:, 0:8], scalar1=1.0,
                                scalar2=None, op0=ALU.is_equal)
                v.tensor_tensor(out=cum[:, 0:8], in0=cum[:, 0:8], in1=m1r[:, 0:8],
                                op=ALU.mult)
                v.tensor_tensor(out=msum[:, 0:8], in0=m1b[:, 0:8], in1=cum[:, 0:8],
                                op=ALU.add)
                v.tensor_tensor(out=betE[:, tt * 8:(tt + 1) * 8], in0=ssl,
                                in1=msum[:, 0:8], op=ALU.mult)
            for dmb in range(16):
                g.dma_start(out=wt1[:], in_=wod[dmb, :, :])
                for q in range(2):
                    qs = slice(q * 512, (q + 1) * 512)
                    for ci in range(16):
                        t.matmul(ps1[:, 0:512], wt1[:, ci * 128:(ci + 1) * 128],
                                 big[:, ci * TOK + qs.start:ci * TOK + qs.stop],
                                 start=(ci == 0), stop=(ci == 15))
                    v.tensor_tensor(out=x1[:, dmb * TOK + qs.start:dmb * TOK + qs.stop],
                                    in0=ps1[:, 0:512],
                                    in1=x1[:, dmb * TOK + qs.start:dmb * TOK + qs.stop],
                                    op=ALU.add)
            for th in range(2):
                ts_ = slice(th * 512, (th + 1) * 512)
                for e in range(9):
                    if e < 8:
                        for tl in range(4):
                            tt = th * 4 + tl
                            v.tensor_scalar(out=spad[:, 0:128], in0=ones_f[:, 0:128],
                                            scalar1=betE[:, tt * 8 + e:tt * 8 + e + 1],
                                            scalar2=None, op0=ALU.mult)
                            t.transpose(psf[:], spad[:, 0:128], idnf[:])
                            v.tensor_copy(Reb[:, tl * 128:(tl + 1) * 128], psf[:, :])
                    for hb in range(8):
                        if e < 8:
                            g.dma_start(out=wt1[:], in_=w1d[e, hb, :, :])
                            g.dma_start(out=wt2[:], in_=w3d[e, hb, :, :])
                        else:
                            g.dma_start(out=wt1[:], in_=ws1d[hb, :, :])
                            g.dma_start(out=wt2[:], in_=ws3d[hb, :, :])
                        for ci in range(16):
                            xnc = xnt[:, ci * TOK + ts_.start:ci * TOK + ts_.stop]
                            t.matmul(ps1[:, 0:512], wt1[:, ci * 128:(ci + 1) * 128],
                                     xnc, start=(ci == 0), stop=(ci == 15))
                            t.matmul(ps2[:, 0:512], wt2[:, ci * 128:(ci + 1) * 128],
                                     xnc, start=(ci == 0), stop=(ci == 15))
                        if e < 8:
                            v.tensor_tensor(out=asb[:, 0:512], in0=ps1[:, 0:512],
                                            in1=Reb[:, 0:512], op=ALU.mult)
                            s.activation(out=ssb[:, 0:512], in_=asb[:, 0:512],
                                         func=AF.Silu)
                            v.tensor_tensor(out=bsb[:, 0:512], in0=ps2[:, 0:512],
                                            in1=Reb[:, 0:512], op=ALU.mult)
                            v.tensor_tensor(out=big[:, (e * 8 + hb) * 512:(e * 8 + hb + 1) * 512],
                                            in0=ssb[:, 0:512], in1=bsb[:, 0:512],
                                            op=ALU.mult)
                        else:
                            s.activation(out=ssb[:, 0:512], in_=ps1[:, 0:512],
                                         func=AF.Silu)
                            v.tensor_copy(bsb[:, 0:512], ps2[:, 0:512])
                            v.tensor_tensor(out=hsb[:, hb * 512:(hb + 1) * 512],
                                            in0=ssb[:, 0:512], in1=bsb[:, 0:512],
                                            op=ALU.mult)
                for dmb in range(16):
                    first = True
                    for e in range(8):
                        g.dma_start(out=wt1[:, 0:1024], in_=w2d[e, dmb, :, :])
                        for hc in range(8):
                            t.matmul(ps1[:, 0:512], wt1[:, hc * 128:(hc + 1) * 128],
                                     big[:, (e * 8 + hc) * 512:(e * 8 + hc + 1) * 512],
                                     start=first, stop=False)
                            first = False
                    g.dma_start(out=wt1[:, 0:1024], in_=ws2d[dmb, :, :])
                    for hc in range(8):
                        t.matmul(ps1[:, 0:512], wt1[:, hc * 128:(hc + 1) * 128],
                                 hsb[:, hc * 512:(hc + 1) * 512],
                                 start=False, stop=(hc == 7))
                    o = dmb * TOK + ts_.start
                    v.tensor_tensor(out=x1[:, o:o + 512], in0=ps1[:, 0:512],
                                    in1=x1[:, o:o + 512], op=ALU.add)
            for ci in range(16):
                g.dma_start(out=yo[ci * 128:(ci + 1) * 128, :],
                            in_=x1[:, ci * TOK:(ci + 1) * TOK])
            ch.finish(nc.gpsimd)
    return nc

def prep_D(inp, b_results, c_results):
    wo1 = head_tiles(np.ascontiguousarray(np.asarray(inp['L1_attn_Wo'])), 16)
    gt = chunk_tile(np.ascontiguousarray(np.asarray(inp['L1_ffn_gate_w']).T))
    w1 = np.asarray(inp['L1_ffn_w1'])
    w3 = np.asarray(inp['L1_ffn_w3'])
    w2 = np.asarray(inp['L1_ffn_w2'])
    w1t = np.stack([head_tiles(np.ascontiguousarray(w1[e].T), 8) for e in range(8)])
    w3t = np.stack([head_tiles(np.ascontiguousarray(w3[e].T), 8) for e in range(8)])
    w2t = np.stack([head_tiles(np.ascontiguousarray(w2[e].T), 16, hd=128)
                    for e in range(8)])
    ws1t = head_tiles(np.ascontiguousarray(np.asarray(inp['L1_ffn_ws1']).T), 8)
    ws3t = head_tiles(np.ascontiguousarray(np.asarray(inp['L1_ffn_ws3']).T), 8)
    ws2t = head_tiles(np.ascontiguousarray(np.asarray(inp['L1_ffn_ws2']).T), 16)
    maps = []
    for core in range(8):
        b, half = core // 2, core % 2
        cs = slice(half * TOK, (half + 1) * TOK)
        o_full = np.concatenate([c_results[2 * b]['o_out'][:, cs],
                                 c_results[2 * b + 1]['o_out'][:, cs]], axis=0)
        maps.append({
            'x1_half': b_results[core]['x1_half'],
            'o_full': np.ascontiguousarray(o_full),
            'wo1_t': wo1, 'gate_t': gt,
            'w1e_t': w1t, 'w3e_t': w3t, 'w2e_t': w2t,
            'ws1_t': ws1t, 'ws3_t': ws3t, 'ws2_t': ws2t,
            'ident_f32': CONSTS['ident_f32'],
        })
    return maps



def build_full():
    nc = bass.Bass("TRN2", target_bir_lowering=False)
    DT = nc.dram_tensor
    xin = DT("x_fm", [D, T], F32, kind="ExternalInput").ap()
    xhalf = DT("x_half", [D, TOK], F32, kind="ExternalInput").ap()
    nonced = DT("nonce8", [1, 8], mybir.dt.int32, kind="ExternalInput").ap()
    wq = DT("wq_t", [HPC, 128, 16 * HD], F32, kind="ExternalInput").ap()
    wk = DT("wk_t", [HPC, 128, 16 * HD], F32, kind="ExternalInput").ap()
    wv = DT("wv_t", [HPC, 128, 16 * HD], F32, kind="ExternalInput").ap()
    bqkv = DT("bqkv", [HD, 3 * HPC], F32, kind="ExternalInput").ap()
    cosd = DT("rope_cos", [HD, T], BF16, kind="ExternalInput").ap()
    sind = DT("rope_sin", [HD, T], BF16, kind="ExternalInput").ap()
    trid = DT("tri_keep", [128, 128], BF16, kind="ExternalInput").ap()
    idnd = DT("ident_bf", [128, 128], BF16, kind="ExternalInput").ap()
    wod = DT("wo_t", [16, 128, D], F32, kind="ExternalInput").ap()
    bod = DT("bo_s", [128, 16], F32, kind="ExternalInput").ap()
    w1d = DT("w1_t", [64, 128, D], F32, kind="ExternalInput").ap()
    w3d = DT("w3_t", [64, 128, D], F32, kind="ExternalInput").ap()
    w2d = DT("w2_t", [FH, D], F32, kind="ExternalInput").ap()
    h128d = DT("h128", [128, 128], BF16, kind="ExternalInput").ap()
    h64d = DT("h64blk", [128, 128], BF16, kind="ExternalInput").ap()
    wqd1 = DT("wq1_t", [HPC, 128, D], F32, kind="ExternalInput").ap()
    wkd1 = DT("wk1_t", [HPC, 128, D], F32, kind="ExternalInput").ap()
    wvd1 = DT("wv1_t", [HPC, 128, D], F32, kind="ExternalInput").ap()
    cqd = DT("cq_s", [HPC, 128, 4], F32, kind="ExternalInput").ap()
    ckd = DT("ck_s", [HPC, 128, 4], F32, kind="ExternalInput").ap()
    cvd = DT("cv_s", [HPC, 128, 4], F32, kind="ExternalInput").ap()
    alogd = DT("alog_s", [1, HPC], F32, kind="ExternalInput").ap()
    wfad = DT("wfa_t", [128, D], F32, kind="ExternalInput").ap()
    wfbd = DT("wfb_s", [128, HPC * 128], F32, kind="ExternalInput").ap()
    dtbd = DT("dtb_s", [HPC, 128], F32, kind="ExternalInput").ap()
    wbd = DT("wb_t", [128, 16 * HPC], F32, kind="ExternalInput").ap()
    wgad = DT("wga_t", [128, D], F32, kind="ExternalInput").ap()
    wgbd = DT("wgb_s", [128, HPC * 128], F32, kind="ExternalInput").ap()
    onwd = DT("onw", [128, 1], F32, kind="ExternalInput").ap()
    cpkd = DT("cumpack", [128, 1280], FP16, kind="ExternalInput").ap()
    mskd = DT("wm_masks", [8, 128, 256], BF16, kind="ExternalInput").ap()
    idnfd = DT("ident_f32", [128, 128], F32, kind="ExternalInput").ap()
    wod1 = DT("wo1_t", [16, 128, D], F32, kind="ExternalInput").ap()
    gted = DT("gate_t", [128, 16 * 8], F32, kind="ExternalInput").ap()
    w1ed = DT("w1e_t", [8, 8, 128, D], F32, kind="ExternalInput").ap()
    w3ed = DT("w3e_t", [8, 8, 128, D], F32, kind="ExternalInput").ap()
    w2ed = DT("w2e_g", [8, 8, 128, D], F32, kind="ExternalInput").ap()
    ws1d = DT("ws1_t", [8, 128, D], F32, kind="ExternalInput").ap()
    ws3d = DT("ws3_t", [8, 128, D], F32, kind="ExternalInput").ap()
    ws2d = DT("ws2_g", [8, 128, D], F32, kind="ExternalInput").ap()
    yo = DT("y_half", [D, TOK], F32, kind="ExternalOutput").ap()
    w2s = DT("w2s", [64, 128, D], BF16, kind="Internal").ap()
    x1dr = DT("x1dr", [D, TOK], F32, kind="Internal").ap()
    mbctx = DT("mb_ctx", [D, T], BF16, kind="Internal", addr_space="Shared").ap()
    mbx1 = DT("mb_x1", [D, T], BF16, kind="Internal", addr_space="Shared").ap()
    mbo = DT("mb_o", [D, T], BF16, kind="Internal", addr_space="Shared").ap()
    mbfl = DT("mb_fl", [8, 4], mybir.dt.int32, kind="Internal", addr_space="Shared").ap()

    st = ExitStack()
    sb = lambda n, s_, d_: st.enter_context(nc.sbuf_tensor(n, s_, d_))
    with st:
        xn = sb("xn", [128, 16 * T], BF16)
        big = sb("big", [128, 32768], BF16)
        wtile = sb("wtile", [128, D], BF16)
        wtile2 = sb("wtile2", [128, D], BF16)
        stats = sb("stats", [128, T], F32)
        stats2 = sb("stats2", [128, T], F32)
        scrB = sb("scrB", [128, 8192], BF16)
        scrF = sb("scrF", [128, 3072], F32)
        g_tm = sb("g_tm", [128, 16 * 128], FP16)
        xs4 = sb("xs4", [128, 512], F32)
        ytile = sb("ytile", [128, 512], F32)
        nonceb = sb("nonceb", [1, 8], mybir.dt.int32)
        biast = sb("biast", [128, 3 * HPC], F32)
        bob = sb("bob", [128, 16], F32)
        cwb = sb("cwb", [128, 12], F32)
        dtbrep = sb("dtbrep", [128, 128], F32)
        arep = sb("arep", [128, HPC], F32)
        onwb = sb("onwb", [128, 1], F32)
        gtb = sb("gtb", [128, 16 * 8], F32)
        cpkb = sb("cpkb", [128, 1280], FP16)
        mskb = sb("mskb", [128, 8 * 256], BF16)
        idnb = sb("idnb", [128, 128], BF16)
        idnf = sb("idnf", [128, 128], F32)
        trib = sb("trib", [128, 128], BF16)
        h128b = sb("h128b", [128, 128], BF16)
        h64b = sb("h64b", [128, 128], BF16)
        ones_f = sb("ones_f", [128, 128], F32)
        ones_b = sb("ones_b", [128, 128], BF16)
        ccell = sb("ccell", [128, 8], F32)
        wfbb = sb("wfbb", [128, HPC * 128], BF16)
        wgbb = sb("wgbb", [128, HPC * 128], BF16)
        wbb = sb("wbb", [128, 16 * HPC], BF16)
        Sf = sb("Sf", [128, 128], F32)
        svec = sb("svec", [8, 1], F32)
        psALL = st.enter_context(nc.psum_tensor("psALL", [128, 3584], F32))
        psTR = st.enter_context(nc.psum_tensor("psTR", [128, 1024], BF16))
        sem = st.enter_context(nc.semaphore("chain"))
        blk = st.enter_context(nc.Block())

        PS = lambda b_, w=512: psALL[:, b_ * 512:b_ * 512 + w]
        ps_trb = psTR[:, 0:128]

        pTt = scrB[:, 0:512]
        sqb = scrB[:, 512:1024]
        Eb = scrB[:, 1024:2304]
        KE = scrB[:, 2304:3584]
        QE = scrB[:, 3584:4864]
        WMb = scrB[:, 4864:5120]
        WMt = scrB[:, 5120:5376]
        Pb = scrB[:, 5376:6144]
        Qb = scrB[:, 6144:6912]
        zb = scrB[:, 6912:7040]
        zb2 = scrB[:, 7040:7168]
        ub = scrB[:, 7168:7296]
        rbb = scrB[:, 7296:7424]
        vtm = scrB[:, 7424:7552]
        kmt = scrB[:, 7552:7680]
        Sbb = scrB[:, 7680:7808]
        Atb = scrB[:, 7808:7936]
        Abb = scrB[:, 7936:8064]
        bndneg = scrB[:, 8064:8192]
        tmp_f = scrF[:, 0:512]
        recipr = scrF[:, 512:1024]
        tmpr = scrF[0:1, 1024:1536]
        tmpr2 = scrF[0:1, 1536:2048]
        t0f = scrF[:, 2048:2176]
        t1f = scrF[:, 2176:2304]
        bpad = scrF[:, 2304:2816]
        bT = scrF[:, 2816:2944]
        kmax = scrF[0:1, 2944:2952]
        lsb = scrF[0:8, 0:512]
        msS = scrF[0:8, 512:1024]
        spad = scrF[:, 2304:2816]
        stm = scrF[:, 2816:2880]
        m1r = scrF[:, 2880:2888]
        m1b = scrF[:, 2888:2896]
        cum = scrF[:, 2896:2904]
        s2b = scrF[:, 2904:2912]
        msum = scrF[:, 2912:2920]
        betE = scrF[:, 2952:3016]
        cosb = big[:, 0:2048]
        sinb = big[:, 2048:4096]
        qbA = big[:, 4096:6144]
        kbA = big[:, 6144:8192]
        vbA = big[:, 8192:10240]
        vtA = big[:, 10240:12288]
        ctxb = big[:, 12288:14336]
        rotb = big[:, 14336:16384]
        qcb = big[:, 0:2048]
        kcb = big[:, 2048:4096]
        vcb = big[:, 4096:6144]
        cvA = big[:, 6144:8192]
        cvB = big[:, 8192:10240]
        sgo = big[:, 10240:12288]
        ofm = big[:, 12288:14336]
        xfa = big[:, 14336:16384]
        xga = big[:, 16384:18432]
        mean_sb = stats[0:1, :]
        rstd_sb = stats2[0:1, :]
        colsq = stats[32:33, :]

        @blk.gpsimd
        def _body(_g):
            ch = Ch(nc, sem)
            sy, g, t, v, s = (E(ch, nc.sync), E(ch, nc.gpsimd), E(ch, nc.tensor),
                              E(ch, nc.vector), E(ch, nc.scalar))
            ge = nc.gpsimd
            rg0 = ge.alloc_register("rg0")
            rg1 = ge.alloc_register("rg1")
            rgn = ge.alloc_register("rgn")

            def arm_dmas(fn, n_ops):
                ch._flush()
                ge.wait_ge(sem, ch.val)
                for arm in ge.switch_core_id(n=128):
                    hh = arm.logical % 2
                    lst = fn(hh)
                    assert len(lst) == n_ops
                    for (dst, src) in lst:
                        ge.dma_start(out=dst, in_=src).then_inc(sem, 16)
                ch.val += 16 * n_ops
                ge.wait_ge(sem, ch.val)
                ch.last_eng = None
                ch.last_name = ''

            def spin(phase):
                ch._flush()
                ge.wait_ge(sem, ch.val)
                ge.load(rgn, nonceb[0:1, phase:phase + 1])
                lp, c2, en = f"sp{phase}", f"sc{phase}", f"se{phase}"
                ge.br(lp)
                eng1 = OrderedEngineSet([ge.engine])
                with nc.body(lp, valid_engines=eng1):
                    ge.load(rg0, mbfl[phase * 2:phase * 2 + 1, 0:1])
                    ge.load(rg1, mbfl[phase * 2 + 1:phase * 2 + 2, 0:1])
                    ge.br_cmp(rg0, rgn, on_true=c2, on_false=lp, comp_op="IS_EQ")
                with nc.body(c2, valid_engines=eng1):
                    ge.br_cmp(rg1, rgn, on_true=en, on_false=lp, comp_op="IS_EQ")
                nc.switch_bb(en)
                ge.sem_inc(sem, 1)
                ch.val += 1
                ge.wait_ge(sem, ch.val)
                ch.last_eng = None
                ch.last_name = ''

            def flag_write(phase):
                arm_dmas(lambda hh: [(mbfl[phase * 2 + hh:phase * 2 + hh + 1, 0:1],
                                      nonceb[0:1, phase:phase + 1])], 1)

            sy.dma_start(out=nonceb[:], in_=nonced[:, :])
            sy.dma_start(out=cosb[:], in_=cosd[:, :])
            sy.dma_start(out=sinb[:], in_=sind[:, :])
            sy.dma_start(out=trib[:], in_=trid[:, :])
            sy.dma_start(out=idnb[:], in_=idnd[:, :])
            sy.dma_start(out=idnf[:], in_=idnfd[:, :])
            sy.dma_start(out=biast[:, 0:3 * HPC], in_=bqkv[:, :])
            sy.dma_start(out=bob[:], in_=bod[:, :])
            sy.dma_start(out=h128b[:], in_=h128d[:, :])
            sy.dma_start(out=h64b[:], in_=h64d[:, :])
            sy.dma_start(out=cpkb[:], in_=cpkd[:, :])
            for i in range(8):
                sy.dma_start(out=mskb[:, i * 256:(i + 1) * 256], in_=mskd[i, :, :])
            sy.dma_start(out=onwb[:], in_=onwd[:, :])
            sy.dma_start(out=gtb[:], in_=gted[:, :])
            g.dma_start(out=wfbb[:], in_=wfbd[:, :])
            g.dma_start(out=wgbb[:], in_=wgbd[:, :])
            g.dma_start(out=wbb[:], in_=wbd[:, :])
            g.memset(ones_f[:], 1.0)
            g.memset(ones_b[:], 1.0)
            for i, cv in enumerate([1e-5, 1e-6, 1.0, float(np.log(128.0 ** -0.5))]):
                g.memset(ccell[:, i:i + 1], cv)
                nc.const_aps.register(F32, cv, ccell[:, i:i + 1])

            def ln_full(src_getter, nchunkT):
                for q in range(4):
                    qs = slice(q * 512, (q + 1) * 512)
                    for ci in range(16):
                        t.matmul(PS(6)[0:1, 0:512], ones_b[:, 0:1],
                                 src_getter(ci)[:, qs], start=(ci == 0), stop=(ci == 15))
                    v.tensor_scalar(out=mean_sb[0:1, qs], in0=PS(6)[0:1, 0:512],
                                    scalar1=1.0 / 2048.0, scalar2=None, op0=ALU.mult)
                    for ci in range(16):
                        s.activation(out=sqb[:, 0:512], in_=src_getter(ci)[:, qs],
                                     func=AF.Square)
                        t.matmul(PS(5)[0:1, 0:512], ones_b[:, 0:1], sqb[:, 0:512],
                                 start=(ci == 0), stop=(ci == 15))
                    v.tensor_scalar(out=rstd_sb[0:1, qs], in0=PS(5)[0:1, 0:512],
                                    scalar1=1.0 / 2048.0, scalar2=None, op0=ALU.mult)
                    s.activation(out=tmpr[0:1, 0:512], in_=mean_sb[0:1, qs],
                                 func=AF.Square)
                    v.tensor_tensor(out=rstd_sb[0:1, qs], in0=rstd_sb[0:1, qs],
                                    in1=tmpr[0:1, 0:512], op=ALU.subtract)
                    s.activation(out=tmpr[0:1, 0:512], in_=rstd_sb[0:1, qs],
                                 func=AF.Ln, bias=1e-5)
                    s.activation(out=rstd_sb[0:1, qs], in_=tmpr[0:1, 0:512],
                                 func=AF.Exp, scale=-0.5)
                for ci in range(16):
                    for q in range(4):
                        qs = slice(q * 512, (q + 1) * 512)
                        t.matmul(PS(6)[:, 0:512], ones_f[0:1, :], mean_sb[0:1, qs],
                                 start=True, stop=True)
                        v.tensor_tensor(out=tmp_f[:, 0:512], in0=src_getter(ci)[:, qs],
                                        in1=PS(6)[:, 0:512], op=ALU.subtract)
                        t.matmul(PS(6)[:, 0:512], ones_f[0:1, :], rstd_sb[0:1, qs],
                                 start=True, stop=True)
                        v.tensor_tensor(out=xn[:, ci * T + qs.start:ci * T + qs.stop],
                                        in0=tmp_f[:, 0:512], in1=PS(6)[:, 0:512],
                                        op=ALU.mult)

            for ci in range(16):
                g.dma_start(out=xn[:, ci * T:(ci + 1) * T],
                            in_=xin[ci * 128:(ci + 1) * 128, :])
            ln_full(lambda ci: xn[:, ci * T:(ci + 1) * T], 16)
            for h in range(HPC):
                g.dma_start(out=wtile[:, 0:2048], in_=wq[h, :, :])
                g.dma_start(out=wtile2[:, 0:2048], in_=wk[h, :, :])
                for q in range(4):
                    qs = slice(q * 512, (q + 1) * 512)
                    for ci in range(16):
                        xnc = xn[:, ci * T + qs.start:ci * T + qs.stop]
                        wsl = slice(ci * HD, (ci + 1) * HD)
                        t.matmul(PS(0)[:, 0:512], wtile[:, wsl], xnc,
                                 start=(ci == 0), stop=(ci == 15))
                        t.matmul(PS(1)[:, 0:512], wtile2[:, wsl], xnc,
                                 start=(ci == 0), stop=(ci == 15))
                    s.add(qbA[:, qs], PS(0)[:, 0:512], biast[:, h:h + 1])
                    s.add(kbA[:, qs], PS(1)[:, 0:512], biast[:, HPC + h:HPC + h + 1])
                g.dma_start(out=wtile[:, 0:2048], in_=wv[h, :, :])
                for q in range(4):
                    qs = slice(q * 512, (q + 1) * 512)
                    for ci in range(16):
                        xnc = xn[:, ci * T + qs.start:ci * T + qs.stop]
                        t.matmul(PS(2)[:, 0:512], wtile[:, ci * HD:(ci + 1) * HD],
                                 xnc, start=(ci == 0), stop=(ci == 15))
                    s.add(tmp_f[:, 0:512], PS(2)[:, 0:512],
                          biast[:, 2 * HPC + h:2 * HPC + h + 1])
                    v.tensor_copy(vbA[:, qs], tmp_f[:, 0:512])
                for src in (qbA, kbA):
                    s.copy(rotb[0:64, :], src[64:128, :])
                    s.copy(rotb[64:128, :], src[0:64, :])
                    v.tensor_tensor(out=rotb[:, :], in0=rotb[:, :], in1=sinb[:, :],
                                    op=ALU.mult)
                    for q in range(4):
                        qs = slice(q * 512, (q + 1) * 512)
                        v.tensor_tensor(out=pTt[:, 0:512], in0=src[:, qs],
                                        in1=cosb[:, qs], op=ALU.mult)
                        v.tensor_tensor(out=src[:, qs], in0=pTt[:, 0:512],
                                        in1=rotb[:, qs], op=ALU.add)
                for kt in range(16):
                    ks = slice(kt * 128, (kt + 1) * 128)
                    t.transpose(ps_trb, vbA[:, ks], idnb[:])
                    v.tensor_copy(vtA[:, ks], ps_trb)
                for q in range(4):
                    qs = slice(q * 512, (q + 1) * 512)
                    s.activation(out=sqb[:, 0:512], in_=kbA[:, qs], func=AF.Square)
                    t.matmul(PS(0)[0:1, 0:512], ones_b[:, 0:1], sqb[:, 0:512],
                             start=True, stop=True)
                    v.tensor_copy(colsq[0:1, qs], PS(0)[0:1, 0:512])
                v.reduce_max(kmax[0:1, 0:1], colsq[0:1, :], axis=AX.X)
                for q in range(4):
                    qs = slice(q * 512, (q + 1) * 512)
                    s.activation(out=sqb[:, 0:512], in_=qbA[:, qs], func=AF.Square)
                    t.matmul(PS(0)[0:1, 0:512], ones_b[:, 0:1], sqb[:, 0:512],
                             start=True, stop=True)
                    v.tensor_scalar(out=tmpr[0:1, 0:512], in0=PS(0)[0:1, 0:512],
                                    scalar1=kmax[0:1, 0:1], scalar2=None, op0=ALU.mult)
                    s.activation(out=tmpr[0:1, 0:512], in_=tmpr[0:1, 0:512],
                                 func=AF.Sqrt)
                    v.tensor_scalar(out=colsq[0:1, qs], in0=tmpr[0:1, 0:512],
                                    scalar1=-1.0, scalar2=None, op0=ALU.mult)
                for q in range(4):
                    qs = slice(q * 512, (q + 1) * 512)
                    nkt = 4 * (q + 1)
                    v.tensor_copy(sqb[0:1, 0:512], colsq[0:1, qs])
                    for kt in range(nkt):
                        ks = slice(kt * 128, (kt + 1) * 128)
                        t.matmul(PS(3)[:, 0:512], kbA[:, ks], qbA[:, qs],
                                 start=True, stop=False)
                        t.matmul(PS(3)[:, 0:512], ones_b[0:1, :], sqb[0:1, 0:512],
                                 start=False, stop=True)
                        s.activation(out=pTt[:, 0:512], in_=PS(3)[:, 0:512],
                                     func=AF.Exp, scale=1.0 / np.sqrt(128.0))
                        if kt * 128 >= qs.start:
                            sub = kt * 128 - qs.start
                            if sub > 0:
                                v.memset(pTt[:, 0:sub], 0.0)
                            v.tensor_tensor(out=pTt[:, sub:sub + 128],
                                            in0=pTt[:, sub:sub + 128], in1=trib[:],
                                            op=ALU.mult)
                        t.matmul(PS(0)[0:1, 0:512], ones_b[:, 0:1], pTt[:, 0:512],
                                 start=(kt == 0), stop=(kt == nkt - 1))
                        t.matmul(PS(4)[:, 0:512], vtA[:, ks], pTt[:, 0:512],
                                 start=(kt == 0), stop=(kt == nkt - 1))
                    v.reciprocal(tmpr[0:1, 0:512], PS(0)[0:1, 0:512])
                    t.matmul(PS(6)[:, 0:512], ones_f[0:1, :], tmpr[0:1, 0:512],
                             start=True, stop=True)
                    v.tensor_copy(recipr[:, 0:512], PS(6)[:, 0:512])
                    v.tensor_tensor(out=ctxb[:, qs], in0=PS(4)[:, 0:512],
                                    in1=recipr[:, 0:512], op=ALU.mult)
                arm_dmas(lambda hh: [(mbctx[(hh * 8 + h) * 128:(hh * 8 + h + 1) * 128, :],
                                      ctxb[:, 0:T])], 1)
            flag_write(0)

            def b_arm(hh):
                lst = []
                for ci in range(16):
                    srcx = xn[:, ci * T + hh * TOK:ci * T + hh * TOK + TOK]
                    dstx = xn[:, ci * T:ci * T + TOK] if hh == 1 else scrB[:, 1024:2048]
                    lst.append((dstx, srcx))
                return lst
            arm_dmas(b_arm, 16)
            for n in range(64):
                g.dma_start(out=wtile[:], in_=w2d[n * 128:(n + 1) * 128, :])
                for q in range(4):
                    t.matmul(PS(0)[:, 0:512], h128b[:],
                             wtile[:, q * 512:(q + 1) * 512], start=True, stop=True)
                    v.tensor_copy(wtile2[:, q * 512:(q + 1) * 512], PS(0)[:, 0:512])
                g.dma_start(out=w2s[:, n:n + 65:64, :], in_=wtile2[:, :])
            spin(0)
            arm_dmas(lambda hh: [(xn[:, ci * T + TOK:(ci + 1) * T],
                                  mbctx[ci * 128:(ci + 1) * 128,
                                        hh * TOK:(hh + 1) * TOK])
                                 for ci in range(16)], 16)
            for th in range(2):
                ts_ = slice(th * 512, (th + 1) * 512)
                for hb in range(64):
                    g.dma_start(out=wtile[:], in_=w1d[hb, :, :])
                    g.dma_start(out=wtile2[:], in_=w3d[hb, :, :])
                    for ci in range(16):
                        xnc = xn[:, ci * T + ts_.start:ci * T + ts_.stop]
                        t.matmul(PS(0)[:, 0:512], wtile[:, ci * 128:(ci + 1) * 128],
                                 xnc, start=(ci == 0), stop=(ci == 15))
                        t.matmul(PS(1)[:, 0:512], wtile2[:, ci * 128:(ci + 1) * 128],
                                 xnc, start=(ci == 0), stop=(ci == 15))
                    s.activation(out=sqb[:, 0:512], in_=PS(0)[:, 0:512], func=AF.Silu)
                    v.tensor_tensor(out=pTt[:, 0:512], in0=sqb[:, 0:512],
                                    in1=PS(1)[:, 0:512], op=ALU.mult)
                    t.matmul(PS(2)[:, 0:512], h64b[:], pTt[:, 0:512],
                             start=True, stop=True)
                    s.mul(big[:, hb * 512:(hb + 1) * 512],
                          PS(2)[:, 0:512], 1.0 / np.sqrt(8192.0))
                for grp in range(4):
                    for j in range(4):
                        dmb = grp * 4 + j
                        g.dma_start(out=wtile[:], in_=wod[dmb, :, :])
                        for ci in range(16):
                            t.matmul(PS(j)[:, 0:512],
                                     wtile[:, ci * 128:(ci + 1) * 128],
                                     xn[:, ci * T + TOK + ts_.start:ci * T + TOK + ts_.stop],
                                     start=(ci == 0), stop=False)
                    for hc in range(64):
                        g.dma_start(out=wtile2[:, 0:512],
                                    in_=w2s[hc, :, grp * 512:(grp + 1) * 512])
                        for j in range(4):
                            t.matmul(PS(j)[:, 0:512],
                                     wtile2[:, j * 128:(j + 1) * 128],
                                     big[:, hc * 512:(hc + 1) * 512],
                                     start=False, stop=(hc == 63))
                    for j in range(4):
                        dmb = grp * 4 + j
                        g.dma_start(out=xs4[:, 0:512],
                                    in_=xhalf[dmb * 128:(dmb + 1) * 128, ts_])
                        v.scalar_tensor_tensor(out=ytile[:, 0:512], in0=PS(j)[:, 0:512],
                                               scalar=bob[:, dmb:dmb + 1],
                                               in1=xs4[:, 0:512], op0=ALU.add,
                                               op1=ALU.add)
                        g.dma_start(out=x1dr[dmb * 128:(dmb + 1) * 128, ts_],
                                    in_=ytile[:, 0:512])
                        arm_dmas(lambda hh: [(mbx1[dmb * 128:(dmb + 1) * 128,
                                                   hh * TOK + ts_.start:hh * TOK + ts_.stop],
                                              ytile[:, 0:512])], 1)
            flag_write(1)

            spin(1)
            for ci in range(16):
                g.dma_start(out=xn[:, ci * T:(ci + 1) * T],
                            in_=mbx1[ci * 128:(ci + 1) * 128, :])
            ln_full(lambda ci: xn[:, ci * T:(ci + 1) * T], 16)
            for (wdr2, dst) in ((wfad, xfa), (wgad, xga)):
                g.dma_start(out=wtile[:], in_=wdr2[:, :])
                wsb = wtile
                for q in range(4):
                    qs = slice(q * 512, (q + 1) * 512)
                    for ci in range(16):
                        t.matmul(PS(0)[:, 0:512], wsb[:, ci * 128:(ci + 1) * 128],
                                 xn[:, ci * T + qs.start:ci * T + qs.stop],
                                 start=(ci == 0), stop=(ci == 15))
                    v.tensor_copy(dst[:, qs], PS(0)[:, 0:512])
            g.dma_start(out=arep[0:1, 0:HPC], in_=alogd[:, :])
            s.activation(out=arep[0:1, 0:HPC], in_=arep[0:1, 0:HPC], func=AF.Exp)
            t.matmul(PS(0)[:, 0:HPC], ones_f[0:1, :], arep[0:1, 0:HPC],
                     start=True, stop=True)
            v.tensor_copy(arep[:, 0:HPC], PS(0)[:, 0:HPC])
            for q in range(4):
                qs = slice(q * 512, (q + 1) * 512)
                for ci in range(16):
                    t.matmul(PS(0)[0:HPC, 0:512], wbb[:, ci * HPC:(ci + 1) * HPC],
                             xn[:, ci * T + qs.start:ci * T + qs.stop],
                             start=(ci == 0), stop=(ci == 15))
                s.activation(out=bpad[0:HPC, 0:512], in_=PS(0)[0:HPC, 0:512],
                             func=AF.Sigmoid)
                for tl in range(4):
                    tt2 = q * 4 + tl
                    v.tensor_copy(sqb[:, 0:128], bpad[:, tl * 128:(tl + 1) * 128])
                    t.transpose(ps_trb, sqb[:, 0:128], idnb[:])
                    v.tensor_copy(bT[:, tt2 * HPC:tt2 * HPC + HPC], ps_trb[:, 0:HPC])
            for h in range(HPC):
                g.dma_start(out=cwb[:, 0:4], in_=cqd[h, :, :])
                g.dma_start(out=cwb[:, 4:8], in_=ckd[h, :, :])
                g.dma_start(out=cwb[:, 8:12], in_=cvd[h, :, :])
                for (wdr, dst) in ((wqd1, qcb), (wkd1, kcb), (wvd1, vcb)):
                    g.dma_start(out=wtile[:], in_=wdr[h, :, :])
                    for q in range(4):
                        qs = slice(q * 512, (q + 1) * 512)
                        for ci in range(16):
                            t.matmul(PS(0)[:, 0:512], wtile[:, ci * 128:(ci + 1) * 128],
                                     xn[:, ci * T + qs.start:ci * T + qs.stop],
                                     start=(ci == 0), stop=(ci == 15))
                        v.tensor_copy(dst[:, qs], PS(0)[:, 0:512])
                for ti, src in enumerate((qcb, kcb, vcb)):
                    cw = cwb[:, ti * 4:(ti + 1) * 4]
                    v.tensor_scalar(out=cvA[:, :], in0=src[:, :],
                                    scalar1=cw[:, 3:4], scalar2=None, op0=ALU.mult)
                    bufs = [cvA, cvB]
                    for si, jj in enumerate((2, 1, 0)):
                        sh = 3 - jj
                        a_, b_ = bufs[si % 2], bufs[(si + 1) % 2]
                        v.scalar_tensor_tensor(out=b_[:, sh:T], in0=src[:, 0:T - sh],
                                               scalar=cw[:, jj:jj + 1], in1=a_[:, sh:T],
                                               op0=ALU.mult, op1=ALU.add)
                        v.tensor_copy(b_[:, 0:sh], a_[:, 0:sh])
                    s.activation(out=src[:, :], in_=cvB[:, :], func=AF.Silu)
                for ti, src in enumerate((qcb, kcb)):
                    for q in range(4):
                        qs = slice(q * 512, (q + 1) * 512)
                        s.activation(out=sqb[:, 0:512], in_=src[:, qs], func=AF.Square)
                        t.matmul(PS(0)[0:1, 0:512], ones_b[:, 0:1], sqb[:, 0:512],
                                 start=True, stop=True)
                        s.activation(out=tmpr[0:1, 0:512], in_=PS(0)[0:1, 0:512],
                                     func=AF.Ln, bias=1e-6)
                        if ti == 0:
                            s.activation(out=tmpr[0:1, 0:512], in_=tmpr[0:1, 0:512],
                                         func=AF.Exp, scale=-0.5,
                                         bias=float(np.log(128.0 ** -0.5)))
                        else:
                            s.activation(out=tmpr[0:1, 0:512], in_=tmpr[0:1, 0:512],
                                         func=AF.Exp, scale=-0.5)
                        t.matmul(PS(0)[:, 0:512], ones_f[0:1, :], tmpr[0:1, 0:512],
                                 start=True, stop=True)
                        v.tensor_tensor(out=src[:, qs], in0=src[:, qs],
                                        in1=PS(0)[:, 0:512], op=ALU.mult)
                g.dma_start(out=dtbrep[0:1, 0:128], in_=dtbd[h:h + 1, :])
                t.matmul(PS(0)[:, 0:128], ones_f[0:1, :], dtbrep[0:1, 0:128],
                         start=True, stop=True)
                v.tensor_copy(dtbrep[:, 0:128], PS(0)[:, 0:128])
                for tt2 in range(16):
                    t.matmul(PS(1)[:, 0:128], xfa[:, tt2 * 128:(tt2 + 1) * 128],
                             wfbb[:, h * 128:(h + 1) * 128], start=True, stop=True)
                    v.tensor_tensor(out=t0f[:, :], in0=PS(1)[:, 0:128],
                                    in1=dtbrep[:, 0:128], op=ALU.add)
                    s.activation(out=t1f[:, :], in_=t0f[:, :], func=AF.Abs)
                    s.activation(out=t1f[:, :], in_=t1f[:, :], func=AF.Exp, scale=-1.0)
                    s.activation(out=t1f[:, :], in_=t1f[:, :], func=AF.Ln, bias=1.0)
                    v.tensor_scalar(out=t0f[:, :], in0=t0f[:, :], scalar1=0.0,
                                    scalar2=None, op0=ALU.max)
                    v.tensor_tensor(out=t0f[:, :], in0=t0f[:, :], in1=t1f[:, :],
                                    op=ALU.add)
                    v.tensor_scalar(out=g_tm[:, tt2 * 128:(tt2 + 1) * 128],
                                    in0=t0f[:, :], scalar1=arep[:, h:h + 1],
                                    scalar2=None, op0=ALU.mult)
                for q in range(4):
                    qs = slice(q * 512, (q + 1) * 512)
                    t.matmul(PS(0)[:, 0:512], wgbb[:, h * 128:(h + 1) * 128],
                             xga[:, qs], start=True, stop=True)
                    s.activation(out=sgo[:, qs], in_=PS(0)[:, 0:512], func=AF.Sigmoid)
                g.memset(Sf[:], 0.0)
                g.memset(Sbb[:, :], 0.0)
                for ck in range(16):
                    cs = slice(ck * 128, (ck + 1) * 128)
                    gsl = g_tm[:, ck * 128:(ck + 1) * 128]
                    for mi in range(3):
                        fs = slice(mi * 512, min((mi + 1) * 512, 1280))
                        t.matmul(psALL[:, fs], gsl, cpkb[:, fs], start=True, stop=True)
                    s.activation(out=Eb[:, :], in_=psALL[:, 0:1280], func=AF.Exp)
                    for si in range(1, 10):
                        sl = slice(si * 128, (si + 1) * 128)
                        v.tensor_tensor(out=KE[:, sl.start:sl.stop], in0=Eb[:, sl],
                                        in1=kcb[:, cs], op=ALU.mult)
                    for si in list(range(1, 8)) + [8]:
                        sl = slice(si * 128, (si + 1) * 128)
                        v.tensor_tensor(out=QE[:, sl.start:sl.stop], in0=Eb[:, sl],
                                        in1=qcb[:, cs], op=ALU.mult)
                    for si in range(8):
                        bsl = kcb[:, cs] if si == 0 else KE[:, si * 128:(si + 1) * 128]
                        aq = qcb[:, cs] if si == 0 else QE[:, si * 128:(si + 1) * 128]
                        t.matmul(PS(6)[:, 0:128], bsl,
                                 kcb[:, cs] if si == 0 else KE[:, si * 128:(si + 1) * 128],
                                 start=True, stop=True)
                        t.matmul(PS(6)[:, 128:256], bsl, aq, start=True, stop=True)
                        if si == 0:
                            v.tensor_tensor(out=WMb[:, :], in0=PS(6)[:, 0:256],
                                            in1=mskb[:, 0:256], op=ALU.mult)
                        else:
                            v.tensor_tensor(out=WMt[:, :], in0=PS(6)[:, 0:256],
                                            in1=mskb[:, si * 256:(si + 1) * 256],
                                            op=ALU.mult)
                            v.tensor_tensor(out=WMb[:, :], in0=WMb[:, :],
                                            in1=WMt[:, :], op=ALU.add)
                    bcol = bT[:, ck * HPC + h:ck * HPC + h + 1]
                    v.tensor_scalar(out=Atb[:, :], in0=WMb[:, 0:128],
                                    scalar1=bcol, scalar2=None, op0=ALU.mult)
                    t.transpose(ps_trb, Atb[:, :], idnb[:])
                    v.tensor_copy(Abb[:, :], ps_trb)
                    t.transpose(ps_trb, vcb[:, cs], idnb[:])
                    v.tensor_copy(vtm[:, :], ps_trb)
                    t.matmul(PS(5)[:, 0:128], KE[:, 8 * 128:9 * 128], Sbb[:, :],
                             start=True, stop=True)
                    v.tensor_tensor(out=rbb[:, :], in0=vtm[:, :], in1=PS(5)[:, 0:128],
                                    op=ALU.subtract)
                    prevP, prevQ = Abb[:, :], Atb[:, :]
                    for jj in range(1, 7):
                        if jj < 6:
                            t.matmul(PS(5)[:, 128:256], prevQ, prevP,
                                     start=True, stop=True)
                            v.tensor_copy(Pb[:, (jj - 1) * 128:jj * 128],
                                          PS(5)[:, 128:256])
                        t.matmul(PS(5)[:, 256:384], prevP, prevQ, start=True, stop=True)
                        v.tensor_copy(Qb[:, (jj - 1) * 128:jj * 128], PS(5)[:, 256:384])
                        prevP = Pb[:, (jj - 1) * 128:jj * 128]
                        prevQ = Qb[:, (jj - 1) * 128:jj * 128]
                    zcur, znxt = zb, zb2
                    v.tensor_copy(zcur[:, :], rbb[:, :])
                    for jj in range(6, 0, -1):
                        t.matmul(PS(5)[:, 128:256], Qb[:, (jj - 1) * 128:jj * 128],
                                 zcur[:, :], start=True, stop=True)
                        v.tensor_tensor(out=znxt[:, :], in0=zcur[:, :],
                                        in1=PS(5)[:, 128:256], op=ALU.add)
                        zcur, znxt = znxt, zcur
                    t.matmul(PS(5)[:, 128:256], Atb[:, :], zcur[:, :],
                             start=True, stop=True)
                    v.tensor_tensor(out=znxt[:, :], in0=zcur[:, :],
                                    in1=PS(5)[:, 128:256], op=ALU.subtract)
                    v.tensor_scalar(out=ub[:, :], in0=znxt[:, :], scalar1=bcol,
                                    scalar2=None, op0=ALU.mult)
                    t.matmul(PS(5)[:, 256:384], ub[:, :], WMb[:, 128:256],
                             start=True, stop=False)
                    t.matmul(PS(5)[:, 256:384], Sbb[:, :], QE[:, 8 * 128:9 * 128],
                             start=False, stop=True)
                    v.tensor_copy(ofm[:, cs], PS(5)[:, 256:384])
                    t.transpose(ps_trb, KE[:, 9 * 128:10 * 128], idnb[:])
                    v.tensor_copy(kmt[:, :], ps_trb)
                    t.matmul(PS(5)[:, 0:128], kmt[:, :], ub[:, :], start=True, stop=True)
                    v.scalar_tensor_tensor(out=Sf[:, :], in0=Sf[:, :],
                                           scalar=Eb[:, 8 * 128 + 127:8 * 128 + 128],
                                           in1=PS(5)[:, 0:128], op0=ALU.mult,
                                           op1=ALU.add)
                    v.tensor_copy(Sbb[:, :], Sf[:, :])
                for q in range(4):
                    qs = slice(q * 512, (q + 1) * 512)
                    s.activation(out=sqb[:, 0:512], in_=ofm[:, qs], func=AF.Square)
                    t.matmul(PS(0)[0:1, 0:512], ones_b[:, 0:1], sqb[:, 0:512],
                             start=True, stop=True)
                    v.tensor_scalar(out=tmpr[0:1, 0:512], in0=PS(0)[0:1, 0:512],
                                    scalar1=1.0 / 128.0, scalar2=None, op0=ALU.mult)
                    s.activation(out=tmpr[0:1, 0:512], in_=tmpr[0:1, 0:512],
                                 func=AF.Ln, bias=1e-6)
                    s.activation(out=tmpr[0:1, 0:512], in_=tmpr[0:1, 0:512],
                                 func=AF.Exp, scale=-0.5)
                    t.matmul(PS(0)[:, 0:512], ones_f[0:1, :], tmpr[0:1, 0:512],
                             start=True, stop=True)
                    v.tensor_copy(recipr[:, 0:512], PS(0)[:, 0:512])
                    v.scalar_tensor_tensor(out=ofm[:, qs], in0=ofm[:, qs],
                                           scalar=onwb[:, 0:1], in1=sgo[:, qs],
                                           op0=ALU.mult, op1=ALU.mult)
                    v.tensor_tensor(out=ofm[:, qs], in0=ofm[:, qs],
                                    in1=recipr[:, 0:512], op=ALU.mult)
                arm_dmas(lambda hh: [(mbo[(hh * 8 + h) * 128:(hh * 8 + h + 1) * 128, :],
                                      ofm[:, 0:T])], 1)
            flag_write(2)

            def d_arm(hh):
                lst = []
                for ci in range(16):
                    srcx = xn[:, ci * T + hh * TOK:ci * T + hh * TOK + TOK]
                    dstx = xn[:, ci * T:ci * T + TOK] if hh == 1 else scrB[:, 1024:2048]
                    lst.append((dstx, srcx))
                lst.append((mean_sb[0:1, 0:TOK] if hh == 1 else stats[64:65, 0:TOK],
                            mean_sb[0:1, hh * TOK:hh * TOK + TOK]))
                lst.append((rstd_sb[0:1, 0:TOK] if hh == 1 else stats[64:65, 0:TOK],
                            rstd_sb[0:1, hh * TOK:hh * TOK + TOK]))
                return lst
            arm_dmas(d_arm, 18)
            for ci in range(16):
                t.matmul(PS(0)[0:1, 0:8], ones_f[:, 0:1], gtb[:, ci * 8:(ci + 1) * 8],
                         start=(ci == 0), stop=(ci == 15))
            v.tensor_copy(spad[0:1, 0:8], PS(0)[0:1, 0:8])
            t.transpose(PS(6)[:, 0:128], spad[:, 0:128], idnf[:])
            v.tensor_copy(svec[0:8, 0:1], PS(6)[0:8, 0:1])
            for q in range(2):
                qs = slice(q * 512, (q + 1) * 512)
                for ci in range(16):
                    g.dma_start(out=xs4[:, 0:512],
                                in_=x1dr[ci * 128:(ci + 1) * 128, qs])
                    t.matmul(PS(0)[0:8, 0:512], gtb[:, ci * 8:(ci + 1) * 8],
                             xs4[:, 0:512], start=(ci == 0), stop=(ci == 15))
                v.tensor_copy(lsb[:, 0:512], PS(0)[0:8, 0:512])
                t.matmul(PS(1)[0:8, 0:512], ones_f[0:1, 0:8], mean_sb[0:1, qs],
                         start=True, stop=True)
                v.tensor_scalar(out=msS[:, 0:512], in0=PS(1)[0:8, 0:512],
                                scalar1=svec[0:8, 0:1], scalar2=None, op0=ALU.mult)
                v.tensor_tensor(out=lsb[:, 0:512], in0=lsb[:, 0:512],
                                in1=msS[:, 0:512], op=ALU.subtract)
                t.matmul(PS(1)[0:8, 0:512], ones_f[0:1, 0:8], rstd_sb[0:1, qs],
                         start=True, stop=True)
                v.tensor_tensor(out=lsb[:, 0:512], in0=lsb[:, 0:512],
                                in1=PS(1)[0:8, 0:512], op=ALU.mult)
                s.activation(out=spad[0:8, 0:512], in_=lsb[:, 0:512], func=AF.Sigmoid)
                for tl in range(4):
                    tt2 = q * 4 + tl
                    t.transpose(PS(6)[:, 0:128], spad[:, tl * 128:(tl + 1) * 128],
                                idnf[:])
                    v.tensor_copy(stm[:, tt2 * 8:(tt2 + 1) * 8], PS(6)[:, 0:8])
            for tt2 in range(8):
                ssl = stm[:, tt2 * 8:(tt2 + 1) * 8]
                v.reduce_max(m1r[:, 0:1], ssl, axis=AX.X)
                v.tensor_scalar(out=m1r[:, 0:8], in0=ssl, scalar1=m1r[:, 0:1],
                                scalar2=None, op0=ALU.is_ge)
                v.tensor_tensor_scan(out=cum[:, 0:8], data0=ones_f[:, 0:8],
                                     data1=m1r[:, 0:8], initial=0.0,
                                     op0=ALU.mult, op1=ALU.add)
                v.tensor_scalar(out=cum[:, 0:8], in0=cum[:, 0:8], scalar1=1.0,
                                scalar2=None, op0=ALU.is_equal)
                v.tensor_tensor(out=m1b[:, 0:8], in0=cum[:, 0:8], in1=m1r[:, 0:8],
                                op=ALU.mult)
                v.scalar_tensor_tensor(out=s2b[:, 0:8], in0=m1b[:, 0:8], scalar=-2.0,
                                       in1=ssl, op0=ALU.mult, op1=ALU.add)
                v.reduce_max(m1r[:, 0:1], s2b[:, 0:8], axis=AX.X)
                v.tensor_scalar(out=m1r[:, 0:8], in0=s2b[:, 0:8], scalar1=m1r[:, 0:1],
                                scalar2=None, op0=ALU.is_ge)
                v.tensor_tensor_scan(out=cum[:, 0:8], data0=ones_f[:, 0:8],
                                     data1=m1r[:, 0:8], initial=0.0,
                                     op0=ALU.mult, op1=ALU.add)
                v.tensor_scalar(out=cum[:, 0:8], in0=cum[:, 0:8], scalar1=1.0,
                                scalar2=None, op0=ALU.is_equal)
                v.tensor_tensor(out=cum[:, 0:8], in0=cum[:, 0:8], in1=m1r[:, 0:8],
                                op=ALU.mult)
                v.tensor_tensor(out=msum[:, 0:8], in0=m1b[:, 0:8], in1=cum[:, 0:8],
                                op=ALU.add)
                v.tensor_tensor(out=betE[:, tt2 * 8:(tt2 + 1) * 8], in0=ssl,
                                in1=msum[:, 0:8], op=ALU.mult)
            spin(2)
            arm_dmas(lambda hh: [(big[:, ci * TOK:(ci + 1) * TOK],
                                  mbo[ci * 128:(ci + 1) * 128,
                                      hh * TOK:(hh + 1) * TOK])
                                 for ci in range(16)], 16)
            def hslot(idx):
                if idx < 32:
                    return big[:, 16384 + idx * 512:16384 + (idx + 1) * 512]
                if idx < 64:
                    k = idx - 32
                    o = (k // 2) * T + TOK + (k % 2) * 512
                    return xn[:, o:o + 512]
                return scrB[:, 1024 + (idx - 64) * 512:1024 + (idx - 63) * 512]
            for th in range(2):
                ts_ = slice(th * 512, (th + 1) * 512)
                for e in range(9):
                    if e < 8:
                        for tl in range(4):
                            tt2 = th * 4 + tl
                            v.tensor_scalar(out=spad[:, 0:128], in0=ones_f[:, 0:128],
                                            scalar1=betE[:, tt2 * 8 + e:tt2 * 8 + e + 1],
                                            scalar2=None, op0=ALU.mult)
                            t.transpose(PS(6)[:, 0:128], spad[:, 0:128], idnf[:])
                            v.tensor_copy(recipr[:, tl * 128:(tl + 1) * 128],
                                          PS(6)[:, 0:128])
                    for hb in range(8):
                        if e < 8:
                            g.dma_start(out=wtile[:], in_=w1ed[e, hb, :, :])
                            g.dma_start(out=wtile2[:], in_=w3ed[e, hb, :, :])
                        else:
                            g.dma_start(out=wtile[:], in_=ws1d[hb, :, :])
                            g.dma_start(out=wtile2[:], in_=ws3d[hb, :, :])
                        for ci in range(16):
                            xnc = xn[:, ci * T + ts_.start:ci * T + ts_.stop]
                            t.matmul(PS(0)[:, 0:512],
                                     wtile[:, ci * 128:(ci + 1) * 128], xnc,
                                     start=(ci == 0), stop=(ci == 15))
                            t.matmul(PS(1)[:, 0:512],
                                     wtile2[:, ci * 128:(ci + 1) * 128], xnc,
                                     start=(ci == 0), stop=(ci == 15))
                        hs_ = hslot(e * 8 + hb)
                        if e < 8:
                            v.tensor_tensor(out=pTt[:, 0:512], in0=PS(0)[:, 0:512],
                                            in1=recipr[:, 0:512], op=ALU.mult)
                            s.activation(out=sqb[:, 0:512], in_=pTt[:, 0:512],
                                         func=AF.Silu)
                            v.tensor_tensor(out=pTt[:, 0:512], in0=PS(1)[:, 0:512],
                                            in1=recipr[:, 0:512], op=ALU.mult)
                            v.tensor_tensor(out=hs_, in0=sqb[:, 0:512],
                                            in1=pTt[:, 0:512], op=ALU.mult)
                        else:
                            s.activation(out=sqb[:, 0:512], in_=PS(0)[:, 0:512],
                                         func=AF.Silu)
                            v.tensor_copy(pTt[:, 0:512], PS(1)[:, 0:512])
                            v.tensor_tensor(out=hs_, in0=sqb[:, 0:512],
                                            in1=pTt[:, 0:512], op=ALU.mult)
                for grp in range(4):
                    for j in range(4):
                        dmb = grp * 4 + j
                        g.dma_start(out=wtile[:], in_=wod1[dmb, :, :])
                        for ci in range(16):
                            t.matmul(PS(j)[:, 0:512],
                                     wtile[:, ci * 128:(ci + 1) * 128],
                                     big[:, ci * TOK + ts_.start:ci * TOK + ts_.stop],
                                     start=(ci == 0), stop=False)
                    for e in range(9):
                        for hc in range(8):
                            g.dma_start(out=wtile2[:, 0:512],
                                        in_=(w2ed[e, hc, :, grp * 512:(grp + 1) * 512]
                                             if e < 8 else
                                             ws2d[hc, :, grp * 512:(grp + 1) * 512]))
                            hs_ = hslot(e * 8 + hc)
                            for j in range(4):
                                last = (e == 8 and hc == 7 and j == 3)
                                t.matmul(PS(j)[:, 0:512],
                                         wtile2[:, j * 128:(j + 1) * 128], hs_,
                                         start=False,
                                         stop=(e == 8 and hc == 7))
                    for j in range(4):
                        dmb = grp * 4 + j
                        g.dma_start(out=xs4[:, 0:512],
                                    in_=x1dr[dmb * 128:(dmb + 1) * 128, ts_])
                        v.tensor_tensor(out=ytile[:, 0:512], in0=PS(j)[:, 0:512],
                                        in1=xs4[:, 0:512], op=ALU.add)
                        g.dma_start(out=yo[dmb * 128:(dmb + 1) * 128, ts_],
                                    in_=ytile[:, 0:512])
            flag_write(3)
            spin(3)
            ch.finish(nc.gpsimd)
    return nc


def prep_full(inp):
    import time
    x = np.asarray(inp['L0_attn_hidden_states'])
    wqT = head_tiles(np.ascontiguousarray(np.asarray(inp['L0_attn_Wq']).T), NH)
    wkT = head_tiles(np.ascontiguousarray(np.asarray(inp['L0_attn_Wk']).T), NH)
    wvT = head_tiles(np.ascontiguousarray(np.asarray(inp['L0_attn_Wv']).T), NH)
    bq = np.asarray(inp['L0_attn_bq']).reshape(NH, HD)
    bk = np.asarray(inp['L0_attn_bk']).reshape(NH, HD)
    bv = np.asarray(inp['L0_attn_bv']).reshape(NH, HD)
    woT = head_tiles(np.ascontiguousarray(np.asarray(inp['L0_attn_Wo']).T), 16)
    bo = np.ascontiguousarray(np.asarray(inp['L0_attn_bo']).reshape(16, 128).T)
    w1t = head_tiles(np.ascontiguousarray(np.asarray(inp['L0_ffn_W1'])[PI].T), 64)
    w3t = head_tiles(np.ascontiguousarray(np.asarray(inp['L0_ffn_W3'])[PI].T), 64)
    w2T = np.ascontiguousarray(np.asarray(inp['L0_ffn_W2']).T)
    wq1 = head_tiles(np.ascontiguousarray(np.asarray(inp['L1_attn_Wq'])), NH)
    wk1 = head_tiles(np.ascontiguousarray(np.asarray(inp['L1_attn_Wk'])), NH)
    wv1 = head_tiles(np.ascontiguousarray(np.asarray(inp['L1_attn_Wv'])), NH)
    cq = np.asarray(inp['L1_attn_conv_q_w']).reshape(NH, 128, 4)
    ck = np.asarray(inp['L1_attn_conv_k_w']).reshape(NH, 128, 4)
    cv = np.asarray(inp['L1_attn_conv_v_w']).reshape(NH, 128, 4)
    alog = np.asarray(inp['L1_attn_A_log']).reshape(1, NH)
    wfa = chunk_tile(np.ascontiguousarray(np.asarray(inp['L1_attn_Wfa'])))
    wfb = np.asarray(inp['L1_attn_Wfb'])
    dtb = np.asarray(inp['L1_attn_dt_bias']).reshape(NH, 128)
    wga = chunk_tile(np.ascontiguousarray(np.asarray(inp['L1_attn_Wga'])))
    wgb = np.asarray(inp['L1_attn_Wgb'])
    onw = np.ascontiguousarray(np.asarray(inp['L1_attn_o_norm_w']).reshape(128, 1))
    wo1 = head_tiles(np.ascontiguousarray(np.asarray(inp['L1_attn_Wo'])), 16)
    gt = chunk_tile(np.ascontiguousarray(np.asarray(inp['L1_ffn_gate_w']).T))
    w1 = np.asarray(inp['L1_ffn_w1'])
    w3 = np.asarray(inp['L1_ffn_w3'])
    w2 = np.asarray(inp['L1_ffn_w2'])
    w1t1 = np.stack([head_tiles(np.ascontiguousarray(w1[e].T), 8) for e in range(8)])
    w3t1 = np.stack([head_tiles(np.ascontiguousarray(w3[e].T), 8) for e in range(8)])
    w2g = np.stack([np.ascontiguousarray(w2[e].T).reshape(8, 128, 2048)
                    for e in range(8)])
    ws1t = head_tiles(np.ascontiguousarray(np.asarray(inp['L1_ffn_ws1']).T), 8)
    ws3t = head_tiles(np.ascontiguousarray(np.asarray(inp['L1_ffn_ws3']).T), 8)
    ws2g = np.ascontiguousarray(np.asarray(inp['L1_ffn_ws2']).T).reshape(8, 128, 2048)
    nonce = int(time.time_ns() % 1000000000)
    non8 = np.arange(8, dtype=np.int32).reshape(1, 8) + nonce
    maps = []
    for core in range(8):
        b, hh = core // 2, core % 2
        hs = slice(hh * HPC, (hh + 1) * HPC)
        bqkv = np.ascontiguousarray(
            np.concatenate([bq[hs].T, bk[hs].T, bv[hs].T], axis=1))
        wbt = chunk_tile(np.ascontiguousarray(np.asarray(inp['L1_attn_Wb'])[:, hs]))
        maps.append({
            'x_fm': np.ascontiguousarray(x[b].T),
            'x_half': np.ascontiguousarray(x[b].T[:, hh * TOK:(hh + 1) * TOK]),
            'nonce8': non8,
            'wq_t': np.ascontiguousarray(wqT[hs]),
            'wk_t': np.ascontiguousarray(wkT[hs]),
            'wv_t': np.ascontiguousarray(wvT[hs]),
            'bqkv': bqkv,
            'rope_cos': CONSTS['rope_cos'], 'rope_sin': CONSTS['rope_sin'],
            'tri_keep': CONSTS['tri_keep'], 'ident_bf': CONSTS['ident_bf'],
            'ident_f32': CONSTS['ident_f32'],
            'wo_t': woT, 'bo_s': bo, 'w1_t': w1t, 'w3_t': w3t, 'w2_t': w2T,
            'h128': CONSTS['h128'], 'h64blk': CONSTS['h64blk'],
            'wq1_t': np.ascontiguousarray(wq1[hs]),
            'wk1_t': np.ascontiguousarray(wk1[hs]),
            'wv1_t': np.ascontiguousarray(wv1[hs]),
            'cq_s': np.ascontiguousarray(cq[hs]),
            'ck_s': np.ascontiguousarray(ck[hs]),
            'cv_s': np.ascontiguousarray(cv[hs]),
            'alog_s': np.ascontiguousarray(alog[:, hs]),
            'wfa_t': wfa,
            'wfb_s': np.ascontiguousarray(wfb[:, hh * 1024:(hh + 1) * 1024]),
            'dtb_s': np.ascontiguousarray(dtb[hs]),
            'wb_t': wbt,
            'wga_t': wga,
            'wgb_s': np.ascontiguousarray(wgb[:, hh * 1024:(hh + 1) * 1024]),
            'onw': onw,
            'cumpack': CONSTS['cumpack'], 'wm_masks': CONSTS['wm_masks'],
            'wo1_t': wo1, 'gate_t': gt,
            'w1e_t': w1t1, 'w3e_t': w3t1, 'w2e_g': w2g,
            'ws1_t': ws1t, 'ws3_t': ws3t, 'ws2_g': ws2g,
        })
    return maps

def kernel(**inputs):
    res = run_bass_kernel_spmd(build_full(), prep_full(inputs),
                               core_ids=list(range(8)))
    out = np.empty((B, T, D), dtype=np.float32)
    for core in range(8):
        b, half = core // 2, core % 2
        out[b, half * TOK:(half + 1) * TOK, :] = res.results[core]['y_half'].T
    return out


# version 03
# speedup vs baseline: 1.3956x; 1.3956x over previous
import numpy as np
import ml_dtypes
from contextlib import ExitStack
import concourse.bass as bass
import concourse.mybir as mybir
from concourse.bass_utils import run_bass_kernel_spmd
from concourse.bass import OrderedEngineSet

F32 = mybir.dt.float32
BF16 = mybir.dt.bfloat16
FP16 = mybir.dt.float16
AF = mybir.ActivationFunctionType
ALU = mybir.AluOpType
AX = mybir.AxisListType
NPBF = ml_dtypes.bfloat16

B, T, D = 4, 2048, 2048
NH, HD = 16, 128
HPC = 8
FH = 8192
C = 128
NCK = T // C
TOK = 1024
RS2 = 1.0 / np.sqrt(2048.0)

def hadamard(n):
    h = np.array([[1.0]])
    while h.shape[0] < n:
        h = np.block([[h, h], [h, -h]])
    return h

def chunk_tile(a):
    Dr, Nc = a.shape
    nck = Dr // 128
    return np.ascontiguousarray(
        a.reshape(nck, 128, Nc).transpose(1, 0, 2).reshape(128, nck * Nc))

def head_tiles(aT, nheads, hd=128):
    Dr = aT.shape[0]
    nck = Dr // 128
    out = np.empty((nheads, 128, nck * hd), dtype=aT.dtype)
    for h in range(nheads):
        out[h] = chunk_tile(np.ascontiguousarray(aT[:, h * hd:(h + 1) * hd]))
    return out

def host_consts():
    c = {}
    inv = 1.0 / 10000.0 ** (np.arange(0, HD, 2, dtype=np.float64) / HD)
    t = np.arange(T, dtype=np.float64)
    fr = t[:, None] * inv[None, :]
    c['rope_cos'] = np.concatenate([np.cos(fr)] * 2, 1).T.astype(NPBF)
    sn = np.concatenate([np.sin(fr)] * 2, 1).T
    sn[0:64] = -sn[0:64]
    c['rope_sin'] = sn.astype(NPBF)
    kk, qq = np.meshgrid(np.arange(128), np.arange(128), indexing='ij')
    c['tri_keep'] = (kk <= qq).astype(NPBF)
    c['ident_bf'] = np.eye(128, dtype=np.float32).astype(NPBF)
    c['ident_f32'] = np.eye(128, dtype=np.float32)
    c['h128'] = hadamard(128).astype(NPBF)
    h64 = hadamard(64)
    z64 = np.zeros((64, 64))
    c['h64blk'] = np.block([[h64, z64], [z64, h64]]).astype(NPBF)
    tau = np.arange(C)[:, None]
    tt = np.arange(C)[None, :]
    U = (tau <= tt).astype(np.float64)
    packs = [np.zeros((C, C))]
    for lv in [1, 2, 4, 8, 16, 32, 64]:
        ablk = ((tt // lv) % 2) == 1
        CUM = (tau < lv * (tt // lv)).astype(np.float64)
        CUMB = (tau < lv * (tt // lv + 1)).astype(np.float64)
        packs.append(np.where(ablk, CUM - U, U - CUMB))
    packs.append(-U)
    packs.append(U - 1.0)
    c['cumpack'] = np.concatenate(packs, axis=1).astype(np.float16)
    masks = [np.concatenate([np.zeros((C, C)), np.eye(C)], 1).astype(NPBF)]
    tv, sv = np.meshgrid(np.arange(C), np.arange(C), indexing='ij')
    for lv in [1, 2, 4, 8, 16, 32, 64]:
        m = ((tv // lv == sv // lv + 1) & ((tv // lv) % 2 == 1)).astype(np.float64)
        masks.append(np.concatenate([m.T, m.T], 1).astype(NPBF))
    c['wm_masks'] = np.ascontiguousarray(np.stack(masks))
    return c

def make_pi():
    j = np.arange(FH)
    return (j % 64) * 128 + 2 * (j // 128) + (j // 64) % 2

CONSTS = host_consts()
PI = make_pi()

class Ch:
    def __init__(self, nc, sem):
        self.nc, self.sem = nc, sem
        self.val = 0
        self.last_eng = None
        self.last_name = ''
        self.pending = None
    def _flush(self):
        if self.pending is not None:
            inst, inc = self.pending
            inst.then_inc(self.sem, inc)
            self.val += inc
            self.pending = None
    PE_OPS = ('matmul', 'transpose', 'ldweights')
    def op(self, e, name, *args, **kw):
        is_dma = name.startswith('dma_start')
        pe_chain = (name in self.PE_OPS and self.last_eng is e
                    and self.last_name in self.PE_OPS)
        if not pe_chain:
            self._flush()
            e.wait_ge(self.sem, self.val)
            self.last_eng = e
        inst = getattr(e, name)(*args, **kw)
        self.last_name = name
        self.pending = (inst, 16 if is_dma else 1)
        return inst
    def finish(self, e):
        self._flush()
        e.wait_ge(self.sem, self.val)

class E:
    def __init__(self, ch, eng):
        self.ch, self.eng = ch, eng
    def __getattr__(self, name):
        def f(*a, **kw):
            return self.ch.op(self.eng, name, *a, **kw)
        return f


class Stream:
    def __init__(self, ch, ge, te, sw, bufs):
        self.ch, self.ge, self.te, self.sw, self.bufs = ch, ge, te, sw, bufs
        if not hasattr(ch, 'scnt'):
            ch.scnt = {}
        self.base = ch.scnt.get(sw.num if hasattr(sw, 'num') else id(sw), 0)
        self.key = sw.num if hasattr(sw, 'num') else id(sw)
        self.n_issued = 0
        self.n_acq = 0
        self.vhist = []
        ch._flush()
        self.safe_after = ch.val
    def issue(self, src):
        it = self.n_issued
        if it >= 2:
            self.ge.wait_ge(self.ch.sem, self.vhist[it - 2])
        else:
            self.ge.wait_ge(self.ch.sem, self.safe_after)
        self.ge.dma_start(out=(self.bufs[it % 2] if not callable(self.bufs[it % 2])
                               else self.bufs[it % 2]()),
                          in_=src).then_inc(self.sw, 16)
        self.n_issued += 1
        self.ch.scnt[self.key] = self.base + 16 * self.n_issued
    def acquire(self):
        it = self.n_acq
        self.te.wait_ge(self.sw, self.base + 16 * (it + 1))
        self.n_acq += 1
        b = self.bufs[it % 2]
        return b() if callable(b) else b
    def done(self):
        self.ch._flush()
        self.vhist.append(self.ch.val)


def build_full():
    nc = bass.Bass("TRN2", target_bir_lowering=False)
    DT = nc.dram_tensor
    xin = DT("x_fm", [D, T], F32, kind="ExternalInput").ap()
    xhalf = DT("x_half", [D, TOK], F32, kind="ExternalInput").ap()
    nonced = DT("nonce8", [1, 8], mybir.dt.int32, kind="ExternalInput").ap()
    wq = DT("wq_t", [HPC, 128, 16 * HD], F32, kind="ExternalInput").ap()
    wk = DT("wk_t", [HPC, 128, 16 * HD], F32, kind="ExternalInput").ap()
    wv = DT("wv_t", [HPC, 128, 16 * HD], F32, kind="ExternalInput").ap()
    bqkv = DT("bqkv", [HD, 3 * HPC], F32, kind="ExternalInput").ap()
    cosd = DT("rope_cos", [HD, T], BF16, kind="ExternalInput").ap()
    sind = DT("rope_sin", [HD, T], BF16, kind="ExternalInput").ap()
    trid = DT("tri_keep", [128, 128], BF16, kind="ExternalInput").ap()
    idnd = DT("ident_bf", [128, 128], BF16, kind="ExternalInput").ap()
    wod = DT("wo_t", [16, 128, D], F32, kind="ExternalInput").ap()
    bod = DT("bo_s", [128, 16], F32, kind="ExternalInput").ap()
    w1d = DT("w1_t", [64, 128, D], F32, kind="ExternalInput").ap()
    w3d = DT("w3_t", [64, 128, D], F32, kind="ExternalInput").ap()
    w2d = DT("w2_t", [FH, D], F32, kind="ExternalInput").ap()
    h128d = DT("h128", [128, 128], BF16, kind="ExternalInput").ap()
    h64d = DT("h64blk", [128, 128], BF16, kind="ExternalInput").ap()
    wqd1 = DT("wq1_t", [HPC, 128, D], F32, kind="ExternalInput").ap()
    wkd1 = DT("wk1_t", [HPC, 128, D], F32, kind="ExternalInput").ap()
    wvd1 = DT("wv1_t", [HPC, 128, D], F32, kind="ExternalInput").ap()
    cqd = DT("cq_s", [HPC, 128, 4], F32, kind="ExternalInput").ap()
    ckd = DT("ck_s", [HPC, 128, 4], F32, kind="ExternalInput").ap()
    cvd = DT("cv_s", [HPC, 128, 4], F32, kind="ExternalInput").ap()
    alogd = DT("alog_s", [1, HPC], F32, kind="ExternalInput").ap()
    wfad = DT("wfa_t", [128, D], F32, kind="ExternalInput").ap()
    wfbd = DT("wfb_s", [128, HPC * 128], F32, kind="ExternalInput").ap()
    dtbd = DT("dtb_s", [HPC, 128], F32, kind="ExternalInput").ap()
    wbd = DT("wb_t", [128, 16 * HPC], F32, kind="ExternalInput").ap()
    wgad = DT("wga_t", [128, D], F32, kind="ExternalInput").ap()
    wgbd = DT("wgb_s", [128, HPC * 128], F32, kind="ExternalInput").ap()
    onwd = DT("onw", [128, 1], F32, kind="ExternalInput").ap()
    cpkd = DT("cumpack", [128, 1280], FP16, kind="ExternalInput").ap()
    mskd = DT("wm_masks", [8, 128, 256], BF16, kind="ExternalInput").ap()
    idnfd = DT("ident_f32", [128, 128], F32, kind="ExternalInput").ap()
    wod1 = DT("wo1_t", [16, 128, D], F32, kind="ExternalInput").ap()
    gted = DT("gate_t", [128, 16 * 8], F32, kind="ExternalInput").ap()
    w1ed = DT("w1e_t", [8, 8, 128, D], F32, kind="ExternalInput").ap()
    w3ed = DT("w3e_t", [8, 8, 128, D], F32, kind="ExternalInput").ap()
    w2ed = DT("w2e_g", [8, 8, 128, D], F32, kind="ExternalInput").ap()
    ws1d = DT("ws1_t", [8, 128, D], F32, kind="ExternalInput").ap()
    ws3d = DT("ws3_t", [8, 128, D], F32, kind="ExternalInput").ap()
    ws2d = DT("ws2_g", [8, 128, D], F32, kind="ExternalInput").ap()
    yo = DT("y_half", [D, TOK], F32, kind="ExternalOutput").ap()
    w2s = DT("w2s", [64, 128, D], BF16, kind="Internal").ap()
    x1dr = DT("x1dr", [D, TOK], F32, kind="Internal").ap()
    mbctx = DT("mb_ctx", [D, T], BF16, kind="Internal", addr_space="Shared").ap()
    mbx1 = DT("mb_x1", [D, T], BF16, kind="Internal", addr_space="Shared").ap()
    mbo = DT("mb_o", [D, T], BF16, kind="Internal", addr_space="Shared").ap()
    mbfl = DT("mb_fl", [8, 4], mybir.dt.int32, kind="Internal", addr_space="Shared").ap()

    st = ExitStack()
    sb = lambda n, s_, d_: st.enter_context(nc.sbuf_tensor(n, s_, d_))
    with st:
        xn = sb("xn", [128, 16 * T], BF16)
        big = sb("big", [128, 32768], BF16)
        wtile = sb("wtile", [128, D], BF16)
        wtile2 = sb("wtile2", [128, D], BF16)
        stats = sb("stats", [128, T], F32)
        stats2 = sb("stats2", [128, T], F32)
        scrB = sb("scrB", [128, 8192], BF16)
        scrF = sb("scrF", [128, 3072], F32)
        g_tm = sb("g_tm", [128, 16 * 128], FP16)
        xs4 = sb("xs4", [128, 512], F32)
        ytile = sb("ytile", [128, 512], F32)
        nonceb = sb("nonceb", [1, 8], mybir.dt.int32)
        biast = sb("biast", [128, 3 * HPC], F32)
        bob = sb("bob", [128, 16], F32)
        cwb = sb("cwb", [128, 12], F32)
        dtbrep = sb("dtbrep", [128, 128], F32)
        arep = sb("arep", [128, HPC], F32)
        onwb = sb("onwb", [128, 1], F32)
        gtb = sb("gtb", [128, 16 * 8], F32)
        cpkb = sb("cpkb", [128, 1280], FP16)
        mskb = sb("mskb", [128, 8 * 256], BF16)
        idnb = sb("idnb", [128, 128], BF16)
        idnf = sb("idnf", [128, 128], F32)
        trib = sb("trib", [128, 128], BF16)
        h128b = sb("h128b", [128, 128], BF16)
        h64b = sb("h64b", [128, 128], BF16)
        ones_f = sb("ones_f", [128, 128], F32)
        ones_b = sb("ones_b", [128, 128], BF16)
        ccell = sb("ccell", [128, 8], F32)
        wfbb = sb("wfbb", [128, HPC * 128], BF16)
        wgbb = sb("wgbb", [128, HPC * 128], BF16)
        wbb = sb("wbb", [128, 16 * HPC], BF16)
        Sf = sb("Sf", [128, 128], F32)
        svec = sb("svec", [8, 1], F32)
        psALL = st.enter_context(nc.psum_tensor("psALL", [128, 3584], F32))
        psTR = st.enter_context(nc.psum_tensor("psTR", [128, 1024], BF16))
        sem = st.enter_context(nc.semaphore("chain"))
        swA = st.enter_context(nc.semaphore("swA"))
        swB = st.enter_context(nc.semaphore("swB"))
        blk = st.enter_context(nc.Block())

        PS = lambda b_, w=512: psALL[:, b_ * 512:b_ * 512 + w]
        ps_trb = psTR[:, 0:128]

        pTt = scrB[:, 0:512]
        sqb = scrB[:, 512:1024]
        Eb = scrB[:, 1024:2304]
        KE = scrB[:, 2304:3584]
        QE = scrB[:, 3584:4864]
        WMb = scrB[:, 4864:5120]
        WMt = scrB[:, 5120:5376]
        Pb = scrB[:, 5376:6144]
        Qb = scrB[:, 6144:6912]
        zb = scrB[:, 6912:7040]
        zb2 = scrB[:, 7040:7168]
        ub = scrB[:, 7168:7296]
        rbb = scrB[:, 7296:7424]
        vtm = scrB[:, 7424:7552]
        kmt = scrB[:, 7552:7680]
        Sbb = scrB[:, 7680:7808]
        Atb = scrB[:, 7808:7936]
        Abb = scrB[:, 7936:8064]
        bndneg = scrB[:, 8064:8192]
        tmp_f = scrF[:, 0:512]
        recipr = scrF[:, 512:1024]
        tmpr = scrF[0:1, 1024:1536]
        tmpr2 = scrF[0:1, 1536:2048]
        t0f = scrF[:, 2048:2176]
        t1f = scrF[:, 2176:2304]
        bpad = scrF[:, 2304:2816]
        bT = scrF[:, 2816:2944]
        kmax = scrF[0:1, 2944:2952]
        lsb = scrF[0:8, 0:512]
        msS = scrF[0:8, 512:1024]
        spad = scrF[:, 2304:2816]
        stm = scrF[:, 2816:2880]
        m1r = scrF[:, 2880:2888]
        m1b = scrF[:, 2888:2896]
        cum = scrF[:, 2896:2904]
        s2b = scrF[:, 2904:2912]
        msum = scrF[:, 2912:2920]
        betE = scrF[:, 2952:3016]
        cosb = big[:, 0:2048]
        sinb = big[:, 2048:4096]
        qbA = big[:, 4096:6144]
        kbA = big[:, 6144:8192]
        vbA = big[:, 8192:10240]
        vtA = big[:, 10240:12288]
        ctxb = big[:, 12288:14336]
        rotb = big[:, 14336:16384]
        qcb = big[:, 0:2048]
        kcb = big[:, 2048:4096]
        vcb = big[:, 4096:6144]
        cvA = big[:, 6144:8192]
        cvB = big[:, 8192:10240]
        sgo = big[:, 10240:12288]
        ofm = big[:, 12288:14336]
        xfa = big[:, 14336:16384]
        xga = big[:, 16384:18432]
        mean_sb = stats[0:1, :]
        rstd_sb = stats2[0:1, :]
        colsq = stats[32:33, :]

        @blk.gpsimd
        def _body(_g):
            ch = Ch(nc, sem)
            sy, g, t, v, s = (E(ch, nc.sync), E(ch, nc.gpsimd), E(ch, nc.tensor),
                              E(ch, nc.vector), E(ch, nc.scalar))
            ge = nc.gpsimd
            rg0 = ge.alloc_register("rg0")
            rg1 = ge.alloc_register("rg1")
            rgn = ge.alloc_register("rgn")

            def arm_dmas(fn, n_ops):
                ch._flush()
                ge.wait_ge(sem, ch.val)
                for arm in ge.switch_core_id(n=128):
                    hh = arm.logical % 2
                    lst = fn(hh)
                    assert len(lst) == n_ops
                    for (dst, src) in lst:
                        ge.dma_start(out=dst, in_=src).then_inc(sem, 16)
                ch.val += 16 * n_ops
                ge.wait_ge(sem, ch.val)
                ch.last_eng = None
                ch.last_name = ''

            def spin(phase):
                ch._flush()
                ge.wait_ge(sem, ch.val)
                ge.load(rgn, nonceb[0:1, phase:phase + 1])
                lp, c2, en = f"sp{phase}", f"sc{phase}", f"se{phase}"
                ge.br(lp)
                eng1 = OrderedEngineSet([ge.engine])
                with nc.body(lp, valid_engines=eng1):
                    ge.load(rg0, mbfl[phase * 2:phase * 2 + 1, 0:1])
                    ge.load(rg1, mbfl[phase * 2 + 1:phase * 2 + 2, 0:1])
                    ge.br_cmp(rg0, rgn, on_true=c2, on_false=lp, comp_op="IS_EQ")
                with nc.body(c2, valid_engines=eng1):
                    ge.br_cmp(rg1, rgn, on_true=en, on_false=lp, comp_op="IS_EQ")
                nc.switch_bb(en)
                ge.sem_inc(sem, 1)
                ch.val += 1
                ge.wait_ge(sem, ch.val)
                ch.last_eng = None
                ch.last_name = ''

            def flag_write(phase):
                arm_dmas(lambda hh: [(mbfl[phase * 2 + hh:phase * 2 + hh + 1, 0:1],
                                      nonceb[0:1, phase:phase + 1])], 1)

            sy.dma_start(out=nonceb[:], in_=nonced[:, :])
            sy.dma_start(out=cosb[:], in_=cosd[:, :])
            sy.dma_start(out=sinb[:], in_=sind[:, :])
            sy.dma_start(out=trib[:], in_=trid[:, :])
            sy.dma_start(out=idnb[:], in_=idnd[:, :])
            sy.dma_start(out=idnf[:], in_=idnfd[:, :])
            sy.dma_start(out=biast[:, 0:3 * HPC], in_=bqkv[:, :])
            sy.dma_start(out=bob[:], in_=bod[:, :])
            sy.dma_start(out=h128b[:], in_=h128d[:, :])
            sy.dma_start(out=h64b[:], in_=h64d[:, :])
            sy.dma_start(out=cpkb[:], in_=cpkd[:, :])
            for i in range(8):
                sy.dma_start(out=mskb[:, i * 256:(i + 1) * 256], in_=mskd[i, :, :])
            sy.dma_start(out=onwb[:], in_=onwd[:, :])
            sy.dma_start(out=gtb[:], in_=gted[:, :])
            g.dma_start(out=wfbb[:], in_=wfbd[:, :])
            g.dma_start(out=wgbb[:], in_=wgbd[:, :])
            g.dma_start(out=wbb[:], in_=wbd[:, :])
            g.memset(ones_f[:], 1.0)
            g.memset(ones_b[:], 1.0)
            for i, cv in enumerate([1e-5, 1e-6, 1.0, float(np.log(128.0 ** -0.5))]):
                g.memset(ccell[:, i:i + 1], cv)
                nc.const_aps.register(F32, cv, ccell[:, i:i + 1])

            def ln_full(src_getter, nchunkT):
                for q in range(4):
                    qs = slice(q * 512, (q + 1) * 512)
                    for ci in range(16):
                        t.matmul(PS(6)[0:1, 0:512], ones_b[:, 0:1],
                                 src_getter(ci)[:, qs], start=(ci == 0), stop=(ci == 15))
                    v.tensor_scalar(out=mean_sb[0:1, qs], in0=PS(6)[0:1, 0:512],
                                    scalar1=1.0 / 2048.0, scalar2=None, op0=ALU.mult)
                    for ci in range(16):
                        s.activation(out=sqb[:, 0:512], in_=src_getter(ci)[:, qs],
                                     func=AF.Square)
                        t.matmul(PS(5)[0:1, 0:512], ones_b[:, 0:1], sqb[:, 0:512],
                                 start=(ci == 0), stop=(ci == 15))
                    v.tensor_scalar(out=rstd_sb[0:1, qs], in0=PS(5)[0:1, 0:512],
                                    scalar1=1.0 / 2048.0, scalar2=None, op0=ALU.mult)
                    s.activation(out=tmpr[0:1, 0:512], in_=mean_sb[0:1, qs],
                                 func=AF.Square)
                    v.tensor_tensor(out=rstd_sb[0:1, qs], in0=rstd_sb[0:1, qs],
                                    in1=tmpr[0:1, 0:512], op=ALU.subtract)
                    s.activation(out=tmpr[0:1, 0:512], in_=rstd_sb[0:1, qs],
                                 func=AF.Ln, bias=1e-5)
                    s.activation(out=rstd_sb[0:1, qs], in_=tmpr[0:1, 0:512],
                                 func=AF.Exp, scale=-0.5)
                for ci in range(16):
                    for q in range(4):
                        qs = slice(q * 512, (q + 1) * 512)
                        t.matmul(PS(6)[:, 0:512], ones_f[0:1, :], mean_sb[0:1, qs],
                                 start=True, stop=True)
                        v.tensor_tensor(out=tmp_f[:, 0:512], in0=src_getter(ci)[:, qs],
                                        in1=PS(6)[:, 0:512], op=ALU.subtract)
                        t.matmul(PS(6)[:, 0:512], ones_f[0:1, :], rstd_sb[0:1, qs],
                                 start=True, stop=True)
                        v.tensor_tensor(out=xn[:, ci * T + qs.start:ci * T + qs.stop],
                                        in0=tmp_f[:, 0:512], in1=PS(6)[:, 0:512],
                                        op=ALU.mult)

            for ci in range(16):
                g.dma_start(out=xn[:, ci * T:(ci + 1) * T],
                            in_=xin[ci * 128:(ci + 1) * 128, :])
            ln_full(lambda ci: xn[:, ci * T:(ci + 1) * T], 16)
            stA = Stream(ch, nc.gpsimd, nc.tensor, swA,
                         [wtile[:, 0:2048], big[:, 16384:18432]])
            srcsA = [w[h, :, :] for h in range(HPC) for w in (wq, wk, wv)]
            stA.issue(srcsA[0])
            stA.issue(srcsA[1])
            nsrcA = [2]
            for h in range(HPC):
                for pi_, (dst, bcol) in enumerate((
                        (qbA, biast[:, h:h + 1]),
                        (kbA, biast[:, HPC + h:HPC + h + 1]),
                        (vbA, biast[:, 2 * HPC + h:2 * HPC + h + 1]))):
                    wbuf = stA.acquire()
                    for q in range(4):
                        qs = slice(q * 512, (q + 1) * 512)
                        for ci in range(16):
                            xnc = xn[:, ci * T + qs.start:ci * T + qs.stop]
                            t.matmul(PS(pi_)[:, 0:512],
                                     wbuf[:, ci * HD:(ci + 1) * HD], xnc,
                                     start=(ci == 0), stop=(ci == 15))
                        if pi_ < 2:
                            s.add(dst[:, qs], PS(pi_)[:, 0:512], bcol)
                        else:
                            s.add(tmp_f[:, 0:512], PS(pi_)[:, 0:512], bcol)
                            v.tensor_copy(vbA[:, qs], tmp_f[:, 0:512])
                    stA.done()
                    if nsrcA[0] < len(srcsA):
                        stA.issue(srcsA[nsrcA[0]])
                        nsrcA[0] += 1
                for src in (qbA, kbA):
                    s.copy(rotb[0:64, :], src[64:128, :])
                    s.copy(rotb[64:128, :], src[0:64, :])
                    v.tensor_tensor(out=rotb[:, :], in0=rotb[:, :], in1=sinb[:, :],
                                    op=ALU.mult)
                    for q in range(4):
                        qs = slice(q * 512, (q + 1) * 512)
                        v.tensor_tensor(out=pTt[:, 0:512], in0=src[:, qs],
                                        in1=cosb[:, qs], op=ALU.mult)
                        v.tensor_tensor(out=src[:, qs], in0=pTt[:, 0:512],
                                        in1=rotb[:, qs], op=ALU.add)
                for kt in range(16):
                    ks = slice(kt * 128, (kt + 1) * 128)
                    t.transpose(ps_trb, vbA[:, ks], idnb[:])
                    v.tensor_copy(vtA[:, ks], ps_trb)
                for q in range(4):
                    qs = slice(q * 512, (q + 1) * 512)
                    s.activation(out=sqb[:, 0:512], in_=kbA[:, qs], func=AF.Square)
                    t.matmul(PS(0)[0:1, 0:512], ones_b[:, 0:1], sqb[:, 0:512],
                             start=True, stop=True)
                    v.tensor_copy(colsq[0:1, qs], PS(0)[0:1, 0:512])
                v.reduce_max(kmax[0:1, 0:1], colsq[0:1, :], axis=AX.X)
                for q in range(4):
                    qs = slice(q * 512, (q + 1) * 512)
                    s.activation(out=sqb[:, 0:512], in_=qbA[:, qs], func=AF.Square)
                    t.matmul(PS(0)[0:1, 0:512], ones_b[:, 0:1], sqb[:, 0:512],
                             start=True, stop=True)
                    v.tensor_scalar(out=tmpr[0:1, 0:512], in0=PS(0)[0:1, 0:512],
                                    scalar1=kmax[0:1, 0:1], scalar2=None, op0=ALU.mult)
                    s.activation(out=tmpr[0:1, 0:512], in_=tmpr[0:1, 0:512],
                                 func=AF.Sqrt)
                    v.tensor_scalar(out=colsq[0:1, qs], in0=tmpr[0:1, 0:512],
                                    scalar1=-1.0, scalar2=None, op0=ALU.mult)
                for q in range(4):
                    qs = slice(q * 512, (q + 1) * 512)
                    nkt = 4 * (q + 1)
                    v.tensor_copy(sqb[0:1, 0:512], colsq[0:1, qs])
                    for kt in range(nkt):
                        ks = slice(kt * 128, (kt + 1) * 128)
                        t.matmul(PS(3)[:, 0:512], kbA[:, ks], qbA[:, qs],
                                 start=True, stop=False)
                        t.matmul(PS(3)[:, 0:512], ones_b[0:1, :], sqb[0:1, 0:512],
                                 start=False, stop=True)
                        s.activation(out=pTt[:, 0:512], in_=PS(3)[:, 0:512],
                                     func=AF.Exp, scale=1.0 / np.sqrt(128.0))
                        if kt * 128 >= qs.start:
                            sub = kt * 128 - qs.start
                            if sub > 0:
                                v.memset(pTt[:, 0:sub], 0.0)
                            v.tensor_tensor(out=pTt[:, sub:sub + 128],
                                            in0=pTt[:, sub:sub + 128], in1=trib[:],
                                            op=ALU.mult)
                        t.matmul(PS(0)[0:1, 0:512], ones_b[:, 0:1], pTt[:, 0:512],
                                 start=(kt == 0), stop=(kt == nkt - 1))
                        t.matmul(PS(4)[:, 0:512], vtA[:, ks], pTt[:, 0:512],
                                 start=(kt == 0), stop=(kt == nkt - 1))
                    v.reciprocal(tmpr[0:1, 0:512], PS(0)[0:1, 0:512])
                    t.matmul(PS(6)[:, 0:512], ones_f[0:1, :], tmpr[0:1, 0:512],
                             start=True, stop=True)
                    v.tensor_copy(recipr[:, 0:512], PS(6)[:, 0:512])
                    v.tensor_tensor(out=ctxb[:, qs], in0=PS(4)[:, 0:512],
                                    in1=recipr[:, 0:512], op=ALU.mult)
                arm_dmas(lambda hh: [(mbctx[(hh * 8 + h) * 128:(hh * 8 + h + 1) * 128, :],
                                      ctxb[:, 0:T])], 1)
            flag_write(0)

            def b_arm(hh):
                lst = []
                for ci in range(16):
                    srcx = xn[:, ci * T + hh * TOK:ci * T + hh * TOK + TOK]
                    dstx = xn[:, ci * T:ci * T + TOK] if hh == 1 else scrB[:, 1024:2048]
                    lst.append((dstx, srcx))
                return lst
            arm_dmas(b_arm, 16)
            stP = Stream(ch, nc.gpsimd, nc.tensor, swA,
                         [wtile[:, 0:2048], scrB[:, 3072:5120]])
            stP.issue(w2d[0:128, :])
            stP.issue(w2d[128:256, :])
            for n in range(64):
                wb_ = stP.acquire()
                for q in range(4):
                    t.matmul(PS(q)[:, 0:512], h128b[:],
                             wb_[:, q * 512:(q + 1) * 512], start=True, stop=True)
                stP.done()
                if n + 2 < 64:
                    stP.issue(w2d[(n + 2) * 128:(n + 3) * 128, :])
                for q in range(4):
                    v.tensor_copy(wtile2[:, q * 512:(q + 1) * 512], PS(q)[:, 0:512])
                g.dma_start(out=w2s[:, n:n + 65:64, :], in_=wtile2[:, :])
            spin(0)
            arm_dmas(lambda hh: [(xn[:, ci * T + TOK:(ci + 1) * T],
                                  mbctx[ci * 128:(ci + 1) * 128,
                                        hh * TOK:(hh + 1) * TOK])
                                 for ci in range(16)], 16)
            for th in range(2):
                ts_ = slice(th * 512, (th + 1) * 512)
                st1 = Stream(ch, nc.gpsimd, nc.tensor, swA,
                             [wtile[:, 0:2048], scrB[:, 3072:5120]])
                st3 = Stream(ch, nc.gpsimd, nc.tensor, swB,
                             [wtile2[:, 0:2048], scrB[:, 1024:3072]])
                st1.issue(w1d[0, :, :]); st3.issue(w3d[0, :, :])
                st1.issue(w1d[1, :, :]); st3.issue(w3d[1, :, :])
                for hb in range(64):
                    b1 = st1.acquire()
                    b3 = st3.acquire()
                    for ci in range(16):
                        xnc = xn[:, ci * T + ts_.start:ci * T + ts_.stop]
                        t.matmul(PS(0)[:, 0:512], b1[:, ci * 128:(ci + 1) * 128],
                                 xnc, start=(ci == 0), stop=(ci == 15))
                        t.matmul(PS(1)[:, 0:512], b3[:, ci * 128:(ci + 1) * 128],
                                 xnc, start=(ci == 0), stop=(ci == 15))
                    st1.done(); st3.done()
                    if hb + 2 < 64:
                        st1.issue(w1d[hb + 2, :, :])
                        st3.issue(w3d[hb + 2, :, :])
                    s.activation(out=sqb[:, 0:512], in_=PS(0)[:, 0:512], func=AF.Silu)
                    v.tensor_tensor(out=pTt[:, 0:512], in0=sqb[:, 0:512],
                                    in1=PS(1)[:, 0:512], op=ALU.mult)
                    t.matmul(PS(2)[:, 0:512], h64b[:], pTt[:, 0:512],
                             start=True, stop=True)
                    s.mul(big[:, hb * 512:(hb + 1) * 512],
                          PS(2)[:, 0:512], 1.0 / np.sqrt(8192.0))
                stWo = Stream(ch, nc.gpsimd, nc.tensor, swA,
                              [wtile[:, 0:2048], scrB[:, 3072:5120]])
                stw2 = Stream(ch, nc.gpsimd, nc.tensor, swB,
                              [wtile2[:, 0:512], wtile2[:, 512:1024]])
                stWo.issue(wod[0, :, :]); stWo.issue(wod[1, :, :])
                stw2.issue(w2s[0, :, 0:512]); stw2.issue(w2s[1, :, 0:512])
                nWo = [2]; nw2 = [2]
                w2srcs = [(hc, grp) for grp in range(4) for hc in range(64)]
                for grp in range(4):
                    for j in range(4):
                        dmb = grp * 4 + j
                        wb_ = stWo.acquire()
                        for ci in range(16):
                            t.matmul(PS(j)[:, 0:512],
                                     wb_[:, ci * 128:(ci + 1) * 128],
                                     xn[:, ci * T + TOK + ts_.start:ci * T + TOK + ts_.stop],
                                     start=(ci == 0), stop=False)
                        stWo.done()
                        if nWo[0] < 16:
                            stWo.issue(wod[nWo[0], :, :])
                            nWo[0] += 1
                        elif th == 0 and nWo[0] < 17:
                            pass
                    for hc in range(64):
                        wb2 = stw2.acquire()
                        for j in range(4):
                            t.matmul(PS(j)[:, 0:512],
                                     wb2[:, j * 128:(j + 1) * 128],
                                     big[:, hc * 512:(hc + 1) * 512],
                                     start=False, stop=(hc == 63))
                        stw2.done()
                        if nw2[0] < 256:
                            h2, g2 = w2srcs[nw2[0]]
                            stw2.issue(w2s[h2, :, g2 * 512:(g2 + 1) * 512])
                            nw2[0] += 1
                    for j in range(4):
                        dmb = grp * 4 + j
                        g.dma_start(out=xs4[:, 0:512],
                                    in_=xhalf[dmb * 128:(dmb + 1) * 128, ts_])
                        v.scalar_tensor_tensor(out=ytile[:, 0:512], in0=PS(j)[:, 0:512],
                                               scalar=bob[:, dmb:dmb + 1],
                                               in1=xs4[:, 0:512], op0=ALU.add,
                                               op1=ALU.add)
                        g.dma_start(out=x1dr[dmb * 128:(dmb + 1) * 128, ts_],
                                    in_=ytile[:, 0:512])
                        arm_dmas(lambda hh: [(mbx1[dmb * 128:(dmb + 1) * 128,
                                                   hh * TOK + ts_.start:hh * TOK + ts_.stop],
                                              ytile[:, 0:512])], 1)
            flag_write(1)

            spin(1)
            for ci in range(16):
                g.dma_start(out=xn[:, ci * T:(ci + 1) * T],
                            in_=mbx1[ci * 128:(ci + 1) * 128, :])
            ln_full(lambda ci: xn[:, ci * T:(ci + 1) * T], 16)
            stC = Stream(ch, nc.gpsimd, nc.tensor, swA,
                         [wtile[:, 0:2048], big[:, 18432:20480]])
            srcsC = [wfad[:, :], wgad[:, :]] + [w_[h, :, :] for h in range(HPC)
                                               for w_ in (wqd1, wkd1, wvd1)]
            stC.issue(srcsC[0]); stC.issue(srcsC[1])
            nC = [2]
            for dst in (xfa, xga):
                wsb = stC.acquire()
                for q in range(4):
                    qs = slice(q * 512, (q + 1) * 512)
                    for ci in range(16):
                        t.matmul(PS(0)[:, 0:512], wsb[:, ci * 128:(ci + 1) * 128],
                                 xn[:, ci * T + qs.start:ci * T + qs.stop],
                                 start=(ci == 0), stop=(ci == 15))
                    v.tensor_copy(dst[:, qs], PS(0)[:, 0:512])
                stC.done()
                if nC[0] < len(srcsC):
                    stC.issue(srcsC[nC[0]]); nC[0] += 1
            g.dma_start(out=arep[0:1, 0:HPC], in_=alogd[:, :])
            s.activation(out=arep[0:1, 0:HPC], in_=arep[0:1, 0:HPC], func=AF.Exp)
            t.matmul(PS(0)[:, 0:HPC], ones_f[0:1, :], arep[0:1, 0:HPC],
                     start=True, stop=True)
            v.tensor_copy(arep[:, 0:HPC], PS(0)[:, 0:HPC])
            for q in range(4):
                qs = slice(q * 512, (q + 1) * 512)
                for ci in range(16):
                    t.matmul(PS(0)[0:HPC, 0:512], wbb[:, ci * HPC:(ci + 1) * HPC],
                             xn[:, ci * T + qs.start:ci * T + qs.stop],
                             start=(ci == 0), stop=(ci == 15))
                s.activation(out=bpad[0:HPC, 0:512], in_=PS(0)[0:HPC, 0:512],
                             func=AF.Sigmoid)
                for tl in range(4):
                    tt2 = q * 4 + tl
                    v.tensor_copy(sqb[:, 0:128], bpad[:, tl * 128:(tl + 1) * 128])
                    t.transpose(ps_trb, sqb[:, 0:128], idnb[:])
                    v.tensor_copy(bT[:, tt2 * HPC:tt2 * HPC + HPC], ps_trb[:, 0:HPC])
            for h in range(HPC):
                g.dma_start(out=cwb[:, 0:4], in_=cqd[h, :, :])
                g.dma_start(out=cwb[:, 4:8], in_=ckd[h, :, :])
                g.dma_start(out=cwb[:, 8:12], in_=cvd[h, :, :])
                for dst in (qcb, kcb, vcb):
                    wsb = stC.acquire()
                    for q in range(4):
                        qs = slice(q * 512, (q + 1) * 512)
                        for ci in range(16):
                            t.matmul(PS(0)[:, 0:512], wsb[:, ci * 128:(ci + 1) * 128],
                                     xn[:, ci * T + qs.start:ci * T + qs.stop],
                                     start=(ci == 0), stop=(ci == 15))
                        v.tensor_copy(dst[:, qs], PS(0)[:, 0:512])
                    stC.done()
                    if nC[0] < len(srcsC):
                        stC.issue(srcsC[nC[0]]); nC[0] += 1
                for ti, src in enumerate((qcb, kcb, vcb)):
                    cw = cwb[:, ti * 4:(ti + 1) * 4]
                    v.tensor_scalar(out=cvA[:, :], in0=src[:, :],
                                    scalar1=cw[:, 3:4], scalar2=None, op0=ALU.mult)
                    bufs = [cvA, cvB]
                    for si, jj in enumerate((2, 1, 0)):
                        sh = 3 - jj
                        a_, b_ = bufs[si % 2], bufs[(si + 1) % 2]
                        v.scalar_tensor_tensor(out=b_[:, sh:T], in0=src[:, 0:T - sh],
                                               scalar=cw[:, jj:jj + 1], in1=a_[:, sh:T],
                                               op0=ALU.mult, op1=ALU.add)
                        v.tensor_copy(b_[:, 0:sh], a_[:, 0:sh])
                    s.activation(out=src[:, :], in_=cvB[:, :], func=AF.Silu)
                for ti, src in enumerate((qcb, kcb)):
                    for q in range(4):
                        qs = slice(q * 512, (q + 1) * 512)
                        s.activation(out=sqb[:, 0:512], in_=src[:, qs], func=AF.Square)
                        t.matmul(PS(0)[0:1, 0:512], ones_b[:, 0:1], sqb[:, 0:512],
                                 start=True, stop=True)
                        s.activation(out=tmpr[0:1, 0:512], in_=PS(0)[0:1, 0:512],
                                     func=AF.Ln, bias=1e-6)
                        if ti == 0:
                            s.activation(out=tmpr[0:1, 0:512], in_=tmpr[0:1, 0:512],
                                         func=AF.Exp, scale=-0.5,
                                         bias=float(np.log(128.0 ** -0.5)))
                        else:
                            s.activation(out=tmpr[0:1, 0:512], in_=tmpr[0:1, 0:512],
                                         func=AF.Exp, scale=-0.5)
                        t.matmul(PS(0)[:, 0:512], ones_f[0:1, :], tmpr[0:1, 0:512],
                                 start=True, stop=True)
                        v.tensor_tensor(out=src[:, qs], in0=src[:, qs],
                                        in1=PS(0)[:, 0:512], op=ALU.mult)
                g.dma_start(out=dtbrep[0:1, 0:128], in_=dtbd[h:h + 1, :])
                t.matmul(PS(0)[:, 0:128], ones_f[0:1, :], dtbrep[0:1, 0:128],
                         start=True, stop=True)
                v.tensor_copy(dtbrep[:, 0:128], PS(0)[:, 0:128])
                for tt2 in range(16):
                    t.matmul(PS(1)[:, 0:128], xfa[:, tt2 * 128:(tt2 + 1) * 128],
                             wfbb[:, h * 128:(h + 1) * 128], start=True, stop=True)
                    v.tensor_tensor(out=t0f[:, :], in0=PS(1)[:, 0:128],
                                    in1=dtbrep[:, 0:128], op=ALU.add)
                    s.activation(out=t1f[:, :], in_=t0f[:, :], func=AF.Abs)
                    s.activation(out=t1f[:, :], in_=t1f[:, :], func=AF.Exp, scale=-1.0)
                    s.activation(out=t1f[:, :], in_=t1f[:, :], func=AF.Ln, bias=1.0)
                    v.tensor_scalar(out=t0f[:, :], in0=t0f[:, :], scalar1=0.0,
                                    scalar2=None, op0=ALU.max)
                    v.tensor_tensor(out=t0f[:, :], in0=t0f[:, :], in1=t1f[:, :],
                                    op=ALU.add)
                    v.tensor_scalar(out=g_tm[:, tt2 * 128:(tt2 + 1) * 128],
                                    in0=t0f[:, :], scalar1=arep[:, h:h + 1],
                                    scalar2=None, op0=ALU.mult)
                for q in range(4):
                    qs = slice(q * 512, (q + 1) * 512)
                    t.matmul(PS(0)[:, 0:512], wgbb[:, h * 128:(h + 1) * 128],
                             xga[:, qs], start=True, stop=True)
                    s.activation(out=sgo[:, qs], in_=PS(0)[:, 0:512], func=AF.Sigmoid)
                g.memset(Sf[:], 0.0)
                g.memset(Sbb[:, :], 0.0)
                for ck in range(16):
                    cs = slice(ck * 128, (ck + 1) * 128)
                    gsl = g_tm[:, ck * 128:(ck + 1) * 128]
                    for mi in range(3):
                        fs = slice(mi * 512, min((mi + 1) * 512, 1280))
                        t.matmul(psALL[:, fs], gsl, cpkb[:, fs], start=True, stop=True)
                    s.activation(out=Eb[:, :], in_=psALL[:, 0:1280], func=AF.Exp)
                    for si in range(1, 10):
                        sl = slice(si * 128, (si + 1) * 128)
                        v.tensor_tensor(out=KE[:, sl.start:sl.stop], in0=Eb[:, sl],
                                        in1=kcb[:, cs], op=ALU.mult)
                    for si in list(range(1, 8)) + [8]:
                        sl = slice(si * 128, (si + 1) * 128)
                        v.tensor_tensor(out=QE[:, sl.start:sl.stop], in0=Eb[:, sl],
                                        in1=qcb[:, cs], op=ALU.mult)
                    for si in range(8):
                        bsl = kcb[:, cs] if si == 0 else KE[:, si * 128:(si + 1) * 128]
                        aq = qcb[:, cs] if si == 0 else QE[:, si * 128:(si + 1) * 128]
                        t.matmul(PS(6)[:, 0:128], bsl,
                                 kcb[:, cs] if si == 0 else KE[:, si * 128:(si + 1) * 128],
                                 start=True, stop=True)
                        t.matmul(PS(6)[:, 128:256], bsl, aq, start=True, stop=True)
                        if si == 0:
                            v.tensor_tensor(out=WMb[:, :], in0=PS(6)[:, 0:256],
                                            in1=mskb[:, 0:256], op=ALU.mult)
                        else:
                            v.tensor_tensor(out=WMt[:, :], in0=PS(6)[:, 0:256],
                                            in1=mskb[:, si * 256:(si + 1) * 256],
                                            op=ALU.mult)
                            v.tensor_tensor(out=WMb[:, :], in0=WMb[:, :],
                                            in1=WMt[:, :], op=ALU.add)
                    bcol = bT[:, ck * HPC + h:ck * HPC + h + 1]
                    v.tensor_scalar(out=Atb[:, :], in0=WMb[:, 0:128],
                                    scalar1=bcol, scalar2=None, op0=ALU.mult)
                    t.transpose(ps_trb, Atb[:, :], idnb[:])
                    v.tensor_copy(Abb[:, :], ps_trb)
                    t.transpose(ps_trb, vcb[:, cs], idnb[:])
                    v.tensor_copy(vtm[:, :], ps_trb)
                    t.matmul(PS(5)[:, 0:128], KE[:, 8 * 128:9 * 128], Sbb[:, :],
                             start=True, stop=True)
                    v.tensor_tensor(out=rbb[:, :], in0=vtm[:, :], in1=PS(5)[:, 0:128],
                                    op=ALU.subtract)
                    prevP, prevQ = Abb[:, :], Atb[:, :]
                    for jj in range(1, 7):
                        if jj < 6:
                            t.matmul(PS(5)[:, 128:256], prevQ, prevP,
                                     start=True, stop=True)
                            v.tensor_copy(Pb[:, (jj - 1) * 128:jj * 128],
                                          PS(5)[:, 128:256])
                        t.matmul(PS(5)[:, 256:384], prevP, prevQ, start=True, stop=True)
                        v.tensor_copy(Qb[:, (jj - 1) * 128:jj * 128], PS(5)[:, 256:384])
                        prevP = Pb[:, (jj - 1) * 128:jj * 128]
                        prevQ = Qb[:, (jj - 1) * 128:jj * 128]
                    zcur, znxt = zb, zb2
                    v.tensor_copy(zcur[:, :], rbb[:, :])
                    for jj in range(6, 0, -1):
                        t.matmul(PS(5)[:, 128:256], Qb[:, (jj - 1) * 128:jj * 128],
                                 zcur[:, :], start=True, stop=True)
                        v.tensor_tensor(out=znxt[:, :], in0=zcur[:, :],
                                        in1=PS(5)[:, 128:256], op=ALU.add)
                        zcur, znxt = znxt, zcur
                    t.matmul(PS(5)[:, 128:256], Atb[:, :], zcur[:, :],
                             start=True, stop=True)
                    v.tensor_tensor(out=znxt[:, :], in0=zcur[:, :],
                                    in1=PS(5)[:, 128:256], op=ALU.subtract)
                    v.tensor_scalar(out=ub[:, :], in0=znxt[:, :], scalar1=bcol,
                                    scalar2=None, op0=ALU.mult)
                    t.matmul(PS(5)[:, 256:384], ub[:, :], WMb[:, 128:256],
                             start=True, stop=False)
                    t.matmul(PS(5)[:, 256:384], Sbb[:, :], QE[:, 8 * 128:9 * 128],
                             start=False, stop=True)
                    v.tensor_copy(ofm[:, cs], PS(5)[:, 256:384])
                    t.transpose(ps_trb, KE[:, 9 * 128:10 * 128], idnb[:])
                    v.tensor_copy(kmt[:, :], ps_trb)
                    t.matmul(PS(5)[:, 0:128], kmt[:, :], ub[:, :], start=True, stop=True)
                    v.scalar_tensor_tensor(out=Sf[:, :], in0=Sf[:, :],
                                           scalar=Eb[:, 8 * 128 + 127:8 * 128 + 128],
                                           in1=PS(5)[:, 0:128], op0=ALU.mult,
                                           op1=ALU.add)
                    v.tensor_copy(Sbb[:, :], Sf[:, :])
                for q in range(4):
                    qs = slice(q * 512, (q + 1) * 512)
                    s.activation(out=sqb[:, 0:512], in_=ofm[:, qs], func=AF.Square)
                    t.matmul(PS(0)[0:1, 0:512], ones_b[:, 0:1], sqb[:, 0:512],
                             start=True, stop=True)
                    v.tensor_scalar(out=tmpr[0:1, 0:512], in0=PS(0)[0:1, 0:512],
                                    scalar1=1.0 / 128.0, scalar2=None, op0=ALU.mult)
                    s.activation(out=tmpr[0:1, 0:512], in_=tmpr[0:1, 0:512],
                                 func=AF.Ln, bias=1e-6)
                    s.activation(out=tmpr[0:1, 0:512], in_=tmpr[0:1, 0:512],
                                 func=AF.Exp, scale=-0.5)
                    t.matmul(PS(0)[:, 0:512], ones_f[0:1, :], tmpr[0:1, 0:512],
                             start=True, stop=True)
                    v.tensor_copy(recipr[:, 0:512], PS(0)[:, 0:512])
                    v.scalar_tensor_tensor(out=ofm[:, qs], in0=ofm[:, qs],
                                           scalar=onwb[:, 0:1], in1=sgo[:, qs],
                                           op0=ALU.mult, op1=ALU.mult)
                    v.tensor_tensor(out=ofm[:, qs], in0=ofm[:, qs],
                                    in1=recipr[:, 0:512], op=ALU.mult)
                arm_dmas(lambda hh: [(mbo[(hh * 8 + h) * 128:(hh * 8 + h + 1) * 128, :],
                                      ofm[:, 0:T])], 1)
            flag_write(2)

            def d_arm(hh):
                lst = []
                for ci in range(16):
                    srcx = xn[:, ci * T + hh * TOK:ci * T + hh * TOK + TOK]
                    dstx = xn[:, ci * T:ci * T + TOK] if hh == 1 else scrB[:, 1024:2048]
                    lst.append((dstx, srcx))
                lst.append((mean_sb[0:1, 0:TOK] if hh == 1 else stats[64:65, 0:TOK],
                            mean_sb[0:1, hh * TOK:hh * TOK + TOK]))
                lst.append((rstd_sb[0:1, 0:TOK] if hh == 1 else stats[64:65, 0:TOK],
                            rstd_sb[0:1, hh * TOK:hh * TOK + TOK]))
                return lst
            arm_dmas(d_arm, 18)
            for ci in range(16):
                t.matmul(PS(0)[0:1, 0:8], ones_f[:, 0:1], gtb[:, ci * 8:(ci + 1) * 8],
                         start=(ci == 0), stop=(ci == 15))
            v.tensor_copy(spad[0:1, 0:8], PS(0)[0:1, 0:8])
            t.transpose(PS(6)[:, 0:128], spad[:, 0:128], idnf[:])
            v.tensor_copy(svec[0:8, 0:1], PS(6)[0:8, 0:1])
            for q in range(2):
                qs = slice(q * 512, (q + 1) * 512)
                for ci in range(16):
                    g.dma_start(out=xs4[:, 0:512],
                                in_=x1dr[ci * 128:(ci + 1) * 128, qs])
                    t.matmul(PS(0)[0:8, 0:512], gtb[:, ci * 8:(ci + 1) * 8],
                             xs4[:, 0:512], start=(ci == 0), stop=(ci == 15))
                v.tensor_copy(lsb[:, 0:512], PS(0)[0:8, 0:512])
                t.matmul(PS(1)[0:8, 0:512], ones_f[0:1, 0:8], mean_sb[0:1, qs],
                         start=True, stop=True)
                v.tensor_scalar(out=msS[:, 0:512], in0=PS(1)[0:8, 0:512],
                                scalar1=svec[0:8, 0:1], scalar2=None, op0=ALU.mult)
                v.tensor_tensor(out=lsb[:, 0:512], in0=lsb[:, 0:512],
                                in1=msS[:, 0:512], op=ALU.subtract)
                t.matmul(PS(1)[0:8, 0:512], ones_f[0:1, 0:8], rstd_sb[0:1, qs],
                         start=True, stop=True)
                v.tensor_tensor(out=lsb[:, 0:512], in0=lsb[:, 0:512],
                                in1=PS(1)[0:8, 0:512], op=ALU.mult)
                s.activation(out=spad[0:8, 0:512], in_=lsb[:, 0:512], func=AF.Sigmoid)
                for tl in range(4):
                    tt2 = q * 4 + tl
                    t.transpose(PS(6)[:, 0:128], spad[:, tl * 128:(tl + 1) * 128],
                                idnf[:])
                    v.tensor_copy(stm[:, tt2 * 8:(tt2 + 1) * 8], PS(6)[:, 0:8])
            for tt2 in range(8):
                ssl = stm[:, tt2 * 8:(tt2 + 1) * 8]
                v.reduce_max(m1r[:, 0:1], ssl, axis=AX.X)
                v.tensor_scalar(out=m1r[:, 0:8], in0=ssl, scalar1=m1r[:, 0:1],
                                scalar2=None, op0=ALU.is_ge)
                v.tensor_tensor_scan(out=cum[:, 0:8], data0=ones_f[:, 0:8],
                                     data1=m1r[:, 0:8], initial=0.0,
                                     op0=ALU.mult, op1=ALU.add)
                v.tensor_scalar(out=cum[:, 0:8], in0=cum[:, 0:8], scalar1=1.0,
                                scalar2=None, op0=ALU.is_equal)
                v.tensor_tensor(out=m1b[:, 0:8], in0=cum[:, 0:8], in1=m1r[:, 0:8],
                                op=ALU.mult)
                v.scalar_tensor_tensor(out=s2b[:, 0:8], in0=m1b[:, 0:8], scalar=-2.0,
                                       in1=ssl, op0=ALU.mult, op1=ALU.add)
                v.reduce_max(m1r[:, 0:1], s2b[:, 0:8], axis=AX.X)
                v.tensor_scalar(out=m1r[:, 0:8], in0=s2b[:, 0:8], scalar1=m1r[:, 0:1],
                                scalar2=None, op0=ALU.is_ge)
                v.tensor_tensor_scan(out=cum[:, 0:8], data0=ones_f[:, 0:8],
                                     data1=m1r[:, 0:8], initial=0.0,
                                     op0=ALU.mult, op1=ALU.add)
                v.tensor_scalar(out=cum[:, 0:8], in0=cum[:, 0:8], scalar1=1.0,
                                scalar2=None, op0=ALU.is_equal)
                v.tensor_tensor(out=cum[:, 0:8], in0=cum[:, 0:8], in1=m1r[:, 0:8],
                                op=ALU.mult)
                v.tensor_tensor(out=msum[:, 0:8], in0=m1b[:, 0:8], in1=cum[:, 0:8],
                                op=ALU.add)
                v.tensor_tensor(out=betE[:, tt2 * 8:(tt2 + 1) * 8], in0=ssl,
                                in1=msum[:, 0:8], op=ALU.mult)
            spin(2)
            arm_dmas(lambda hh: [(big[:, ci * TOK:(ci + 1) * TOK],
                                  mbo[ci * 128:(ci + 1) * 128,
                                      hh * TOK:(hh + 1) * TOK])
                                 for ci in range(16)], 16)
            def hslot(idx):
                if idx < 32:
                    return big[:, 16384 + idx * 512:16384 + (idx + 1) * 512]
                if idx < 64:
                    k = idx - 32
                    o = (k // 2) * T + TOK + (k % 2) * 512
                    return xn[:, o:o + 512]
                return scrB[:, 1024 + (idx - 64) * 512:1024 + (idx - 63) * 512]
            srcsD1 = [(w1ed[e, hb, :, :] if e < 8 else ws1d[hb, :, :])
                      for e in range(9) for hb in range(8)]
            srcsD3 = [(w3ed[e, hb, :, :] if e < 8 else ws3d[hb, :, :])
                      for e in range(9) for hb in range(8)]
            for th in range(2):
                ts_ = slice(th * 512, (th + 1) * 512)
                std1 = Stream(ch, nc.gpsimd, nc.tensor, swA,
                              [wtile[:, 0:2048], mskb[:, 0:2048]])
                std3 = Stream(ch, nc.gpsimd, nc.tensor, swB,
                              [wtile2[:, 0:2048], scrB[:, 5120:7168]])
                std1.issue(srcsD1[0]); std3.issue(srcsD3[0])
                std1.issue(srcsD1[1]); std3.issue(srcsD3[1])
                nD = [2]
                for e in range(9):
                    if e < 8:
                        for tl in range(4):
                            tt2 = th * 4 + tl
                            v.tensor_scalar(out=spad[:, 0:128], in0=ones_f[:, 0:128],
                                            scalar1=betE[:, tt2 * 8 + e:tt2 * 8 + e + 1],
                                            scalar2=None, op0=ALU.mult)
                            t.transpose(PS(6)[:, 0:128], spad[:, 0:128], idnf[:])
                            v.tensor_copy(recipr[:, tl * 128:(tl + 1) * 128],
                                          PS(6)[:, 0:128])
                    for hb in range(8):
                        b1 = std1.acquire()
                        b3 = std3.acquire()
                        for ci in range(16):
                            xnc = xn[:, ci * T + ts_.start:ci * T + ts_.stop]
                            t.matmul(PS(0)[:, 0:512],
                                     b1[:, ci * 128:(ci + 1) * 128], xnc,
                                     start=(ci == 0), stop=(ci == 15))
                            t.matmul(PS(1)[:, 0:512],
                                     b3[:, ci * 128:(ci + 1) * 128], xnc,
                                     start=(ci == 0), stop=(ci == 15))
                        std1.done(); std3.done()
                        if nD[0] < len(srcsD1):
                            std1.issue(srcsD1[nD[0]])
                            std3.issue(srcsD3[nD[0]])
                            nD[0] += 1
                        hs_ = hslot(e * 8 + hb)
                        if e < 8:
                            v.tensor_tensor(out=pTt[:, 0:512], in0=PS(0)[:, 0:512],
                                            in1=recipr[:, 0:512], op=ALU.mult)
                            s.activation(out=sqb[:, 0:512], in_=pTt[:, 0:512],
                                         func=AF.Silu)
                            v.tensor_tensor(out=pTt[:, 0:512], in0=PS(1)[:, 0:512],
                                            in1=recipr[:, 0:512], op=ALU.mult)
                            v.tensor_tensor(out=hs_, in0=sqb[:, 0:512],
                                            in1=pTt[:, 0:512], op=ALU.mult)
                        else:
                            s.activation(out=sqb[:, 0:512], in_=PS(0)[:, 0:512],
                                         func=AF.Silu)
                            v.tensor_copy(pTt[:, 0:512], PS(1)[:, 0:512])
                            v.tensor_tensor(out=hs_, in0=sqb[:, 0:512],
                                            in1=pTt[:, 0:512], op=ALU.mult)
                stWo1 = Stream(ch, nc.gpsimd, nc.tensor, swA,
                               [wtile[:, 0:2048], mskb[:, 0:2048]])
                stw2d = Stream(ch, nc.gpsimd, nc.tensor, swB,
                               [wtile2[:, 0:512], wtile2[:, 512:1024]])
                w2dsrc = [(w2ed[e, hc, :, grp * 512:(grp + 1) * 512] if e < 8
                           else ws2d[hc, :, grp * 512:(grp + 1) * 512])
                          for grp in range(4) for e in range(9) for hc in range(8)]
                stWo1.issue(wod1[0, :, :]); stWo1.issue(wod1[1, :, :])
                stw2d.issue(w2dsrc[0]); stw2d.issue(w2dsrc[1])
                nWo1 = [2]; nw2d = [2]
                for grp in range(4):
                    for j in range(4):
                        dmb = grp * 4 + j
                        wb_ = stWo1.acquire()
                        for ci in range(16):
                            t.matmul(PS(j)[:, 0:512],
                                     wb_[:, ci * 128:(ci + 1) * 128],
                                     big[:, ci * TOK + ts_.start:ci * TOK + ts_.stop],
                                     start=(ci == 0), stop=False)
                        stWo1.done()
                        if nWo1[0] < 16:
                            stWo1.issue(wod1[nWo1[0], :, :]); nWo1[0] += 1
                    for e in range(9):
                        for hc in range(8):
                            wb2 = stw2d.acquire()
                            hs_ = hslot(e * 8 + hc)
                            for j in range(4):
                                t.matmul(PS(j)[:, 0:512],
                                         wb2[:, j * 128:(j + 1) * 128], hs_,
                                         start=False,
                                         stop=(e == 8 and hc == 7))
                            stw2d.done()
                            if nw2d[0] < len(w2dsrc):
                                stw2d.issue(w2dsrc[nw2d[0]]); nw2d[0] += 1
                    for j in range(4):
                        dmb = grp * 4 + j
                        g.dma_start(out=xs4[:, 0:512],
                                    in_=x1dr[dmb * 128:(dmb + 1) * 128, ts_])
                        v.tensor_tensor(out=ytile[:, 0:512], in0=PS(j)[:, 0:512],
                                        in1=xs4[:, 0:512], op=ALU.add)
                        g.dma_start(out=yo[dmb * 128:(dmb + 1) * 128, ts_],
                                    in_=ytile[:, 0:512])
            flag_write(3)
            spin(3)
            ch.finish(nc.gpsimd)
    return nc


def prep_full(inp):
    import time
    x = np.asarray(inp['L0_attn_hidden_states'])
    wqT = head_tiles(np.ascontiguousarray(np.asarray(inp['L0_attn_Wq']).T), NH)
    wkT = head_tiles(np.ascontiguousarray(np.asarray(inp['L0_attn_Wk']).T), NH)
    wvT = head_tiles(np.ascontiguousarray(np.asarray(inp['L0_attn_Wv']).T), NH)
    bq = np.asarray(inp['L0_attn_bq']).reshape(NH, HD)
    bk = np.asarray(inp['L0_attn_bk']).reshape(NH, HD)
    bv = np.asarray(inp['L0_attn_bv']).reshape(NH, HD)
    woT = head_tiles(np.ascontiguousarray(np.asarray(inp['L0_attn_Wo']).T), 16)
    bo = np.ascontiguousarray(np.asarray(inp['L0_attn_bo']).reshape(16, 128).T)
    w1t = head_tiles(np.ascontiguousarray(np.asarray(inp['L0_ffn_W1'])[PI].T), 64)
    w3t = head_tiles(np.ascontiguousarray(np.asarray(inp['L0_ffn_W3'])[PI].T), 64)
    w2T = np.ascontiguousarray(np.asarray(inp['L0_ffn_W2']).T)
    wq1 = head_tiles(np.ascontiguousarray(np.asarray(inp['L1_attn_Wq'])), NH)
    wk1 = head_tiles(np.ascontiguousarray(np.asarray(inp['L1_attn_Wk'])), NH)
    wv1 = head_tiles(np.ascontiguousarray(np.asarray(inp['L1_attn_Wv'])), NH)
    cq = np.asarray(inp['L1_attn_conv_q_w']).reshape(NH, 128, 4)
    ck = np.asarray(inp['L1_attn_conv_k_w']).reshape(NH, 128, 4)
    cv = np.asarray(inp['L1_attn_conv_v_w']).reshape(NH, 128, 4)
    alog = np.asarray(inp['L1_attn_A_log']).reshape(1, NH)
    wfa = chunk_tile(np.ascontiguousarray(np.asarray(inp['L1_attn_Wfa'])))
    wfb = np.asarray(inp['L1_attn_Wfb'])
    dtb = np.asarray(inp['L1_attn_dt_bias']).reshape(NH, 128)
    wga = chunk_tile(np.ascontiguousarray(np.asarray(inp['L1_attn_Wga'])))
    wgb = np.asarray(inp['L1_attn_Wgb'])
    onw = np.ascontiguousarray(np.asarray(inp['L1_attn_o_norm_w']).reshape(128, 1))
    wo1 = head_tiles(np.ascontiguousarray(np.asarray(inp['L1_attn_Wo'])), 16)
    gt = chunk_tile(np.ascontiguousarray(np.asarray(inp['L1_ffn_gate_w']).T))
    w1 = np.asarray(inp['L1_ffn_w1'])
    w3 = np.asarray(inp['L1_ffn_w3'])
    w2 = np.asarray(inp['L1_ffn_w2'])
    w1t1 = np.stack([head_tiles(np.ascontiguousarray(w1[e].T), 8) for e in range(8)])
    w3t1 = np.stack([head_tiles(np.ascontiguousarray(w3[e].T), 8) for e in range(8)])
    w2g = np.stack([np.ascontiguousarray(w2[e].T).reshape(8, 128, 2048)
                    for e in range(8)])
    ws1t = head_tiles(np.ascontiguousarray(np.asarray(inp['L1_ffn_ws1']).T), 8)
    ws3t = head_tiles(np.ascontiguousarray(np.asarray(inp['L1_ffn_ws3']).T), 8)
    ws2g = np.ascontiguousarray(np.asarray(inp['L1_ffn_ws2']).T).reshape(8, 128, 2048)
    nonce = int(time.time_ns() % 1000000000)
    non8 = np.arange(8, dtype=np.int32).reshape(1, 8) + nonce
    maps = []
    for core in range(8):
        b, hh = core // 2, core % 2
        hs = slice(hh * HPC, (hh + 1) * HPC)
        bqkv = np.ascontiguousarray(
            np.concatenate([bq[hs].T, bk[hs].T, bv[hs].T], axis=1))
        wbt = chunk_tile(np.ascontiguousarray(np.asarray(inp['L1_attn_Wb'])[:, hs]))
        maps.append({
            'x_fm': np.ascontiguousarray(x[b].T),
            'x_half': np.ascontiguousarray(x[b].T[:, hh * TOK:(hh + 1) * TOK]),
            'nonce8': non8,
            'wq_t': np.ascontiguousarray(wqT[hs]),
            'wk_t': np.ascontiguousarray(wkT[hs]),
            'wv_t': np.ascontiguousarray(wvT[hs]),
            'bqkv': bqkv,
            'rope_cos': CONSTS['rope_cos'], 'rope_sin': CONSTS['rope_sin'],
            'tri_keep': CONSTS['tri_keep'], 'ident_bf': CONSTS['ident_bf'],
            'ident_f32': CONSTS['ident_f32'],
            'wo_t': woT, 'bo_s': bo, 'w1_t': w1t, 'w3_t': w3t, 'w2_t': w2T,
            'h128': CONSTS['h128'], 'h64blk': CONSTS['h64blk'],
            'wq1_t': np.ascontiguousarray(wq1[hs]),
            'wk1_t': np.ascontiguousarray(wk1[hs]),
            'wv1_t': np.ascontiguousarray(wv1[hs]),
            'cq_s': np.ascontiguousarray(cq[hs]),
            'ck_s': np.ascontiguousarray(ck[hs]),
            'cv_s': np.ascontiguousarray(cv[hs]),
            'alog_s': np.ascontiguousarray(alog[:, hs]),
            'wfa_t': wfa,
            'wfb_s': np.ascontiguousarray(wfb[:, hh * 1024:(hh + 1) * 1024]),
            'dtb_s': np.ascontiguousarray(dtb[hs]),
            'wb_t': wbt,
            'wga_t': wga,
            'wgb_s': np.ascontiguousarray(wgb[:, hh * 1024:(hh + 1) * 1024]),
            'onw': onw,
            'cumpack': CONSTS['cumpack'], 'wm_masks': CONSTS['wm_masks'],
            'wo1_t': wo1, 'gate_t': gt,
            'w1e_t': w1t1, 'w3e_t': w3t1, 'w2e_g': w2g,
            'ws1_t': ws1t, 'ws3_t': ws3t, 'ws2_g': ws2g,
        })
    return maps

def kernel(**inputs):
    res = run_bass_kernel_spmd(build_full(), prep_full(inputs),
                               core_ids=list(range(8)))
    out = np.empty((B, T, D), dtype=np.float32)
    for core in range(8):
        b, half = core // 2, core % 2
        out[b, half * TOK:(half + 1) * TOK, :] = res.results[core]['y_half'].T
    return out


# version 04 (final)
# speedup vs baseline: 1.4014x; 1.0042x over previous
import numpy as np
import ml_dtypes
from contextlib import ExitStack
import concourse.bass as bass
import concourse.mybir as mybir
from concourse.bass_utils import run_bass_kernel_spmd
from concourse.bass import OrderedEngineSet

F32 = mybir.dt.float32
BF16 = mybir.dt.bfloat16
FP16 = mybir.dt.float16
AF = mybir.ActivationFunctionType
ALU = mybir.AluOpType
AX = mybir.AxisListType
NPBF = ml_dtypes.bfloat16

B, T, D = 4, 2048, 2048
NH, HD = 16, 128
HPC = 8
FH = 8192
C = 128
NCK = T // C
TOK = 1024
RS2 = 1.0 / np.sqrt(2048.0)

def hadamard(n):
    h = np.array([[1.0]])
    while h.shape[0] < n:
        h = np.block([[h, h], [h, -h]])
    return h

def chunk_tile(a):
    Dr, Nc = a.shape
    nck = Dr // 128
    return np.ascontiguousarray(
        a.reshape(nck, 128, Nc).transpose(1, 0, 2).reshape(128, nck * Nc))

def head_tiles(aT, nheads, hd=128):
    Dr = aT.shape[0]
    nck = Dr // 128
    out = np.empty((nheads, 128, nck * hd), dtype=aT.dtype)
    for h in range(nheads):
        out[h] = chunk_tile(np.ascontiguousarray(aT[:, h * hd:(h + 1) * hd]))
    return out

def host_consts():
    c = {}
    inv = 1.0 / 10000.0 ** (np.arange(0, HD, 2, dtype=np.float64) / HD)
    t = np.arange(T, dtype=np.float64)
    fr = t[:, None] * inv[None, :]
    c['rope_cos'] = np.concatenate([np.cos(fr)] * 2, 1).T.astype(NPBF)
    sn = np.concatenate([np.sin(fr)] * 2, 1).T
    sn[0:64] = -sn[0:64]
    c['rope_sin'] = sn.astype(NPBF)
    kk, qq = np.meshgrid(np.arange(128), np.arange(128), indexing='ij')
    c['tri_keep'] = (kk <= qq).astype(NPBF)
    c['ident_bf'] = np.eye(128, dtype=np.float32).astype(NPBF)
    c['ident_f32'] = np.eye(128, dtype=np.float32)
    c['h128'] = hadamard(128).astype(NPBF)
    h64 = hadamard(64)
    z64 = np.zeros((64, 64))
    c['h64blk'] = np.block([[h64, z64], [z64, h64]]).astype(NPBF)
    tau = np.arange(C)[:, None]
    tt = np.arange(C)[None, :]
    U = (tau <= tt).astype(np.float64)
    packs = [np.zeros((C, C))]
    for lv in [1, 2, 4, 8, 16, 32, 64]:
        ablk = ((tt // lv) % 2) == 1
        CUM = (tau < lv * (tt // lv)).astype(np.float64)
        CUMB = (tau < lv * (tt // lv + 1)).astype(np.float64)
        packs.append(np.where(ablk, CUM - U, U - CUMB))
    packs.append(-U)
    packs.append(U - 1.0)
    c['cumpack'] = np.concatenate(packs, axis=1).astype(np.float16)
    masks = [np.concatenate([np.zeros((C, C)), np.eye(C)], 1).astype(NPBF)]
    tv, sv = np.meshgrid(np.arange(C), np.arange(C), indexing='ij')
    for lv in [1, 2, 4, 8, 16, 32, 64]:
        m = ((tv // lv == sv // lv + 1) & ((tv // lv) % 2 == 1)).astype(np.float64)
        masks.append(np.concatenate([m.T, m.T], 1).astype(NPBF))
    c['wm_masks'] = np.ascontiguousarray(np.stack(masks))
    return c

def make_pi():
    j = np.arange(FH)
    return (j % 64) * 128 + 2 * (j // 128) + (j // 64) % 2

CONSTS = host_consts()
PI = make_pi()

class Ch:
    def __init__(self, nc, sem):
        self.nc, self.sem = nc, sem
        self.val = 0
        self.last_eng = None
        self.last_name = ''
        self.pending = None
    def _flush(self):
        if self.pending is not None:
            inst, inc = self.pending
            inst.then_inc(self.sem, inc)
            self.val += inc
            self.pending = None
    PE_OPS = ('matmul', 'transpose', 'ldweights')
    def op(self, e, name, *args, **kw):
        is_dma = name.startswith('dma_start')
        pe_chain = (name in self.PE_OPS and self.last_eng is e
                    and self.last_name in self.PE_OPS)
        if not pe_chain:
            self._flush()
            e.wait_ge(self.sem, self.val)
            self.last_eng = e
        inst = getattr(e, name)(*args, **kw)
        self.last_name = name
        self.pending = (inst, 16 if is_dma else 1)
        return inst
    def finish(self, e):
        self._flush()
        e.wait_ge(self.sem, self.val)

class E:
    def __init__(self, ch, eng):
        self.ch, self.eng = ch, eng
    def __getattr__(self, name):
        def f(*a, **kw):
            return self.ch.op(self.eng, name, *a, **kw)
        return f


import os
SSER = set(os.environ.get('SSER', '').split(','))

class Stream:
    def __init__(self, ch, ge, te, sw, bufs, name='?'):
        self.name = name
        self.serial = (name in SSER) or ('ALL' in SSER)
        self.q = []
        self.ch, self.ge, self.te, self.sw, self.bufs = ch, ge, te, sw, bufs
        if not hasattr(ch, 'scnt'):
            ch.scnt = {}
        self.key = id(sw)
        self.base = ch.scnt.get(self.key, 0)
        self.n_issued = 0
        self.n_acq = 0
        self.vhist = []
        ch._flush()
        self.safe_after = ch.val
    def issue(self, src):
        if self.serial:
            self.q.append(src)
            return
        it = self.n_issued
        self.ch._flush()
        self.ge.wait_ge(self.ch.sem, self.ch.val)
        self.ge.dma_start(out=(self.bufs[it % 2] if not callable(self.bufs[it % 2])
                               else self.bufs[it % 2]()),
                          in_=src).then_inc(self.sw, 16)
        self.n_issued += 1
        self.ch.scnt[self.key] = self.base + 16 * self.n_issued
    def acquire(self):
        it = self.n_acq
        if self.serial:
            src = self.q.pop(0)
            b = self.bufs[it % 2]
            self.ch.op(self.ge_raw if False else self._geng(), 'dma_start',
                       out=b, in_=src)
            self.n_acq += 1
            return b
        self.te.wait_ge(self.sw, self.base + 16 * (it + 1))
        self.n_acq += 1
        b = self.bufs[it % 2]
        return b() if callable(b) else b
    def _geng(self):
        return self.ge
    def done(self):
        self.ch._flush()
        self.vhist.append(self.ch.val)


def build_full():
    nc = bass.Bass("TRN2", target_bir_lowering=False)
    DT = nc.dram_tensor
    xin = DT("x_fm", [D, T], F32, kind="ExternalInput").ap()
    xhalf = DT("x_half", [D, TOK], F32, kind="ExternalInput").ap()
    nonced = DT("nonce8", [1, 8], mybir.dt.int32, kind="ExternalInput").ap()
    wq = DT("wq_t", [HPC, 128, 16 * HD], F32, kind="ExternalInput").ap()
    wk = DT("wk_t", [HPC, 128, 16 * HD], F32, kind="ExternalInput").ap()
    wv = DT("wv_t", [HPC, 128, 16 * HD], F32, kind="ExternalInput").ap()
    bqkv = DT("bqkv", [HD, 3 * HPC], F32, kind="ExternalInput").ap()
    cosd = DT("rope_cos", [HD, T], BF16, kind="ExternalInput").ap()
    sind = DT("rope_sin", [HD, T], BF16, kind="ExternalInput").ap()
    trid = DT("tri_keep", [128, 128], BF16, kind="ExternalInput").ap()
    idnd = DT("ident_bf", [128, 128], BF16, kind="ExternalInput").ap()
    wod = DT("wo_t", [16, 128, D], F32, kind="ExternalInput").ap()
    bod = DT("bo_s", [128, 16], F32, kind="ExternalInput").ap()
    w1d = DT("w1_t", [64, 128, D], F32, kind="ExternalInput").ap()
    w3d = DT("w3_t", [64, 128, D], F32, kind="ExternalInput").ap()
    w2d = DT("w2_t", [FH, D], F32, kind="ExternalInput").ap()
    h128d = DT("h128", [128, 128], BF16, kind="ExternalInput").ap()
    h64d = DT("h64blk", [128, 128], BF16, kind="ExternalInput").ap()
    wqd1 = DT("wq1_t", [HPC, 128, D], F32, kind="ExternalInput").ap()
    wkd1 = DT("wk1_t", [HPC, 128, D], F32, kind="ExternalInput").ap()
    wvd1 = DT("wv1_t", [HPC, 128, D], F32, kind="ExternalInput").ap()
    cqd = DT("cq_s", [HPC, 128, 4], F32, kind="ExternalInput").ap()
    ckd = DT("ck_s", [HPC, 128, 4], F32, kind="ExternalInput").ap()
    cvd = DT("cv_s", [HPC, 128, 4], F32, kind="ExternalInput").ap()
    alogd = DT("alog_s", [1, HPC], F32, kind="ExternalInput").ap()
    wfad = DT("wfa_t", [128, D], F32, kind="ExternalInput").ap()
    wfbd = DT("wfb_s", [128, HPC * 128], F32, kind="ExternalInput").ap()
    dtbd = DT("dtb_s", [HPC, 128], F32, kind="ExternalInput").ap()
    wbd = DT("wb_t", [128, 16 * HPC], F32, kind="ExternalInput").ap()
    wgad = DT("wga_t", [128, D], F32, kind="ExternalInput").ap()
    wgbd = DT("wgb_s", [128, HPC * 128], F32, kind="ExternalInput").ap()
    onwd = DT("onw", [128, 1], F32, kind="ExternalInput").ap()
    cpkd = DT("cumpack", [128, 1280], FP16, kind="ExternalInput").ap()
    mskd = DT("wm_masks", [8, 128, 256], BF16, kind="ExternalInput").ap()
    idnfd = DT("ident_f32", [128, 128], F32, kind="ExternalInput").ap()
    wod1 = DT("wo1_t", [16, 128, D], F32, kind="ExternalInput").ap()
    gted = DT("gate_t", [128, 16 * 8], F32, kind="ExternalInput").ap()
    w1ed = DT("w1e_t", [8, 8, 128, D], F32, kind="ExternalInput").ap()
    w3ed = DT("w3e_t", [8, 8, 128, D], F32, kind="ExternalInput").ap()
    w2ed = DT("w2e_g", [8, 8, 128, D], F32, kind="ExternalInput").ap()
    ws1d = DT("ws1_t", [8, 128, D], F32, kind="ExternalInput").ap()
    ws3d = DT("ws3_t", [8, 128, D], F32, kind="ExternalInput").ap()
    ws2d = DT("ws2_g", [8, 128, D], F32, kind="ExternalInput").ap()
    yo = DT("y_half", [D, TOK], F32, kind="ExternalOutput").ap()
    w2s = DT("w2s", [64, 128, D], BF16, kind="Internal").ap()
    x1dr = DT("x1dr", [D, TOK], F32, kind="Internal").ap()
    mbctx = DT("mb_ctx", [D, T], BF16, kind="Internal", addr_space="Shared").ap()
    mbx1 = DT("mb_x1", [D, T], BF16, kind="Internal", addr_space="Shared").ap()
    mbo = DT("mb_o", [D, T], BF16, kind="Internal", addr_space="Shared").ap()
    mbfl = DT("mb_fl", [8, 4], mybir.dt.int32, kind="Internal", addr_space="Shared").ap()

    st = ExitStack()
    sb = lambda n, s_, d_: st.enter_context(nc.sbuf_tensor(n, s_, d_))
    with st:
        xn = sb("xn", [128, 16 * T], BF16)
        big = sb("big", [128, 32768], BF16)
        wtile = sb("wtile", [128, D], BF16)
        wtile2 = sb("wtile2", [128, D], BF16)
        stats = sb("stats", [128, T], F32)
        stats2 = sb("stats2", [128, T], F32)
        scrB = sb("scrB", [128, 8192], BF16)
        scrF = sb("scrF", [128, 3072], F32)
        g_tm = sb("g_tm", [128, 16 * 128], FP16)
        xs4 = sb("xs4", [128, 512], F32)
        ytile = sb("ytile", [128, 512], F32)
        nonceb = sb("nonceb", [1, 8], mybir.dt.int32)
        biast = sb("biast", [128, 3 * HPC], F32)
        bob = sb("bob", [128, 16], F32)
        cwb = sb("cwb", [128, 12], F32)
        dtbrep = sb("dtbrep", [128, 128], F32)
        arep = sb("arep", [128, HPC], F32)
        onwb = sb("onwb", [128, 1], F32)
        gtb = sb("gtb", [128, 16 * 8], F32)
        cpkb = sb("cpkb", [128, 1280], FP16)
        mskb = sb("mskb", [128, 8 * 256], BF16)
        idnb = sb("idnb", [128, 128], BF16)
        idnf = sb("idnf", [128, 128], F32)
        trib = sb("trib", [128, 128], BF16)
        h128b = sb("h128b", [128, 128], BF16)
        h64b = sb("h64b", [128, 128], BF16)
        ones_f = sb("ones_f", [128, 128], F32)
        ones_b = sb("ones_b", [128, 128], BF16)
        ccell = sb("ccell", [128, 8], F32)
        wfbb = sb("wfbb", [128, HPC * 128], BF16)
        wgbb = sb("wgbb", [128, HPC * 128], BF16)
        wbb = sb("wbb", [128, 16 * HPC], BF16)
        Sf = sb("Sf", [128, 128], F32)
        svec = sb("svec", [8, 1], F32)
        psALL = st.enter_context(nc.psum_tensor("psALL", [128, 3584], F32))
        psTR = st.enter_context(nc.psum_tensor("psTR", [128, 1024], BF16))
        sem = st.enter_context(nc.semaphore("chain"))
        swA = st.enter_context(nc.semaphore("swA"))
        swB = st.enter_context(nc.semaphore("swB"))
        blk = st.enter_context(nc.Block())

        PS = lambda b_, w=512: psALL[:, b_ * 512:b_ * 512 + w]
        ps_trb = psTR[:, 0:128]

        pTt = scrB[:, 0:512]
        sqb = scrB[:, 512:1024]
        Eb = scrB[:, 1024:2304]
        KE = scrB[:, 2304:3584]
        QE = scrB[:, 3584:4864]
        WMb = scrB[:, 4864:5120]
        WMt = scrB[:, 5120:5376]
        Pb = scrB[:, 5376:6144]
        Qb = scrB[:, 6144:6912]
        zb = scrB[:, 6912:7040]
        zb2 = scrB[:, 7040:7168]
        ub = scrB[:, 7168:7296]
        rbb = scrB[:, 7296:7424]
        vtm = scrB[:, 7424:7552]
        kmt = scrB[:, 7552:7680]
        Sbb = scrB[:, 7680:7808]
        Atb = scrB[:, 7808:7936]
        Abb = scrB[:, 7936:8064]
        bndneg = scrB[:, 8064:8192]
        tmp_f = scrF[:, 0:512]
        recipr = scrF[:, 512:1024]
        tmpr = scrF[0:1, 1024:1536]
        tmpr2 = scrF[0:1, 1536:2048]
        t0f = scrF[:, 2048:2176]
        t1f = scrF[:, 2176:2304]
        bpad = scrF[:, 2304:2816]
        bT = scrF[:, 2816:2944]
        kmax = scrF[0:1, 2944:2952]
        lsb = scrF[0:8, 0:512]
        msS = scrF[0:8, 512:1024]
        spad = scrF[:, 2304:2816]
        stm = scrF[:, 2816:2880]
        m1r = scrF[:, 2880:2888]
        m1b = scrF[:, 2888:2896]
        cum = scrF[:, 2896:2904]
        s2b = scrF[:, 2904:2912]
        msum = scrF[:, 2912:2920]
        betE = scrF[:, 2952:3016]
        cosb = big[:, 0:2048]
        sinb = big[:, 2048:4096]
        qbA = big[:, 4096:6144]
        kbA = big[:, 6144:8192]
        vbA = big[:, 8192:10240]
        vtA = big[:, 10240:12288]
        ctxb = big[:, 12288:14336]
        rotb = big[:, 14336:16384]
        qcb = big[:, 0:2048]
        kcb = big[:, 2048:4096]
        vcb = big[:, 4096:6144]
        cvA = big[:, 6144:8192]
        cvB = big[:, 8192:10240]
        sgo = big[:, 10240:12288]
        ofm = big[:, 12288:14336]
        xfa = big[:, 14336:16384]
        xga = big[:, 16384:18432]
        mean_sb = stats[0:1, :]
        rstd_sb = stats2[0:1, :]
        colsq = stats[32:33, :]

        @blk.gpsimd
        def _body(_g):
            ch = Ch(nc, sem)
            sy, g, t, v, s = (E(ch, nc.sync), E(ch, nc.gpsimd), E(ch, nc.tensor),
                              E(ch, nc.vector), E(ch, nc.scalar))
            ge = nc.gpsimd
            rg0 = ge.alloc_register("rg0")
            rg1 = ge.alloc_register("rg1")
            rgn = ge.alloc_register("rgn")

            def arm_dmas(fn, n_ops):
                ch._flush()
                ge.wait_ge(sem, ch.val)
                for arm in ge.switch_core_id(n=128):
                    hh = arm.logical % 2
                    lst = fn(hh)
                    assert len(lst) == n_ops
                    for (dst, src) in lst:
                        ge.dma_start(out=dst, in_=src).then_inc(sem, 16)
                ch.val += 16 * n_ops
                ge.wait_ge(sem, ch.val)
                ch.last_eng = None
                ch.last_name = ''

            def spin(phase):
                ch._flush()
                ge.wait_ge(sem, ch.val)
                ge.load(rgn, nonceb[0:1, phase:phase + 1])
                lp, c2, en = f"sp{phase}", f"sc{phase}", f"se{phase}"
                ge.br(lp)
                eng1 = OrderedEngineSet([ge.engine])
                with nc.body(lp, valid_engines=eng1):
                    ge.load(rg0, mbfl[phase * 2:phase * 2 + 1, 0:1])
                    ge.load(rg1, mbfl[phase * 2 + 1:phase * 2 + 2, 0:1])
                    ge.br_cmp(rg0, rgn, on_true=c2, on_false=lp, comp_op="IS_EQ")
                with nc.body(c2, valid_engines=eng1):
                    ge.br_cmp(rg1, rgn, on_true=en, on_false=lp, comp_op="IS_EQ")
                nc.switch_bb(en)
                ge.sem_inc(sem, 1)
                ch.val += 1
                ge.wait_ge(sem, ch.val)
                ch.last_eng = None
                ch.last_name = ''

            def flag_write(phase):
                arm_dmas(lambda hh: [(mbfl[phase * 2 + hh:phase * 2 + hh + 1, 0:1],
                                      nonceb[0:1, phase:phase + 1])], 1)

            sy.dma_start(out=nonceb[:], in_=nonced[:, :])
            sy.dma_start(out=cosb[:], in_=cosd[:, :])
            sy.dma_start(out=sinb[:], in_=sind[:, :])
            sy.dma_start(out=trib[:], in_=trid[:, :])
            sy.dma_start(out=idnb[:], in_=idnd[:, :])
            sy.dma_start(out=idnf[:], in_=idnfd[:, :])
            sy.dma_start(out=biast[:, 0:3 * HPC], in_=bqkv[:, :])
            sy.dma_start(out=bob[:], in_=bod[:, :])
            sy.dma_start(out=h128b[:], in_=h128d[:, :])
            sy.dma_start(out=h64b[:], in_=h64d[:, :])
            sy.dma_start(out=cpkb[:], in_=cpkd[:, :])
            for i in range(8):
                sy.dma_start(out=mskb[:, i * 256:(i + 1) * 256], in_=mskd[i, :, :])
            sy.dma_start(out=onwb[:], in_=onwd[:, :])
            sy.dma_start(out=gtb[:], in_=gted[:, :])
            g.dma_start(out=wfbb[:], in_=wfbd[:, :])
            g.dma_start(out=wgbb[:], in_=wgbd[:, :])
            g.dma_start(out=wbb[:], in_=wbd[:, :])
            g.memset(ones_f[:], 1.0)
            g.memset(ones_b[:], 1.0)
            for i, cv in enumerate([1e-5, 1e-6, 1.0, float(np.log(128.0 ** -0.5))]):
                g.memset(ccell[:, i:i + 1], cv)
                nc.const_aps.register(F32, cv, ccell[:, i:i + 1])

            def ln_full(src_getter, nchunkT):
                for q in range(4):
                    qs = slice(q * 512, (q + 1) * 512)
                    for ci in range(16):
                        t.matmul(PS(6)[0:1, 0:512], ones_b[:, 0:1],
                                 src_getter(ci)[:, qs], start=(ci == 0), stop=(ci == 15))
                    v.tensor_scalar(out=mean_sb[0:1, qs], in0=PS(6)[0:1, 0:512],
                                    scalar1=1.0 / 2048.0, scalar2=None, op0=ALU.mult)
                    for ci in range(16):
                        s.activation(out=sqb[:, 0:512], in_=src_getter(ci)[:, qs],
                                     func=AF.Square)
                        t.matmul(PS(5)[0:1, 0:512], ones_b[:, 0:1], sqb[:, 0:512],
                                 start=(ci == 0), stop=(ci == 15))
                    v.tensor_scalar(out=rstd_sb[0:1, qs], in0=PS(5)[0:1, 0:512],
                                    scalar1=1.0 / 2048.0, scalar2=None, op0=ALU.mult)
                    s.activation(out=tmpr[0:1, 0:512], in_=mean_sb[0:1, qs],
                                 func=AF.Square)
                    v.tensor_tensor(out=rstd_sb[0:1, qs], in0=rstd_sb[0:1, qs],
                                    in1=tmpr[0:1, 0:512], op=ALU.subtract)
                    s.activation(out=tmpr[0:1, 0:512], in_=rstd_sb[0:1, qs],
                                 func=AF.Ln, bias=1e-5)
                    s.activation(out=rstd_sb[0:1, qs], in_=tmpr[0:1, 0:512],
                                 func=AF.Exp, scale=-0.5)
                for ci in range(16):
                    for q in range(4):
                        qs = slice(q * 512, (q + 1) * 512)
                        t.matmul(PS(6)[:, 0:512], ones_f[0:1, :], mean_sb[0:1, qs],
                                 start=True, stop=True)
                        v.tensor_tensor(out=tmp_f[:, 0:512], in0=src_getter(ci)[:, qs],
                                        in1=PS(6)[:, 0:512], op=ALU.subtract)
                        t.matmul(PS(6)[:, 0:512], ones_f[0:1, :], rstd_sb[0:1, qs],
                                 start=True, stop=True)
                        v.tensor_tensor(out=xn[:, ci * T + qs.start:ci * T + qs.stop],
                                        in0=tmp_f[:, 0:512], in1=PS(6)[:, 0:512],
                                        op=ALU.mult)

            for ci in range(16):
                g.dma_start(out=xn[:, ci * T:(ci + 1) * T],
                            in_=xin[ci * 128:(ci + 1) * 128, :])
            ln_full(lambda ci: xn[:, ci * T:(ci + 1) * T], 16)
            stA = Stream(ch, nc.gpsimd, nc.tensor, swA,
                         [wtile[:, 0:2048], big[:, 16384:18432]], name='A')
            srcsA = [w[h, :, :] for h in range(HPC) for w in (wq, wk, wv)]
            stA.issue(srcsA[0])
            stA.issue(srcsA[1])
            nsrcA = [2]
            for h in range(HPC):
                for pi_, (dst, bcol) in enumerate((
                        (qbA, biast[:, h:h + 1]),
                        (kbA, biast[:, HPC + h:HPC + h + 1]),
                        (vbA, biast[:, 2 * HPC + h:2 * HPC + h + 1]))):
                    wbuf = stA.acquire()
                    for q in range(4):
                        qs = slice(q * 512, (q + 1) * 512)
                        for ci in range(16):
                            xnc = xn[:, ci * T + qs.start:ci * T + qs.stop]
                            t.matmul(PS(pi_)[:, 0:512],
                                     wbuf[:, ci * HD:(ci + 1) * HD], xnc,
                                     start=(ci == 0), stop=(ci == 15))
                        if pi_ < 2:
                            s.add(dst[:, qs], PS(pi_)[:, 0:512], bcol)
                        else:
                            s.add(tmp_f[:, 0:512], PS(pi_)[:, 0:512], bcol)
                            v.tensor_copy(vbA[:, qs], tmp_f[:, 0:512])
                    stA.done()
                    if nsrcA[0] < len(srcsA):
                        stA.issue(srcsA[nsrcA[0]])
                        nsrcA[0] += 1
                for src in (qbA, kbA):
                    s.copy(rotb[0:64, :], src[64:128, :])
                    s.copy(rotb[64:128, :], src[0:64, :])
                    v.tensor_tensor(out=rotb[:, :], in0=rotb[:, :], in1=sinb[:, :],
                                    op=ALU.mult)
                    for q in range(4):
                        qs = slice(q * 512, (q + 1) * 512)
                        v.tensor_tensor(out=pTt[:, 0:512], in0=src[:, qs],
                                        in1=cosb[:, qs], op=ALU.mult)
                        v.tensor_tensor(out=src[:, qs], in0=pTt[:, 0:512],
                                        in1=rotb[:, qs], op=ALU.add)
                for kt in range(16):
                    ks = slice(kt * 128, (kt + 1) * 128)
                    t.transpose(ps_trb, vbA[:, ks], idnb[:])
                    v.tensor_copy(vtA[:, ks], ps_trb)
                for q in range(4):
                    qs = slice(q * 512, (q + 1) * 512)
                    s.activation(out=sqb[:, 0:512], in_=kbA[:, qs], func=AF.Square)
                    t.matmul(PS(0)[0:1, 0:512], ones_b[:, 0:1], sqb[:, 0:512],
                             start=True, stop=True)
                    v.tensor_copy(colsq[0:1, qs], PS(0)[0:1, 0:512])
                v.reduce_max(kmax[0:1, 0:1], colsq[0:1, :], axis=AX.X)
                for q in range(4):
                    qs = slice(q * 512, (q + 1) * 512)
                    s.activation(out=sqb[:, 0:512], in_=qbA[:, qs], func=AF.Square)
                    t.matmul(PS(0)[0:1, 0:512], ones_b[:, 0:1], sqb[:, 0:512],
                             start=True, stop=True)
                    v.tensor_scalar(out=tmpr[0:1, 0:512], in0=PS(0)[0:1, 0:512],
                                    scalar1=kmax[0:1, 0:1], scalar2=None, op0=ALU.mult)
                    s.activation(out=tmpr[0:1, 0:512], in_=tmpr[0:1, 0:512],
                                 func=AF.Sqrt)
                    v.tensor_scalar(out=colsq[0:1, qs], in0=tmpr[0:1, 0:512],
                                    scalar1=-1.0, scalar2=None, op0=ALU.mult)
                for q in range(4):
                    qs = slice(q * 512, (q + 1) * 512)
                    nkt = 4 * (q + 1)
                    v.tensor_copy(sqb[0:1, 0:512], colsq[0:1, qs])
                    for kt in range(nkt):
                        ks = slice(kt * 128, (kt + 1) * 128)
                        t.matmul(PS(3)[:, 0:512], kbA[:, ks], qbA[:, qs],
                                 start=True, stop=False)
                        t.matmul(PS(3)[:, 0:512], ones_b[0:1, :], sqb[0:1, 0:512],
                                 start=False, stop=True)
                        s.activation(out=pTt[:, 0:512], in_=PS(3)[:, 0:512],
                                     func=AF.Exp, scale=1.0 / np.sqrt(128.0))
                        if kt * 128 >= qs.start:
                            sub = kt * 128 - qs.start
                            if sub > 0:
                                v.memset(pTt[:, 0:sub], 0.0)
                            v.tensor_tensor(out=pTt[:, sub:sub + 128],
                                            in0=pTt[:, sub:sub + 128], in1=trib[:],
                                            op=ALU.mult)
                        t.matmul(PS(0)[0:1, 0:512], ones_b[:, 0:1], pTt[:, 0:512],
                                 start=(kt == 0), stop=(kt == nkt - 1))
                        t.matmul(PS(4)[:, 0:512], vtA[:, ks], pTt[:, 0:512],
                                 start=(kt == 0), stop=(kt == nkt - 1))
                    v.reciprocal(tmpr[0:1, 0:512], PS(0)[0:1, 0:512])
                    t.matmul(PS(6)[:, 0:512], ones_f[0:1, :], tmpr[0:1, 0:512],
                             start=True, stop=True)
                    v.tensor_copy(recipr[:, 0:512], PS(6)[:, 0:512])
                    v.tensor_tensor(out=ctxb[:, qs], in0=PS(4)[:, 0:512],
                                    in1=recipr[:, 0:512], op=ALU.mult)
                arm_dmas(lambda hh: [(mbctx[(hh * 8 + h) * 128:(hh * 8 + h + 1) * 128, :],
                                      ctxb[:, 0:T])], 1)
            flag_write(0)

            def b_arm(hh):
                lst = []
                for ci in range(16):
                    srcx = xn[:, ci * T + hh * TOK:ci * T + hh * TOK + TOK]
                    dstx = xn[:, ci * T:ci * T + TOK] if hh == 1 else scrB[:, 1024:2048]
                    lst.append((dstx, srcx))
                return lst
            arm_dmas(b_arm, 16)
            stP = Stream(ch, nc.gpsimd, nc.tensor, swA,
                         [wtile[:, 0:2048], scrB[:, 3072:5120]], name='P')
            stP.issue(w2d[0:128, :])
            stP.issue(w2d[128:256, :])
            for n in range(64):
                wb_ = stP.acquire()
                for q in range(4):
                    t.matmul(PS(q)[:, 0:512], h128b[:],
                             wb_[:, q * 512:(q + 1) * 512], start=True, stop=True)
                stP.done()
                if n + 2 < 64:
                    stP.issue(w2d[(n + 2) * 128:(n + 3) * 128, :])
                for q in range(4):
                    v.tensor_copy(wtile2[:, q * 512:(q + 1) * 512], PS(q)[:, 0:512])
                g.dma_start(out=w2s[:, n:n + 65:64, :], in_=wtile2[:, :])
            spin(0)
            arm_dmas(lambda hh: [(xn[:, ci * T + TOK:(ci + 1) * T],
                                  mbctx[ci * 128:(ci + 1) * 128,
                                        hh * TOK:(hh + 1) * TOK])
                                 for ci in range(16)], 16)
            for th in range(2):
                ts_ = slice(th * 512, (th + 1) * 512)
                st1 = Stream(ch, nc.gpsimd, nc.tensor, swA,
                             [wtile[:, 0:2048], scrB[:, 3072:5120]], name='B1')
                st3 = Stream(ch, nc.gpsimd, nc.tensor, swB,
                             [wtile2[:, 0:2048], scrB[:, 1024:3072]], name='B1')
                st1.issue(w1d[0, :, :]); st3.issue(w3d[0, :, :])
                st1.issue(w1d[1, :, :]); st3.issue(w3d[1, :, :])
                for hb in range(64):
                    b1 = st1.acquire()
                    b3 = st3.acquire()
                    for ci in range(16):
                        xnc = xn[:, ci * T + ts_.start:ci * T + ts_.stop]
                        t.matmul(PS(0)[:, 0:512], b1[:, ci * 128:(ci + 1) * 128],
                                 xnc, start=(ci == 0), stop=(ci == 15))
                        t.matmul(PS(1)[:, 0:512], b3[:, ci * 128:(ci + 1) * 128],
                                 xnc, start=(ci == 0), stop=(ci == 15))
                    st1.done(); st3.done()
                    if hb + 2 < 64:
                        st1.issue(w1d[hb + 2, :, :])
                        st3.issue(w3d[hb + 2, :, :])
                    s.activation(out=sqb[:, 0:512], in_=PS(0)[:, 0:512], func=AF.Silu)
                    v.tensor_tensor(out=pTt[:, 0:512], in0=sqb[:, 0:512],
                                    in1=PS(1)[:, 0:512], op=ALU.mult)
                    t.matmul(PS(2)[:, 0:512], h64b[:], pTt[:, 0:512],
                             start=True, stop=True)
                    s.mul(big[:, hb * 512:(hb + 1) * 512],
                          PS(2)[:, 0:512], 1.0 / np.sqrt(8192.0))
                stWo = Stream(ch, nc.gpsimd, nc.tensor, swA,
                              [wtile[:, 0:2048], scrB[:, 3072:5120]], name='B2')
                stw2 = Stream(ch, nc.gpsimd, nc.tensor, swB,
                              [wtile2[:, 0:512], wtile2[:, 512:1024]], name='B2')
                stWo.issue(wod[0, :, :]); stWo.issue(wod[1, :, :])
                stw2.issue(w2s[0, :, 0:512]); stw2.issue(w2s[1, :, 0:512])
                nWo = [2]; nw2 = [2]
                w2srcs = [(hc, grp) for grp in range(4) for hc in range(64)]
                for grp in range(4):
                    for j in range(4):
                        dmb = grp * 4 + j
                        wb_ = stWo.acquire()
                        for ci in range(16):
                            t.matmul(PS(j)[:, 0:512],
                                     wb_[:, ci * 128:(ci + 1) * 128],
                                     xn[:, ci * T + TOK + ts_.start:ci * T + TOK + ts_.stop],
                                     start=(ci == 0), stop=False)
                        stWo.done()
                        if nWo[0] < 16:
                            stWo.issue(wod[nWo[0], :, :])
                            nWo[0] += 1
                        elif th == 0 and nWo[0] < 17:
                            pass
                    for hc in range(64):
                        wb2 = stw2.acquire()
                        for j in range(4):
                            t.matmul(PS(j)[:, 0:512],
                                     wb2[:, j * 128:(j + 1) * 128],
                                     big[:, hc * 512:(hc + 1) * 512],
                                     start=False, stop=(hc == 63))
                        stw2.done()
                        if nw2[0] < 256:
                            h2, g2 = w2srcs[nw2[0]]
                            stw2.issue(w2s[h2, :, g2 * 512:(g2 + 1) * 512])
                            nw2[0] += 1
                    for j in range(4):
                        dmb = grp * 4 + j
                        g.dma_start(out=xs4[:, 0:512],
                                    in_=xhalf[dmb * 128:(dmb + 1) * 128, ts_])
                        v.scalar_tensor_tensor(out=ytile[:, 0:512], in0=PS(j)[:, 0:512],
                                               scalar=bob[:, dmb:dmb + 1],
                                               in1=xs4[:, 0:512], op0=ALU.add,
                                               op1=ALU.add)
                        g.dma_start(out=x1dr[dmb * 128:(dmb + 1) * 128, ts_],
                                    in_=ytile[:, 0:512])
                        arm_dmas(lambda hh: [(mbx1[dmb * 128:(dmb + 1) * 128,
                                                   hh * TOK + ts_.start:hh * TOK + ts_.stop],
                                              ytile[:, 0:512])], 1)
            flag_write(1)

            spin(1)
            for ci in range(16):
                g.dma_start(out=xn[:, ci * T:(ci + 1) * T],
                            in_=mbx1[ci * 128:(ci + 1) * 128, :])
            ln_full(lambda ci: xn[:, ci * T:(ci + 1) * T], 16)
            stC = Stream(ch, nc.gpsimd, nc.tensor, swA,
                         [wtile[:, 0:2048], big[:, 18432:20480]], name='C')
            srcsC = [wfad[:, :], wgad[:, :]] + [w_[h, :, :] for h in range(HPC)
                                               for w_ in (wqd1, wkd1, wvd1)]
            stC.issue(srcsC[0]); stC.issue(srcsC[1])
            nC = [2]
            for dst in (xfa, xga):
                wsb = stC.acquire()
                for q in range(4):
                    qs = slice(q * 512, (q + 1) * 512)
                    for ci in range(16):
                        t.matmul(PS(0)[:, 0:512], wsb[:, ci * 128:(ci + 1) * 128],
                                 xn[:, ci * T + qs.start:ci * T + qs.stop],
                                 start=(ci == 0), stop=(ci == 15))
                    v.tensor_copy(dst[:, qs], PS(0)[:, 0:512])
                stC.done()
                if nC[0] < len(srcsC):
                    stC.issue(srcsC[nC[0]]); nC[0] += 1
            g.dma_start(out=arep[0:1, 0:HPC], in_=alogd[:, :])
            s.activation(out=arep[0:1, 0:HPC], in_=arep[0:1, 0:HPC], func=AF.Exp)
            t.matmul(PS(0)[:, 0:HPC], ones_f[0:1, :], arep[0:1, 0:HPC],
                     start=True, stop=True)
            v.tensor_copy(arep[:, 0:HPC], PS(0)[:, 0:HPC])
            for q in range(4):
                qs = slice(q * 512, (q + 1) * 512)
                for ci in range(16):
                    t.matmul(PS(0)[0:HPC, 0:512], wbb[:, ci * HPC:(ci + 1) * HPC],
                             xn[:, ci * T + qs.start:ci * T + qs.stop],
                             start=(ci == 0), stop=(ci == 15))
                s.activation(out=bpad[0:HPC, 0:512], in_=PS(0)[0:HPC, 0:512],
                             func=AF.Sigmoid)
                for tl in range(4):
                    tt2 = q * 4 + tl
                    v.tensor_copy(sqb[:, 0:128], bpad[:, tl * 128:(tl + 1) * 128])
                    t.transpose(ps_trb, sqb[:, 0:128], idnb[:])
                    v.tensor_copy(bT[:, tt2 * HPC:tt2 * HPC + HPC], ps_trb[:, 0:HPC])
            for h in range(HPC):
                g.dma_start(out=cwb[:, 0:4], in_=cqd[h, :, :])
                g.dma_start(out=cwb[:, 4:8], in_=ckd[h, :, :])
                g.dma_start(out=cwb[:, 8:12], in_=cvd[h, :, :])
                for dst in (qcb, kcb, vcb):
                    wsb = stC.acquire()
                    for q in range(4):
                        qs = slice(q * 512, (q + 1) * 512)
                        for ci in range(16):
                            t.matmul(PS(0)[:, 0:512], wsb[:, ci * 128:(ci + 1) * 128],
                                     xn[:, ci * T + qs.start:ci * T + qs.stop],
                                     start=(ci == 0), stop=(ci == 15))
                        v.tensor_copy(dst[:, qs], PS(0)[:, 0:512])
                    stC.done()
                    if nC[0] < len(srcsC):
                        stC.issue(srcsC[nC[0]]); nC[0] += 1
                for ti, src in enumerate((qcb, kcb, vcb)):
                    cw = cwb[:, ti * 4:(ti + 1) * 4]
                    v.tensor_scalar(out=cvA[:, :], in0=src[:, :],
                                    scalar1=cw[:, 3:4], scalar2=None, op0=ALU.mult)
                    bufs = [cvA, cvB]
                    for si, jj in enumerate((2, 1, 0)):
                        sh = 3 - jj
                        a_, b_ = bufs[si % 2], bufs[(si + 1) % 2]
                        v.scalar_tensor_tensor(out=b_[:, sh:T], in0=src[:, 0:T - sh],
                                               scalar=cw[:, jj:jj + 1], in1=a_[:, sh:T],
                                               op0=ALU.mult, op1=ALU.add)
                        v.tensor_copy(b_[:, 0:sh], a_[:, 0:sh])
                    s.activation(out=src[:, :], in_=cvB[:, :], func=AF.Silu)
                for ti, src in enumerate((qcb, kcb)):
                    for q in range(4):
                        qs = slice(q * 512, (q + 1) * 512)
                        s.activation(out=sqb[:, 0:512], in_=src[:, qs], func=AF.Square)
                        t.matmul(PS(0)[0:1, 0:512], ones_b[:, 0:1], sqb[:, 0:512],
                                 start=True, stop=True)
                        s.activation(out=tmpr[0:1, 0:512], in_=PS(0)[0:1, 0:512],
                                     func=AF.Ln, bias=1e-6)
                        if ti == 0:
                            s.activation(out=tmpr[0:1, 0:512], in_=tmpr[0:1, 0:512],
                                         func=AF.Exp, scale=-0.5,
                                         bias=float(np.log(128.0 ** -0.5)))
                        else:
                            s.activation(out=tmpr[0:1, 0:512], in_=tmpr[0:1, 0:512],
                                         func=AF.Exp, scale=-0.5)
                        t.matmul(PS(0)[:, 0:512], ones_f[0:1, :], tmpr[0:1, 0:512],
                                 start=True, stop=True)
                        v.tensor_tensor(out=src[:, qs], in0=src[:, qs],
                                        in1=PS(0)[:, 0:512], op=ALU.mult)
                g.dma_start(out=dtbrep[0:1, 0:128], in_=dtbd[h:h + 1, :])
                t.matmul(PS(0)[:, 0:128], ones_f[0:1, :], dtbrep[0:1, 0:128],
                         start=True, stop=True)
                v.tensor_copy(dtbrep[:, 0:128], PS(0)[:, 0:128])
                for tt2 in range(16):
                    t.matmul(PS(1)[:, 0:128], xfa[:, tt2 * 128:(tt2 + 1) * 128],
                             wfbb[:, h * 128:(h + 1) * 128], start=True, stop=True)
                    v.tensor_tensor(out=t0f[:, :], in0=PS(1)[:, 0:128],
                                    in1=dtbrep[:, 0:128], op=ALU.add)
                    s.activation(out=t1f[:, :], in_=t0f[:, :], func=AF.Abs)
                    s.activation(out=t1f[:, :], in_=t1f[:, :], func=AF.Exp, scale=-1.0)
                    s.activation(out=t1f[:, :], in_=t1f[:, :], func=AF.Ln, bias=1.0)
                    v.tensor_scalar(out=t0f[:, :], in0=t0f[:, :], scalar1=0.0,
                                    scalar2=None, op0=ALU.max)
                    v.tensor_tensor(out=t0f[:, :], in0=t0f[:, :], in1=t1f[:, :],
                                    op=ALU.add)
                    v.tensor_scalar(out=g_tm[:, tt2 * 128:(tt2 + 1) * 128],
                                    in0=t0f[:, :], scalar1=arep[:, h:h + 1],
                                    scalar2=None, op0=ALU.mult)
                for q in range(4):
                    qs = slice(q * 512, (q + 1) * 512)
                    t.matmul(PS(0)[:, 0:512], wgbb[:, h * 128:(h + 1) * 128],
                             xga[:, qs], start=True, stop=True)
                    s.activation(out=sgo[:, qs], in_=PS(0)[:, 0:512], func=AF.Sigmoid)
                g.memset(Sf[:], 0.0)
                g.memset(Sbb[:, :], 0.0)
                for ck in range(16):
                    cs = slice(ck * 128, (ck + 1) * 128)
                    gsl = g_tm[:, ck * 128:(ck + 1) * 128]
                    for mi in range(3):
                        fs = slice(mi * 512, min((mi + 1) * 512, 1280))
                        t.matmul(psALL[:, fs], gsl, cpkb[:, fs], start=True, stop=True)
                    s.activation(out=Eb[:, :], in_=psALL[:, 0:1280], func=AF.Exp)
                    for si in range(1, 10):
                        sl = slice(si * 128, (si + 1) * 128)
                        v.tensor_tensor(out=KE[:, sl.start:sl.stop], in0=Eb[:, sl],
                                        in1=kcb[:, cs], op=ALU.mult)
                    for si in list(range(1, 8)) + [8]:
                        sl = slice(si * 128, (si + 1) * 128)
                        v.tensor_tensor(out=QE[:, sl.start:sl.stop], in0=Eb[:, sl],
                                        in1=qcb[:, cs], op=ALU.mult)
                    for si in range(8):
                        bsl = kcb[:, cs] if si == 0 else KE[:, si * 128:(si + 1) * 128]
                        aq = qcb[:, cs] if si == 0 else QE[:, si * 128:(si + 1) * 128]
                        t.matmul(PS(6)[:, 0:128], bsl,
                                 kcb[:, cs] if si == 0 else KE[:, si * 128:(si + 1) * 128],
                                 start=True, stop=True)
                        t.matmul(PS(6)[:, 128:256], bsl, aq, start=True, stop=True)
                        if si == 0:
                            v.tensor_tensor(out=WMb[:, :], in0=PS(6)[:, 0:256],
                                            in1=mskb[:, 0:256], op=ALU.mult)
                        else:
                            v.tensor_tensor(out=WMt[:, :], in0=PS(6)[:, 0:256],
                                            in1=mskb[:, si * 256:(si + 1) * 256],
                                            op=ALU.mult)
                            v.tensor_tensor(out=WMb[:, :], in0=WMb[:, :],
                                            in1=WMt[:, :], op=ALU.add)
                    bcol = bT[:, ck * HPC + h:ck * HPC + h + 1]
                    v.tensor_scalar(out=Atb[:, :], in0=WMb[:, 0:128],
                                    scalar1=bcol, scalar2=None, op0=ALU.mult)
                    t.transpose(ps_trb, Atb[:, :], idnb[:])
                    v.tensor_copy(Abb[:, :], ps_trb)
                    t.transpose(ps_trb, vcb[:, cs], idnb[:])
                    v.tensor_copy(vtm[:, :], ps_trb)
                    t.matmul(PS(5)[:, 0:128], KE[:, 8 * 128:9 * 128], Sbb[:, :],
                             start=True, stop=True)
                    v.tensor_tensor(out=rbb[:, :], in0=vtm[:, :], in1=PS(5)[:, 0:128],
                                    op=ALU.subtract)
                    prevP, prevQ = Abb[:, :], Atb[:, :]
                    for jj in range(1, 7):
                        if jj < 6:
                            t.matmul(PS(5)[:, 128:256], prevQ, prevP,
                                     start=True, stop=True)
                            v.tensor_copy(Pb[:, (jj - 1) * 128:jj * 128],
                                          PS(5)[:, 128:256])
                        t.matmul(PS(5)[:, 256:384], prevP, prevQ, start=True, stop=True)
                        v.tensor_copy(Qb[:, (jj - 1) * 128:jj * 128], PS(5)[:, 256:384])
                        prevP = Pb[:, (jj - 1) * 128:jj * 128]
                        prevQ = Qb[:, (jj - 1) * 128:jj * 128]
                    zcur, znxt = zb, zb2
                    v.tensor_copy(zcur[:, :], rbb[:, :])
                    for jj in range(6, 0, -1):
                        t.matmul(PS(5)[:, 128:256], Qb[:, (jj - 1) * 128:jj * 128],
                                 zcur[:, :], start=True, stop=True)
                        v.tensor_tensor(out=znxt[:, :], in0=zcur[:, :],
                                        in1=PS(5)[:, 128:256], op=ALU.add)
                        zcur, znxt = znxt, zcur
                    t.matmul(PS(5)[:, 128:256], Atb[:, :], zcur[:, :],
                             start=True, stop=True)
                    v.tensor_tensor(out=znxt[:, :], in0=zcur[:, :],
                                    in1=PS(5)[:, 128:256], op=ALU.subtract)
                    v.tensor_scalar(out=ub[:, :], in0=znxt[:, :], scalar1=bcol,
                                    scalar2=None, op0=ALU.mult)
                    t.matmul(PS(5)[:, 256:384], ub[:, :], WMb[:, 128:256],
                             start=True, stop=False)
                    t.matmul(PS(5)[:, 256:384], Sbb[:, :], QE[:, 8 * 128:9 * 128],
                             start=False, stop=True)
                    v.tensor_copy(ofm[:, cs], PS(5)[:, 256:384])
                    t.transpose(ps_trb, KE[:, 9 * 128:10 * 128], idnb[:])
                    v.tensor_copy(kmt[:, :], ps_trb)
                    t.matmul(PS(5)[:, 0:128], kmt[:, :], ub[:, :], start=True, stop=True)
                    v.scalar_tensor_tensor(out=Sf[:, :], in0=Sf[:, :],
                                           scalar=Eb[:, 8 * 128 + 127:8 * 128 + 128],
                                           in1=PS(5)[:, 0:128], op0=ALU.mult,
                                           op1=ALU.add)
                    v.tensor_copy(Sbb[:, :], Sf[:, :])
                for q in range(4):
                    qs = slice(q * 512, (q + 1) * 512)
                    s.activation(out=sqb[:, 0:512], in_=ofm[:, qs], func=AF.Square)
                    t.matmul(PS(0)[0:1, 0:512], ones_b[:, 0:1], sqb[:, 0:512],
                             start=True, stop=True)
                    v.tensor_scalar(out=tmpr[0:1, 0:512], in0=PS(0)[0:1, 0:512],
                                    scalar1=1.0 / 128.0, scalar2=None, op0=ALU.mult)
                    s.activation(out=tmpr[0:1, 0:512], in_=tmpr[0:1, 0:512],
                                 func=AF.Ln, bias=1e-6)
                    s.activation(out=tmpr[0:1, 0:512], in_=tmpr[0:1, 0:512],
                                 func=AF.Exp, scale=-0.5)
                    t.matmul(PS(0)[:, 0:512], ones_f[0:1, :], tmpr[0:1, 0:512],
                             start=True, stop=True)
                    v.tensor_copy(recipr[:, 0:512], PS(0)[:, 0:512])
                    v.scalar_tensor_tensor(out=ofm[:, qs], in0=ofm[:, qs],
                                           scalar=onwb[:, 0:1], in1=sgo[:, qs],
                                           op0=ALU.mult, op1=ALU.mult)
                    v.tensor_tensor(out=ofm[:, qs], in0=ofm[:, qs],
                                    in1=recipr[:, 0:512], op=ALU.mult)
                arm_dmas(lambda hh: [(mbo[(hh * 8 + h) * 128:(hh * 8 + h + 1) * 128, :],
                                      ofm[:, 0:T])], 1)
            flag_write(2)

            def d_arm(hh):
                lst = []
                for ci in range(16):
                    srcx = xn[:, ci * T + hh * TOK:ci * T + hh * TOK + TOK]
                    dstx = xn[:, ci * T:ci * T + TOK] if hh == 1 else scrB[:, 1024:2048]
                    lst.append((dstx, srcx))
                lst.append((mean_sb[0:1, 0:TOK] if hh == 1 else stats[64:65, 0:TOK],
                            mean_sb[0:1, hh * TOK:hh * TOK + TOK]))
                lst.append((rstd_sb[0:1, 0:TOK] if hh == 1 else stats[64:65, 0:TOK],
                            rstd_sb[0:1, hh * TOK:hh * TOK + TOK]))
                return lst
            arm_dmas(d_arm, 18)
            for ci in range(16):
                t.matmul(PS(0)[0:1, 0:8], ones_f[:, 0:1], gtb[:, ci * 8:(ci + 1) * 8],
                         start=(ci == 0), stop=(ci == 15))
            v.tensor_copy(spad[0:1, 0:8], PS(0)[0:1, 0:8])
            t.transpose(PS(6)[:, 0:128], spad[:, 0:128], idnf[:])
            v.tensor_copy(svec[0:8, 0:1], PS(6)[0:8, 0:1])
            for q in range(2):
                qs = slice(q * 512, (q + 1) * 512)
                for ci in range(16):
                    g.dma_start(out=xs4[:, 0:512],
                                in_=x1dr[ci * 128:(ci + 1) * 128, qs])
                    t.matmul(PS(0)[0:8, 0:512], gtb[:, ci * 8:(ci + 1) * 8],
                             xs4[:, 0:512], start=(ci == 0), stop=(ci == 15))
                v.tensor_copy(lsb[:, 0:512], PS(0)[0:8, 0:512])
                t.matmul(PS(1)[0:8, 0:512], ones_f[0:1, 0:8], mean_sb[0:1, qs],
                         start=True, stop=True)
                v.tensor_scalar(out=msS[:, 0:512], in0=PS(1)[0:8, 0:512],
                                scalar1=svec[0:8, 0:1], scalar2=None, op0=ALU.mult)
                v.tensor_tensor(out=lsb[:, 0:512], in0=lsb[:, 0:512],
                                in1=msS[:, 0:512], op=ALU.subtract)
                t.matmul(PS(1)[0:8, 0:512], ones_f[0:1, 0:8], rstd_sb[0:1, qs],
                         start=True, stop=True)
                v.tensor_tensor(out=lsb[:, 0:512], in0=lsb[:, 0:512],
                                in1=PS(1)[0:8, 0:512], op=ALU.mult)
                s.activation(out=spad[0:8, 0:512], in_=lsb[:, 0:512], func=AF.Sigmoid)
                for tl in range(4):
                    tt2 = q * 4 + tl
                    t.transpose(PS(6)[:, 0:128], spad[:, tl * 128:(tl + 1) * 128],
                                idnf[:])
                    v.tensor_copy(stm[:, tt2 * 8:(tt2 + 1) * 8], PS(6)[:, 0:8])
            for tt2 in range(8):
                ssl = stm[:, tt2 * 8:(tt2 + 1) * 8]
                v.reduce_max(m1r[:, 0:1], ssl, axis=AX.X)
                v.tensor_scalar(out=m1r[:, 0:8], in0=ssl, scalar1=m1r[:, 0:1],
                                scalar2=None, op0=ALU.is_ge)
                v.tensor_tensor_scan(out=cum[:, 0:8], data0=ones_f[:, 0:8],
                                     data1=m1r[:, 0:8], initial=0.0,
                                     op0=ALU.mult, op1=ALU.add)
                v.tensor_scalar(out=cum[:, 0:8], in0=cum[:, 0:8], scalar1=1.0,
                                scalar2=None, op0=ALU.is_equal)
                v.tensor_tensor(out=m1b[:, 0:8], in0=cum[:, 0:8], in1=m1r[:, 0:8],
                                op=ALU.mult)
                v.scalar_tensor_tensor(out=s2b[:, 0:8], in0=m1b[:, 0:8], scalar=-2.0,
                                       in1=ssl, op0=ALU.mult, op1=ALU.add)
                v.reduce_max(m1r[:, 0:1], s2b[:, 0:8], axis=AX.X)
                v.tensor_scalar(out=m1r[:, 0:8], in0=s2b[:, 0:8], scalar1=m1r[:, 0:1],
                                scalar2=None, op0=ALU.is_ge)
                v.tensor_tensor_scan(out=cum[:, 0:8], data0=ones_f[:, 0:8],
                                     data1=m1r[:, 0:8], initial=0.0,
                                     op0=ALU.mult, op1=ALU.add)
                v.tensor_scalar(out=cum[:, 0:8], in0=cum[:, 0:8], scalar1=1.0,
                                scalar2=None, op0=ALU.is_equal)
                v.tensor_tensor(out=cum[:, 0:8], in0=cum[:, 0:8], in1=m1r[:, 0:8],
                                op=ALU.mult)
                v.tensor_tensor(out=msum[:, 0:8], in0=m1b[:, 0:8], in1=cum[:, 0:8],
                                op=ALU.add)
                v.tensor_tensor(out=betE[:, tt2 * 8:(tt2 + 1) * 8], in0=ssl,
                                in1=msum[:, 0:8], op=ALU.mult)
            spin(2)
            arm_dmas(lambda hh: [(big[:, ci * TOK:(ci + 1) * TOK],
                                  mbo[ci * 128:(ci + 1) * 128,
                                      hh * TOK:(hh + 1) * TOK])
                                 for ci in range(16)], 16)
            def hslot(idx):
                if idx < 32:
                    return big[:, 16384 + idx * 512:16384 + (idx + 1) * 512]
                if idx < 64:
                    k = idx - 32
                    o = (k // 2) * T + TOK + (k % 2) * 512
                    return xn[:, o:o + 512]
                return scrB[:, 1024 + (idx - 64) * 512:1024 + (idx - 63) * 512]
            srcsD1 = [(w1ed[e, hb, :, :] if e < 8 else ws1d[hb, :, :])
                      for e in range(9) for hb in range(8)]
            srcsD3 = [(w3ed[e, hb, :, :] if e < 8 else ws3d[hb, :, :])
                      for e in range(9) for hb in range(8)]
            for th in range(2):
                ts_ = slice(th * 512, (th + 1) * 512)
                std1 = Stream(ch, nc.gpsimd, nc.tensor, swA,
                              [wtile[:, 0:2048], mskb[:, 0:2048]], name='D1')
                std3 = Stream(ch, nc.gpsimd, nc.tensor, swB,
                              [wtile2[:, 0:2048], scrB[:, 5120:7168]], name='D1')
                std1.issue(srcsD1[0]); std3.issue(srcsD3[0])
                std1.issue(srcsD1[1]); std3.issue(srcsD3[1])
                nD = [2]
                for e in range(9):
                    if e < 8:
                        for tl in range(4):
                            tt2 = th * 4 + tl
                            v.tensor_scalar(out=spad[:, 0:128], in0=ones_f[:, 0:128],
                                            scalar1=betE[:, tt2 * 8 + e:tt2 * 8 + e + 1],
                                            scalar2=None, op0=ALU.mult)
                            t.transpose(PS(6)[:, 0:128], spad[:, 0:128], idnf[:])
                            v.tensor_copy(recipr[:, tl * 128:(tl + 1) * 128],
                                          PS(6)[:, 0:128])
                    for hb in range(8):
                        b1 = std1.acquire()
                        b3 = std3.acquire()
                        for ci in range(16):
                            xnc = xn[:, ci * T + ts_.start:ci * T + ts_.stop]
                            t.matmul(PS(0)[:, 0:512],
                                     b1[:, ci * 128:(ci + 1) * 128], xnc,
                                     start=(ci == 0), stop=(ci == 15))
                            t.matmul(PS(1)[:, 0:512],
                                     b3[:, ci * 128:(ci + 1) * 128], xnc,
                                     start=(ci == 0), stop=(ci == 15))
                        std1.done(); std3.done()
                        if nD[0] < len(srcsD1):
                            std1.issue(srcsD1[nD[0]])
                            std3.issue(srcsD3[nD[0]])
                            nD[0] += 1
                        hs_ = hslot(e * 8 + hb)
                        if e < 8:
                            v.tensor_tensor(out=pTt[:, 0:512], in0=PS(0)[:, 0:512],
                                            in1=recipr[:, 0:512], op=ALU.mult)
                            s.activation(out=sqb[:, 0:512], in_=pTt[:, 0:512],
                                         func=AF.Silu)
                            v.tensor_tensor(out=pTt[:, 0:512], in0=PS(1)[:, 0:512],
                                            in1=recipr[:, 0:512], op=ALU.mult)
                            v.tensor_tensor(out=hs_, in0=sqb[:, 0:512],
                                            in1=pTt[:, 0:512], op=ALU.mult)
                        else:
                            s.activation(out=sqb[:, 0:512], in_=PS(0)[:, 0:512],
                                         func=AF.Silu)
                            v.tensor_copy(pTt[:, 0:512], PS(1)[:, 0:512])
                            v.tensor_tensor(out=hs_, in0=sqb[:, 0:512],
                                            in1=pTt[:, 0:512], op=ALU.mult)
                stWo1 = Stream(ch, nc.gpsimd, nc.tensor, swA,
                               [wtile[:, 0:2048], mskb[:, 0:2048]], name='D2')
                stw2d = Stream(ch, nc.gpsimd, nc.tensor, swB,
                               [wtile2[:, 0:512], wtile2[:, 512:1024]], name='D2')
                w2dsrc = [(w2ed[e, hc, :, grp * 512:(grp + 1) * 512] if e < 8
                           else ws2d[hc, :, grp * 512:(grp + 1) * 512])
                          for grp in range(4) for e in range(9) for hc in range(8)]
                stWo1.issue(wod1[0, :, :]); stWo1.issue(wod1[1, :, :])
                stw2d.issue(w2dsrc[0]); stw2d.issue(w2dsrc[1])
                nWo1 = [2]; nw2d = [2]
                for grp in range(4):
                    for j in range(4):
                        dmb = grp * 4 + j
                        wb_ = stWo1.acquire()
                        for ci in range(16):
                            t.matmul(PS(j)[:, 0:512],
                                     wb_[:, ci * 128:(ci + 1) * 128],
                                     big[:, ci * TOK + ts_.start:ci * TOK + ts_.stop],
                                     start=(ci == 0), stop=False)
                        stWo1.done()
                        if nWo1[0] < 16:
                            stWo1.issue(wod1[nWo1[0], :, :]); nWo1[0] += 1
                    for e in range(9):
                        for hc in range(8):
                            wb2 = stw2d.acquire()
                            hs_ = hslot(e * 8 + hc)
                            for j in range(4):
                                t.matmul(PS(j)[:, 0:512],
                                         wb2[:, j * 128:(j + 1) * 128], hs_,
                                         start=False,
                                         stop=(e == 8 and hc == 7))
                            stw2d.done()
                            if nw2d[0] < len(w2dsrc):
                                stw2d.issue(w2dsrc[nw2d[0]]); nw2d[0] += 1
                    for j in range(4):
                        dmb = grp * 4 + j
                        g.dma_start(out=xs4[:, 0:512],
                                    in_=x1dr[dmb * 128:(dmb + 1) * 128, ts_])
                        v.tensor_tensor(out=ytile[:, 0:512], in0=PS(j)[:, 0:512],
                                        in1=xs4[:, 0:512], op=ALU.add)
                        g.dma_start(out=yo[dmb * 128:(dmb + 1) * 128, ts_],
                                    in_=ytile[:, 0:512])
            flag_write(3)
            spin(3)
            ch.finish(nc.gpsimd)
    return nc


def prep_full(inp):
    import time
    x = np.asarray(inp['L0_attn_hidden_states'])
    wqT = head_tiles(np.ascontiguousarray(np.asarray(inp['L0_attn_Wq']).T), NH)
    wkT = head_tiles(np.ascontiguousarray(np.asarray(inp['L0_attn_Wk']).T), NH)
    wvT = head_tiles(np.ascontiguousarray(np.asarray(inp['L0_attn_Wv']).T), NH)
    bq = np.asarray(inp['L0_attn_bq']).reshape(NH, HD)
    bk = np.asarray(inp['L0_attn_bk']).reshape(NH, HD)
    bv = np.asarray(inp['L0_attn_bv']).reshape(NH, HD)
    woT = head_tiles(np.ascontiguousarray(np.asarray(inp['L0_attn_Wo']).T), 16)
    bo = np.ascontiguousarray(np.asarray(inp['L0_attn_bo']).reshape(16, 128).T)
    w1t = head_tiles(np.ascontiguousarray(np.asarray(inp['L0_ffn_W1'])[PI].T), 64)
    w3t = head_tiles(np.ascontiguousarray(np.asarray(inp['L0_ffn_W3'])[PI].T), 64)
    w2T = np.ascontiguousarray(np.asarray(inp['L0_ffn_W2']).T)
    wq1 = head_tiles(np.ascontiguousarray(np.asarray(inp['L1_attn_Wq'])), NH)
    wk1 = head_tiles(np.ascontiguousarray(np.asarray(inp['L1_attn_Wk'])), NH)
    wv1 = head_tiles(np.ascontiguousarray(np.asarray(inp['L1_attn_Wv'])), NH)
    cq = np.asarray(inp['L1_attn_conv_q_w']).reshape(NH, 128, 4)
    ck = np.asarray(inp['L1_attn_conv_k_w']).reshape(NH, 128, 4)
    cv = np.asarray(inp['L1_attn_conv_v_w']).reshape(NH, 128, 4)
    alog = np.asarray(inp['L1_attn_A_log']).reshape(1, NH)
    wfa = chunk_tile(np.ascontiguousarray(np.asarray(inp['L1_attn_Wfa'])))
    wfb = np.asarray(inp['L1_attn_Wfb'])
    dtb = np.asarray(inp['L1_attn_dt_bias']).reshape(NH, 128)
    wga = chunk_tile(np.ascontiguousarray(np.asarray(inp['L1_attn_Wga'])))
    wgb = np.asarray(inp['L1_attn_Wgb'])
    onw = np.ascontiguousarray(np.asarray(inp['L1_attn_o_norm_w']).reshape(128, 1))
    wo1 = head_tiles(np.ascontiguousarray(np.asarray(inp['L1_attn_Wo'])), 16)
    gt = chunk_tile(np.ascontiguousarray(np.asarray(inp['L1_ffn_gate_w']).T))
    w1 = np.asarray(inp['L1_ffn_w1'])
    w3 = np.asarray(inp['L1_ffn_w3'])
    w2 = np.asarray(inp['L1_ffn_w2'])
    w1t1 = np.stack([head_tiles(np.ascontiguousarray(w1[e].T), 8) for e in range(8)])
    w3t1 = np.stack([head_tiles(np.ascontiguousarray(w3[e].T), 8) for e in range(8)])
    w2g = np.stack([np.ascontiguousarray(w2[e].T).reshape(8, 128, 2048)
                    for e in range(8)])
    ws1t = head_tiles(np.ascontiguousarray(np.asarray(inp['L1_ffn_ws1']).T), 8)
    ws3t = head_tiles(np.ascontiguousarray(np.asarray(inp['L1_ffn_ws3']).T), 8)
    ws2g = np.ascontiguousarray(np.asarray(inp['L1_ffn_ws2']).T).reshape(8, 128, 2048)
    nonce = int(time.time_ns() % 1000000000)
    non8 = np.arange(8, dtype=np.int32).reshape(1, 8) + nonce
    maps = []
    for core in range(8):
        b, hh = core // 2, core % 2
        hs = slice(hh * HPC, (hh + 1) * HPC)
        bqkv = np.ascontiguousarray(
            np.concatenate([bq[hs].T, bk[hs].T, bv[hs].T], axis=1))
        wbt = chunk_tile(np.ascontiguousarray(np.asarray(inp['L1_attn_Wb'])[:, hs]))
        maps.append({
            'x_fm': np.ascontiguousarray(x[b].T),
            'x_half': np.ascontiguousarray(x[b].T[:, hh * TOK:(hh + 1) * TOK]),
            'nonce8': non8,
            'wq_t': np.ascontiguousarray(wqT[hs]),
            'wk_t': np.ascontiguousarray(wkT[hs]),
            'wv_t': np.ascontiguousarray(wvT[hs]),
            'bqkv': bqkv,
            'rope_cos': CONSTS['rope_cos'], 'rope_sin': CONSTS['rope_sin'],
            'tri_keep': CONSTS['tri_keep'], 'ident_bf': CONSTS['ident_bf'],
            'ident_f32': CONSTS['ident_f32'],
            'wo_t': woT, 'bo_s': bo, 'w1_t': w1t, 'w3_t': w3t, 'w2_t': w2T,
            'h128': CONSTS['h128'], 'h64blk': CONSTS['h64blk'],
            'wq1_t': np.ascontiguousarray(wq1[hs]),
            'wk1_t': np.ascontiguousarray(wk1[hs]),
            'wv1_t': np.ascontiguousarray(wv1[hs]),
            'cq_s': np.ascontiguousarray(cq[hs]),
            'ck_s': np.ascontiguousarray(ck[hs]),
            'cv_s': np.ascontiguousarray(cv[hs]),
            'alog_s': np.ascontiguousarray(alog[:, hs]),
            'wfa_t': wfa,
            'wfb_s': np.ascontiguousarray(wfb[:, hh * 1024:(hh + 1) * 1024]),
            'dtb_s': np.ascontiguousarray(dtb[hs]),
            'wb_t': wbt,
            'wga_t': wga,
            'wgb_s': np.ascontiguousarray(wgb[:, hh * 1024:(hh + 1) * 1024]),
            'onw': onw,
            'cumpack': CONSTS['cumpack'], 'wm_masks': CONSTS['wm_masks'],
            'wo1_t': wo1, 'gate_t': gt,
            'w1e_t': w1t1, 'w3e_t': w3t1, 'w2e_g': w2g,
            'ws1_t': ws1t, 'ws3_t': ws3t, 'ws2_g': ws2g,
        })
    return maps

def kernel(**inputs):
    res = run_bass_kernel_spmd(build_full(), prep_full(inputs),
                               core_ids=list(range(8)))
    out = np.empty((B, T, D), dtype=np.float32)
    for core in range(8):
        b, half = core // 2, core % 2
        out[b, half * TOK:(half + 1) * TOK, :] = res.results[core]['y_half'].T
    return out
